# version 09
# baseline (speedup 1.0000x reference)
import math
import numpy as np
import concourse.bass as bass
import concourse.mybir as mybir
from concourse.bass_utils import run_bass_kernel_spmd
from concourse import library_config
from concourse.bass_isa import ReduceOp

F32 = mybir.dt.float32
BF16 = mybir.dt.bfloat16
AF = mybir.ActivationFunctionType
ALU = mybir.AluOpType
AX = mybir.AxisListType.X

T = 2048
DM = 1024
NH = 16
HD = 64
SEQ = 8192
PATCH = 4064
FFH = 4096
QB = 512
NQB = T // QB
SUP = 512
NSUP = SEQ // SUP
CPS = SUP // 128
FQB = 256
NFQB = T // FQB
RND_C = float(3 << 22)
PAIRSWAP = [x for i in range(16) for x in (2 * i + 1, 2 * i)]

ENG = ("tensor", "vector", "scalar", "gpsimd", "sync")


class Tape:
    def __init__(self):
        self.ops = []
        self.last_write = {}
        self.readers = {}
        self.counts = {}
        self.op_sem = []
        self.drains = {}
        self._W = None

    def _build_W(self):
        import bisect
        W = {}
        for (_, _, waits, _) in self.ops:
            for k, v in waits.items():
                W.setdefault(k, set()).add(v)
        for (k, v) in self.drains.values():
            if v > 0:
                W.setdefault(k, set()).add(v)
        self._W = {k: sorted(s) for k, s in W.items()}
        self._Wset = {k: set(s) for k, s in W.items()}

    def remap(self, k, v, unit):
        import bisect
        if str(k).startswith("dm") or str(k) == "cc":
            return v
        lst = self._W.get(k, [])
        return unit * bisect.bisect_right(lst, v)

    def op(self, engine, emit, reads=(), writes=(), sem=None, inc=1, extra_deps=()):
        idx = len(self.ops)
        deps = set(extra_deps)
        for r in reads:
            w = self.last_write.get(r)
            if w is not None:
                deps.add(w)
            self.readers.setdefault(r, []).append(idx)
        for r in writes:
            w = self.last_write.get(r)
            if w is not None:
                deps.add(w)
            for rd in self.readers.get(r, ()):
                deps.add(rd)
            self.last_write[r] = idx
            self.readers[r] = []
        key = sem if sem is not None else engine
        cnt = self.counts.get(key, 0) + inc
        self.counts[key] = cnt
        self.op_sem.append((key, cnt))
        waits = {}
        for d in deps:
            if d == idx:
                continue
            k, v = self.op_sem[d]
            if waits.get(k, -1) < v:
                waits[k] = v
        self.ops.append((engine, emit, waits, (key, inc)))
        return idx

    def dma_drain(self, queue, semk, sems_ref):
        v = self.counts.get(semk, 0)
        idx = self.op(queue,
                      lambda e, v_=v, k_=semk: (
                          e.wait_ge(sems_ref[k_], self.remap(k_, v_, 16)), None)[1],
                      sem=semk, inc=0)
        self.drains[idx] = (semk, v)

    def batch_end(self, resources):
        i = len(self.ops) - 1
        for r in resources:
            if r in self.last_write:
                self.last_write[r] = i

    def barrier(self, writes, seed):
        idx = self.op("vector", lambda e: e.engine_nop(), writes=tuple(writes))
        for s in seed:
            self.last_write[s] = idx
            self.readers[s] = []
        return idx

    def emit_engine(self, engine, eng, sems):
        if self._W is None:
            self._build_W()
        satisfied = {}
        for i, (e, emit, waits, (key, inc)) in enumerate(self.ops):
            if e != engine:
                continue
            for k in sorted(waits, key=str):
                unit = 16 if str(k).startswith(("dm", "cc_x")) else 1
                v = self.remap(k, waits[k], unit)
                if satisfied.get(k, -1) >= v:
                    continue
                eng.wait_ge(sems[k], v)
                satisfied[k] = v
            ins = emit(eng)
            if ins is not None and inc > 0:
                cnt = self.op_sem[i][1]
                if (str(key).startswith("dm") or str(key) == "cc"
                        or cnt in self._Wset.get(key, ())):
                    ins.then_inc(sems[key], inc)


def build(debug=False):
    nc = bass.Bass("TRN2", target_bir_lowering=False)
    ein = lambda n, s, dt=F32: nc.dram_tensor(n, s, dt, kind="ExternalInput").ap()
    eout = lambda n, s, dt=F32: nc.dram_tensor(n, s, dt, kind="ExternalOutput").ap()
    idram = lambda n, s, dt=F32: nc.dram_tensor(n, s, dt).ap()

    xT = ein("xT", [DM, T])
    wqT = ein("wqT", [DM, DM])
    wkT = ein("wkT", [DM, DM])
    wvT = ein("wvT", [DM, DM])
    woT = ein("woT", [DM, DM])
    w1T = ein("w1T", [DM, FFH])
    w2T = ein("w2T", [FFH, DM])
    wmodT = ein("wmodT", [DM, 6 * DM])
    c_row = ein("c_row", [128, 8])
    ropeC = ein("ropeC", [128, T])
    ropeS = ein("ropeS", [128, T])
    alt = ein("alt", [128, 8])
    wqn2 = ein("wqn2", [2, 128])
    wkn2 = ein("wkn2", [2, 128])
    b1c = ein("b1c", [128, 32])
    b2c = ein("b2c", [128, 8])
    gammar = ein("gammar", [128, 8])
    betac = ein("betac", [128, 8])
    iden = ein("iden", [128, 128])
    iotaM = ein("iotaM", [64, 128])
    outT = eout("outT", [DM, T])

    kbounce = idram("kbounce", [DM, T], BF16)
    kag = idram("kag", [4 * DM, T], BF16)
    vbounce = idram("vbounce", [T, DM], BF16)
    vag = idram("vag", [SEQ, DM], BF16)
    knb = idram("knb", [2, 8], F32)
    kna = idram("kna", [2, 8], F32)
    x1d = idram("x1d", [DM, T], F32)
    xq1d = idram("xq1d", [DM, T], BF16)
    w1qd = idram("w1qd", [DM, FFH], BF16)
    w2qd = idram("w2qd", [FFH, DM], BF16)
    groups = [[0, 1, 2, 3], [4, 5, 6, 7]]
    dbgout = {}
    if debug:
        for nm, shape, dt in [
            ("t_xq2a", [128, FQB], BF16), ("t_xq2b", [128, FQB], BF16),
            ("t_xq2f", [FFH, FQB], BF16), ("t_y2all", [DM, FQB], F32),
            ("t_w2q", [FFH, DM], BF16),
            ("t_psc2", [128, FQB], F32), ("t_y26", [128, FQB], F32),
            ("t_mod", [128, 48], F32), ("t_ha0", [128, T], BF16),
            ("t_q0", [128, T], BF16), ("t_kb", [DM, T], BF16),
            ("t_vb", [T, DM], BF16), ("t_mneg", [128, 8], F32),
            ("t_drec", [64, QB], F32), ("t_osb", [128, 32 * QB], F32),
            ("t_x1", [DM, T], F32), ("t_h20", [128, T], BF16),
            ("t_xq1", [DM, T], BF16), ("t_w1q", [DM, FFH], BF16),
            ("t_hm0", [128, FQB], BF16), ("t_swsc", [128, 8], F32),
            ("t_kag", [4 * DM, T], BF16), ("t_vag", [SEQ, DM], BF16),
        ]:
            dbgout[nm] = eout(nm, shape, dt)

    BASE = 16512
    R_A, R_B, R_C, R_D, R_E = (BASE, BASE + 65536, BASE + 98304, BASE + 131072,
                               BASE + 163840)
    R_S = BASE + 196608
    SM = BASE + 204800

    def at(name, shape, dt, off):
        return nc.alloc_sbuf_tensor_at(name, shape, dt, offset=off)

    x32 = [at(f"x32_{i}", [128, T], F32, R_A + i * 8192) for i in range(8)]
    osb = at("osb", [128, 32 * QB], F32, R_A)
    x1sb = [at(f"x1_{i}", [128, T], F32, R_A + i * 8192) for i in range(8)]
    w1q = [at(f"w1q_{i}", [128, FFH], BF16, R_A + i * 8192) for i in range(8)]
    ha = [at(f"ha_{i}", [128, T], BF16, R_B + i * 4096) for i in range(8)]
    ptb = [at(f"ptb_{i}", [128, 4 * SUP * 2], BF16, R_B + i * 8192) for i in range(2)]
    onrm = at("onrm", [128, 32 * QB], BF16, R_B)
    ropeCs = at("ropeCs", [128, T], F32, R_C)
    ropeSs = at("ropeSs", [128, T], F32, R_C + 8192)
    qstg = [at(f"qstg_{i}", [128, QB], F32, R_C + 16384 + i * 2048) for i in range(4)]
    ksup = [at(f"ksup_{i}", [128, CPS * 128 * 8], BF16, R_C + i * 8192) for i in range(2)]
    vsup = [at(f"vsup_{i}", [128, CPS * DM], BF16, R_C + 16384 + i * 8192) for i in range(2)]
    woTs = [at(f"woTs_{i}", [128, DM], BF16, R_C + i * 2048) for i in range(8)]
    h2lo = [at(f"h2lo_{i}", [128, T], BF16, R_C + 16384 + i * 4096) for i in range(4)]
    w2qh = [at(f"w2qh_{i}", [128, DM], BF16, R_C + i * 2048) for i in range(16)]
    wstg = at("wstg", [128, FFH], F32, R_D)
    wstg2 = at("wstg2", [128, FFH], BF16, R_D + 16384)
    wbf = [at(f"wbf_{i}", [128, DM], BF16, R_D + i * 2048) for i in range(8)]
    wq32 = at("wq32", [128, DM], F32, R_D + 16384)
    kstg = [at(f"kstg_{i}", [128, T], BF16, R_D + 20480 + i * 4096) for i in range(2)]
    vstg = [at(f"vstg_{i}", [128, DM], BF16, R_D + 28672 + i * 2048) for i in range(2)]
    hb = [at(f"hb_{i}", [128, T], BF16, R_D + i * 4096) for i in range(8)]
    hmid = [at(f"hmid_{i}", [128, FQB], BF16, R_D + i * 512) for i in range(32)]
    xq2 = [at(f"xq2_{i}", [128, FQB], BF16, R_D + 16384 + i * 512) for i in range(32)]
    xq1s = [at(f"xq1s_{i}", [128, FQB], BF16, R_B + 10240 + i * 512) for i in range(8)]
    xq1w = [at(f"xq1w_{i}", [128, T], BF16, R_D + i * 4096) for i in range(2)]
    qsb = [at(f"qsb_{i}", [128, T], BF16, R_E + i * 4096) for i in range(8)]
    h2hi = [at(f"h2hi_{i}", [128, T], BF16, R_E + i * 4096) for i in range(4)]
    w2ql = [at(f"w2ql_{i}", [128, DM], BF16, R_E + i * 2048) for i in range(16)]
    rows = at("rows", [6, T], F32, R_S)
    tts = at("tts", [128, T], F32, R_B + 24576)
    q2sb = at("q2sb", [128, QB], F32, R_A)
    w2r = at("w2r", [128, QB], F32, R_A + 2048)
    tq2 = at("tq2", [128, QB], F32, R_A + 4096)
    drec = at("drec", [64, QB], F32, R_C + 16384)
    dbcs = at("dbcs", [128, QB], F32, R_C + 18432)
    xstg = [at(f"xstg_{i}", [128, QB], F32, R_C + 20480 + i * 2048)
            for i in range(2)]
    xstg2 = [at(f"xstg2_{i}", [128, QB], F32, R_B + i * 2048) for i in range(2)]
    pscs = at("pscs", [128, FQB], F32, R_B + 4096)
    psc2s = at("psc2s", [128, FQB], F32, R_B + 5120)
    tf = at("tf", [128, FQB], F32, R_B + 6144)
    tf2 = at("tf2", [128, FQB], F32, R_B + 7168)
    sxf = at("sxf", [128, FQB], F32, R_B + 8192)
    mxf = at("mxf", [128, FQB], F32, R_B + 9216)
    off = [SM]

    def sm(name, shape, dt):
        nbytes = int(np.prod(shape[1:])) * mybir.dt.size(dt)
        t = at(name, shape, dt, off[0])
        off[0] += (nbytes + 31) // 32 * 32
        return t

    tq = sm("tq", [128, QB], F32)
    CONSTS = [0.0, 1.0, -1.0, -0.5, 0.5, 1e-5, 1e-6, 1.0 / HD, math.log(0.125)]
    cpool = sm("cpool", [128, len(CONSTS)], F32)
    mod_sb = sm("mod_sb", [128, 48], F32)
    modg = sm("modg", [128, 16], F32)
    modca = sm("modca", [128, 16], F32)
    gbb2 = sm("gbb2", [128, 8], F32)
    csrow = sm("csrow", [128, 8], F32)
    cwork = sm("cwork", [128, 32], F32)
    onesel = sm("onesel", [128, 64], BF16)
    zer128 = sm("zer128", [128, 128], BF16)
    idn = sm("idn", [128, 128], F32)
    alts = sm("alts", [128, 8], F32)
    wqn2s = sm("wqn2s", [2, 128], F32)
    wkn2s = sm("wkn2s", [2, 128], F32)
    b1cs = sm("b1cs", [128, 32], F32)
    b2cs = sm("b2cs", [128, 8], F32)
    gammacs = sm("gammacs", [128, 8], F32)
    betacs = sm("betacs", [128, 8], F32)
    mneg = sm("mneg", [128, 8], F32)
    qn2m = sm("qn2m", [2, 16], F32)
    kn2m = sm("kn2m", [2, 8], F32)
    kn2g = sm("kn2g", [2, 8], F32)
    mm2s = sm("mm2s", [2, 8], F32)
    selD = sm("selD", [64, 128], F32)
    selDb = sm("selDb", [64, 128], F32)
    mxt = sm("mxt", [128, 16], F32)
    mxr = sm("mxr", [128, 16], F32)
    iotas = sm("iotas", [64, 128], F32)
    hselF = sm("hselF", [128, 2], F32)
    ones_row = sm("ones_row", [1, 128], F32)
    onescF = sm("onescF", [128, 1], F32)
    csrowb = sm("csrowb", [128, 8], BF16)
    swsc = sm("swsc", [128, 8], F32)
    assert off[0] <= 229344, f"smalls overflow {off[0]}"

    sxr = at("sxr", [128, T], F32, R_B)
    mxrow = at("mxrow", [128, T], F32, R_B + 8192)
    tts2 = at("tts2", [128, T], F32, R_B + 16384)

    stb = nc.alloc_psum_tensor("stb", [128, 4 * QB], F32)
    ow = [nc.alloc_psum_tensor(f"ow_{i}", [128, QB], F32) for i in range(2)]
    dbank = nc.alloc_psum_tensor("dbank", [64, QB], F32)
    scr = nc.alloc_psum_tensor("scr", [128, 512], F32)

    tape = Tape()
    SEMS = {}

    def t_mm(out_ap, lhs_ap, rhs_ap, reads, writes, start=True, stop=True, tp=None):
        return tape.op("tensor",
                       lambda e: e.matmul(out_ap, lhs_ap, rhs_ap, start=start, stop=stop,
                                          tile_position=tp),
                       reads=reads, writes=writes)

    def t_act(out_ap, in_ap, func, reads, writes, bias=0.0, scale=1.0):
        return tape.op("scalar",
                       lambda e: e.activation(out_ap, in_ap, func, bias=bias, scale=scale),
                       reads=reads, writes=writes)

    def t_dve(fn, reads, writes):
        return tape.op("vector", fn, reads=reads, writes=writes)

    def tap(nm, in_ap, reads):
        if debug:
            tape.op("gpsimd", lambda e: e.dma_start(out=dbgout[nm], in_=in_ap),
                    reads=reads, writes=(), sem="dmg", inc=16)

    def t_dma(queue, out_ap, in_ap, semk, reads, writes):
        return tape.op(queue, lambda e: e.dma_start(out=out_ap, in_=in_ap),
                       reads=reads, writes=writes, sem=semk, inc=16)

    t_dma("sync", cwork[:, 0:8], c_row, "dml", (), ("c_in",))
    t_dma("sync", ropeCs[:], ropeC, "dml", (), ("ropeCs",))
    t_dma("sync", ropeSs[:], ropeS, "dml", (), ("ropeSs",))
    t_dma("sync", alts[:], alt, "dml", (), ("alts",))
    t_dma("sync", wqn2s[:], wqn2, "dml", (), ("wqn2s",))
    t_dma("sync", wkn2s[:], wkn2, "dml", (), ("wkn2s",))
    t_dma("sync", b1cs[:], b1c, "dml", (), ("b1cs",))
    t_dma("sync", b2cs[:], b2c, "dml", (), ("b2cs",))
    t_dma("sync", gammacs[:], gammar, "dml", (), ("gammacs",))
    t_dma("sync", betacs[:], betac, "dml", (), ("betacs",))
    t_dma("sync", idn[:], iden, "dml", (), ("idn",))
    t_dma("sync", iotas[:], iotaM, "dml", (), ("iotas",))
    tape.batch_end(["c_in", "ropeCs", "ropeSs", "alts", "wqn2s", "wkn2s", "b1cs",
                    "b2cs", "gammacs", "betacs", "idn", "iotas"])
    tape.dma_drain("sync", "dml", SEMS)
    for i in range(8):
        t_dma("sync", x32[i][:], xT[128 * i:128 * (i + 1), :], "dmx", (), (f"x32:{i}",))
    tape.batch_end([f"x32:{i}" for i in range(8)])
    tape.dma_drain("sync", "dmx", SEMS)

    for i_, v_ in enumerate(CONSTS):
        t_dve(lambda e, i=i_, v=v_: e.memset(cpool[:, i:i + 1], v), ("cpool",), ("cpool",))
        nc.const_aps.register(F32, v_, cpool[:, i_:i_ + 1])
    t_dve(lambda e: e.memset(onesel[:, 0:32], 0.0), (), ("onesel",))
    t_dve(lambda e: e.memset(onesel[:, 32:33], 1.0), ("onesel",), ("onesel",))
    t_dve(lambda e: e.memset(onesel[:, 33:64], 0.0), ("onesel",), ("onesel",))
    t_dve(lambda e: e.memset(zer128[:], 0.0), (), ("zer128",))
    t_dve(lambda e: e.memset(hselF[0:64, 0:1], 1.0), (), ("hselF",))
    t_dve(lambda e: e.memset(hselF[64:128, 0:1], 0.0), ("hselF",), ("hselF",))
    t_dve(lambda e: e.memset(hselF[0:64, 1:2], 0.0), ("hselF",), ("hselF",))
    t_dve(lambda e: e.memset(hselF[64:128, 1:2], 1.0), ("hselF",), ("hselF",))
    t_dve(lambda e: e.memset(ones_row[:], 1.0), (), ("ones_row",))
    t_dve(lambda e: e.memset(onescF[:], 1.0), (), ("onescF",))

    t_act(cwork[:, 8:16], cwork[:, 0:8], AF.Exp, ("c_in", "cpool"), ("c_exp",), scale=-1.0)
    t_dve(lambda e: e.tensor_scalar_add(cwork[:, 8:16], cwork[:, 8:16], 1.0),
          ("c_exp",), ("c_exp",))
    t_dve(lambda e: e.reciprocal(cwork[:, 16:24], cwork[:, 8:16]), ("c_exp",), ("c_rec",))
    t_dve(lambda e: e.tensor_mul(csrow[:], cwork[:, 0:8], cwork[:, 16:24]),
          ("c_in", "c_rec"), ("csrow",))
    t_dve(lambda e: e.tensor_copy(out=csrowb[:], in_=csrow[:]), ("csrow",), ("csrowb",))

    first = [True]
    for ic in range(8):
        for half in range(2):
            sl = slice(3072 * half, 3072 * (half + 1))
            t_dma("sync", wstg[:, 0:3072], wmodT[128 * ic:128 * (ic + 1), sl],
                  "dml", ("wstg_r",), ("wstg",))
            t_dve(lambda e: e.tensor_copy(out=wstg2[:, 0:3072], in_=wstg[:, 0:3072]),
                  ("wstg",), ("wstg2", "wstg_r"))
            for j in range(24):
                oc = 24 * half + j
                t_mm(scr[:, oc:oc + 1], wstg2[:, 128 * j:128 * (j + 1)],
                     csrowb[:, ic:ic + 1],
                     reads=("wstg2", "csrowb"), writes=("scr",),
                     start=first[0], stop=(ic == 7 and half == 1 and j == 23))
                first[0] = False
    t_dve(lambda e: e.tensor_copy(out=mod_sb[:], in_=scr[:, 0:48]), ("scr",), ("mod_sb",))
    tape.barrier(["wstg", "wstg2"], ["wq32"] + [f"kst:{i}" for i in range(2)])
    tap("t_mod", mod_sb[:], ("mod_sb",))

    def g_clamp(src_lo, dst_lo):
        t_dve(lambda e: e.tensor_scalar(cwork[:, 0:8], mod_sb[:, src_lo:src_lo + 8],
                                        0.0, -1.0, ALU.is_ge, ALU.bypass),
              ("mod_sb",), ("cw_ge",))
        t_dve(lambda e: e.tensor_scalar(cwork[:, 8:16], cwork[:, 0:8], 2.0, -1.0,
                                        ALU.mult, ALU.add), ("cw_ge",), ("cw_sg",))
        t_dve(lambda e: e.scalar_tensor_tensor(cwork[:, 16:24],
                                               mod_sb[:, src_lo:src_lo + 8], -1.0,
                                               mod_sb[:, src_lo:src_lo + 8],
                                               ALU.mult, ALU.max),
              ("mod_sb",), ("cw_ab",))
        t_dve(lambda e: e.tensor_scalar_max(cwork[:, 16:24], cwork[:, 16:24], 0.25),
              ("cw_ab",), ("cw_ab",))
        t_dve(lambda e, d=dst_lo: e.tensor_mul(modg[:, d:d + 8], cwork[:, 8:16],
                                               cwork[:, 16:24]),
              ("cw_sg", "cw_ab"), ("modg",))
    g_clamp(16, 0)
    g_clamp(40, 8)

    t_dve(lambda e: e.tensor_scalar_add(modca[:, 0:8], mod_sb[:, 8:16], 1.0),
          ("mod_sb",), ("modca",))
    t_dve(lambda e: e.tensor_scalar_add(modca[:, 8:16], mod_sb[:, 32:40], 1.0),
          ("mod_sb", "modca"), ("modca",))
    t_dve(lambda e: e.tensor_mul(gbb2[:], modg[:, 8:16], b2cs[:]), ("modg", "b2cs"), ("gbb2",))

    def emit_adaln_ln(src_tag, src, dst_tag, dst, scale_vec, shift_vec, svtag, tag,
                      src_bf16=False):
        ones_col = onesel[:, 32:33] if src_bf16 else onescF[:]
        ones_tag = "onesel" if src_bf16 else "onescF"
        for qb in range(NQB):
            sl = slice(QB * qb, QB * (qb + 1))
            for ci in range(8):
                t_mm(scr[0:1, 0:QB], ones_col, src[ci][:, sl],
                     reads=(f"{src_tag}:{ci}", ones_tag), writes=("scr",),
                     start=(ci == 0), stop=(ci == 7))
            for ci in range(8):
                t_dve(lambda e, c=ci, s=sl: e.tensor_mul(tq[:], src[c][:, s], src[c][:, s]),
                      reads=(f"{src_tag}:{ci}", "tq"), writes=("tq",))
                t_mm(ow[0][0:1, 0:QB], onescF[:], tq[:],
                     reads=("tq", "onescF"), writes=("ow0",),
                     start=(ci == 0), stop=(ci == 7))
            t_dve(lambda e: e.tensor_scalar_mul(rows[0:1, 1024:1536], scr[0:1, 0:QB],
                                                1.0 / DM),
                  reads=("scr", "rows0"), writes=(f"{tag}mean", "rows0"))
            t_dve(lambda e: e.tensor_scalar_mul(rows[0:1, 1536:2048], ow[0][0:1, 0:QB],
                                                1.0 / DM),
                  reads=("ow0", "rows0"), writes=(f"{tag}e2", "rows0"))
            t_dve(lambda e: e.tensor_mul(scr[0:1, 0:QB], rows[0:1, 1024:1536],
                                         rows[0:1, 1024:1536]),
                  reads=(f"{tag}mean", "scr"), writes=("scr",))
            t_dve(lambda e: e.tensor_sub(rows[0:1, 1536:2048], rows[0:1, 1536:2048],
                                         scr[0:1, 0:QB]),
                  reads=(f"{tag}e2", "scr"), writes=(f"{tag}var", "rows0"))
            t_act(scr[0:1, 0:QB], rows[0:1, 1536:2048], AF.Ln,
                  (f"{tag}var",), ("scr",), bias=1e-5, scale=1.0)
            t_act(rows[0:1, 0:QB], scr[0:1, 0:QB], AF.Exp,
                  ("scr", "rows0"), (f"{tag}rs", "rows0"), bias=0.0, scale=-0.5)
            t_dve(lambda e: e.tensor_mul(rows[0:1, 512:1024], rows[0:1, 0:QB],
                                         rows[0:1, 1024:1536]),
                  reads=(f"{tag}mean", f"{tag}rs", "rows0"), writes=(f"{tag}mrs", "rows0"))
            t_mm(scr[:, 0:QB], ones_row[:], rows[0:1, 0:QB],
                 reads=(f"{tag}rs", "ones_row"), writes=("scr",), start=True, stop=True)
            t_mm(ow[0][:, 0:QB], ones_row[:], rows[0:1, 512:1024],
                 reads=(f"{tag}mrs", "ones_row"), writes=("ow0",), start=True, stop=True)
            for ci in range(8):
                t_dve(lambda e, c=ci, s=sl: e.tensor_mul(tq[:], src[c][:, s], scr[:, 0:QB]),
                      reads=(f"{src_tag}:{ci}", "scr", "tq"), writes=("tq",))
                t_dve(lambda e: e.tensor_sub(tq[:], tq[:], ow[0][:, 0:QB]),
                      reads=("tq", "ow0"), writes=("tq",))
                t_dve(lambda e, c=ci, s=sl: e.tensor_scalar(
                        dst[c][:, s], tq[:], scale_vec(c), shift_vec(c),
                        ALU.mult, ALU.add),
                      reads=("tq", svtag), writes=(f"{dst_tag}:{ci}",))

    tape.last_write["svt_ln1"] = max(tape.last_write.get("modca", 0),
                                     tape.last_write.get("mod_sb", 0))
    emit_adaln_ln("x32", x32, "ha", ha, lambda c: modca[:, c:c + 1],
                  lambda c: mod_sb[:, 0 + c:1 + c], "svt_ln1", "ln1")

    tape.barrier([f"x32:{i}" for i in range(8)], ["q2sb", "w2r", "qk_ln", "qk_rf", "tq2p"])
    tap("t_ha0", ha[0][:], ("ha:0",))

    def load_w_bf16(wT_, dtag):
        for ci in range(8):
            t_dma("sync", wq32[:], wT_[128 * ci:128 * (ci + 1), :], "dml",
                  ("wq32_r",), ("wq32",))
            t_dve(lambda e, c=ci: e.tensor_copy(out=wbf[c][:], in_=wq32[:]),
                  ("wq32",), (f"wbf:{ci}", "wq32_r"))

    def emit_qk(wT_, wn2s_ap, wn2tag, out_write, lnbias, kn_tag=None):
        for oc in range(8):
            for qb in range(NQB):
                sl = slice(QB * qb, QB * (qb + 1))
                bank = stb[:, QB * (qb % 2):QB * (qb % 2) + QB]
                for ic in range(8):
                    t_mm(bank, wbf[ic][:, 128 * oc:128 * (oc + 1)], ha[ic][:, sl],
                         reads=(f"wbf:{ic}", f"ha:{ic}"), writes=(f"stb:{qb % 2}",),
                         start=(ic == 0), stop=(ic == 7))
                t_act(q2sb[:], bank, AF.Square, (f"stb:{qb % 2}",), ("q2sb",))
                t_mm(scr[0:2, 0:QB], hselF[:], q2sb[:],
                     reads=("q2sb", "hselF"), writes=("scr",), start=True, stop=True)
                t_act(tq[0:2, 0:QB], scr[0:2, 0:QB], AF.Ln, ("scr",), ("qk_ln",),
                      bias=1e-6, scale=1.0 / HD)
                t_act(tq2[0:2, 0:QB], tq[0:2, 0:QB], AF.Exp, ("qk_ln",),
                      ("qk_rf", "tq2p"), bias=lnbias, scale=-0.5)
                t_mm(scr[:, 0:QB], wn2s_ap, tq2[0:2, 0:QB],
                     reads=("qk_rf", wn2tag), writes=("scr",), start=True, stop=True)
                t_dve(lambda e: e.tensor_copy(out=w2r[:], in_=scr[:, 0:QB]),
                      ("scr",), ("w2r",))
                stg = qstg[qb % 2]
                stg2 = qstg[2 + qb % 2]
                t_dve(lambda e, b=bank, s=stg: e.tensor_mul(s[:], b, w2r[:]),
                      (f"stb:{qb % 2}", "w2r"), (f"qkstg:{qb % 2}",))
                t_dve(lambda e, s=stg, d=stg2: e.stream_shuffle(d[:], s[:], PAIRSWAP),
                      (f"qkstg:{qb % 2}",), (f"qksh:{qb % 2}",))
                t_dve(lambda e, s=stg, sli=sl: e.tensor_mul(tq[:], s[:], ropeCs[:, sli]),
                      (f"qkstg:{qb % 2}", "ropeCs"), ("tq",))
                t_dve(lambda e, d=stg2, sli=sl: e.scalar_tensor_tensor(
                        d[:], d[:], alts[:, 0:1], ropeSs[:, sli], ALU.mult, ALU.mult),
                      (f"qksh:{qb % 2}", "alts", "ropeSs"), (f"qkt2:{qb % 2}",))
                out_write(oc, qb, stg2)
            if kn_tag is not None:
                kn_tag(oc)

    load_w_bf16(wqT, "wq")

    def q_write(oc, qb, stg2):
        sl = slice(QB * qb, QB * (qb + 1))
        t_dve(lambda e, d=stg2, o=qsb[oc], s=sl: e.tensor_add(o[:, s], tq[:], d[:]),
              ("tq", f"qkt2:{qb % 2}"), (f"qsb:{oc}",))

    emit_qk(wqT, wqn2s[:], "wqn2s", q_write, math.log(0.125))

    load_w_bf16(wkT, "wk")

    def k_write(oc, qb, stg2):
        sl = slice(QB * qb, QB * (qb + 1))
        kst = kstg[oc % 2]
        t_dve(lambda e, d=stg2, o=kst, s=sl: e.tensor_add(o[:, s], tq[:], d[:]),
              ("tq", f"qkt2:{qb % 2}"), (f"kst:{oc % 2}",))
        stg = qstg[qb % 2]
        t_dve(lambda e, s=stg: e.tensor_mul(tq[:], s[:], s[:]),
              (f"qkstg:{qb % 2}",), ("tq",))
        t_mm(scr[0:2, 0:QB], hselF[:], tq[:], reads=("tq", "hselF"), writes=("scr",),
             start=True, stop=True)
        t_dve(lambda e, q=qb: e.tensor_reduce(qn2m[0:2, 8 + q:9 + q], scr[0:2, 0:QB],
                                              AX, ALU.max),
              ("scr",), ("qn2m_s",))

    def k_fin(oc):
        t_dve(lambda e, o=oc: e.tensor_reduce(kn2m[0:2, o:o + 1], qn2m[0:2, 8:12],
                                              AX, ALU.max),
              ("qn2m_s",), ("kn2m",))
        t_dma("sync", kbounce[128 * oc:128 * (oc + 1), :], kstg[oc % 2][:],
              "dms" if oc % 2 == 0 else "dms1",
              (f"kst:{oc % 2}",), (f"kb:{oc}", f"kst_r:{oc % 2}"))

    emit_qk(wkT, wkn2s[:], "wkn2s", k_write, 0.0, kn_tag=k_fin)

    load_w_bf16(wvT, "wv")
    for tc in range(16):
        vst = vstg[tc % 2]
        for half in range(2):
            bank = stb[:, QB * half:QB * half + QB]
            for ic in range(8):
                t_mm(bank, ha[ic][:, 128 * tc:128 * (tc + 1)],
                     wbf[ic][:, 512 * half:512 * (half + 1)],
                     reads=(f"wbf:{ic}", f"ha:{ic}"), writes=(f"stb:{half}",),
                     start=(ic == 0), stop=(ic == 7))
            t_dve(lambda e, h=half, v=vst, b=bank: e.tensor_copy(
                    out=v[:, 512 * h:512 * (h + 1)], in_=b),
                  (f"stb:{half}",), (f"vst:{tc % 2}",))
        t_dma("sync", vbounce[128 * tc:128 * (tc + 1), :], vst[:],
              "dms" if tc % 2 == 0 else "dms1",
              (f"vst:{tc % 2}",), (f"vb:{tc}",))

    for oc in range(8):
        for qb in range(NQB):
            sl = slice(QB * qb, QB * (qb + 1))
            t_dve(lambda e, o=oc, s=sl: e.tensor_mul(tq[:], qsb[o][:, s], qsb[o][:, s]),
                  (f"qsb:{oc}",), ("tq",))
            t_mm(scr[0:2, 0:QB], hselF[:], tq[:], reads=("tq", "hselF"), writes=("scr",),
                 start=True, stop=True)
            t_dve(lambda e, q=qb: e.tensor_reduce(qn2m[0:2, 8 + q:9 + q], scr[0:2, 0:QB],
                                                  AX, ALU.max),
                  ("scr",), ("qn2m_s",))
        t_dve(lambda e, o=oc: e.tensor_reduce(qn2m[0:2, o:o + 1], qn2m[0:2, 8:12],
                                              AX, ALU.max),
              ("qn2m_s",), ("qn2m",))

    t_dma("gpsimd", knb, kn2m[:], "dmg", ("kn2m",), ())
    kn_dma = len(tape.ops) - 1
    tape.op("gpsimd",
            lambda e: e.collective_compute("AllGather", ALU.bypass, replica_groups=groups,
                                           ins=[kbounce], outs=[kag]),
            reads=tuple(f"kb:{i}" for i in range(8)), writes=("d_kag",), sem="cc")
    tape.op("gpsimd",
            lambda e: e.collective_compute("AllGather", ALU.bypass, replica_groups=groups,
                                           ins=[vbounce], outs=[vag]),
            reads=tuple(f"vb:{i}" for i in range(16)), writes=("d_vag",), sem="cc")
    tape.op("gpsimd",
            lambda e: e.collective_compute("AllReduce", ALU.max, replica_groups=groups,
                                           ins=[knb], outs=[kna]),
            reads=(), writes=("d_kna",), sem="cc", extra_deps=(kn_dma,))
    t_dma("gpsimd", kn2g[:], kna, "dmg", ("d_kna",), ("kn2g",))

    t_dve(lambda e: e.tensor_mul(mm2s[:], qn2m[0:2, 0:8], kn2g[:]),
          ("qn2m", "kn2g"), ("mm2s",))
    t_act(mm2s[:], mm2s[:], AF.Ln, ("mm2s",), ("mm2s",))
    t_act(mm2s[:], mm2s[:], AF.Exp, ("mm2s",), ("mm2s",), scale=0.5)
    tape.op("tensor", lambda e: e.transpose(scr[0:8, 32:34], mm2s[:], idn[0:2, 0:2]),
            reads=("mm2s", "idn"), writes=("scr",))
    t_dve(lambda e: e.tensor_reduce(cwork[0:8, 0:1], scr[0:8, 32:34], AX, ALU.max),
          ("scr",), ("cw_m",))
    tape.op("tensor", lambda e: e.transpose(scr[0:1, 64:72], cwork[0:8, 0:1], idn[0:8, 0:8]),
            reads=("cw_m", "idn"), writes=("scr",))
    t_dve(lambda e: e.tensor_copy(out=cwork[0:1, 8:16], in_=scr[0:1, 64:72]),
          ("scr",), ("cw_m2",))
    t_mm(scr[:, 0:8], ones_row[:], cwork[0:1, 8:16], reads=("cw_m2", "ones_row"),
         writes=("scr",), start=True, stop=True)
    t_dve(lambda e: e.tensor_scalar_mul(mneg[:], scr[:, 0:8], -1.0), ("scr",), ("mneg",))
    tap("t_q0", qsb[0][:], ("qsb:0",))
    tap("t_kb", kbounce, tuple(f"kb:{i}" for i in range(8)))
    tap("t_vb", vbounce, tuple(f"vb:{i}" for i in range(16)))
    tap("t_kag", kag, ("d_kag",))
    tap("t_vag", vag, ("d_vag",))
    tap("t_mneg", mneg[:], ("mneg",))



    def emit_tern(wT_, rowsn, cols, wqd, tag, sccol):
        nch = rowsn // 128
        for ci in range(nch):
            t_dma("gpsimd", wstg[:, 0:cols], wT_[128 * ci:128 * (ci + 1), :], "dmg",
                  ("wstg",), ("wstg",))
            t_dve(lambda e: e.scalar_tensor_tensor(wstg2[:, 0:cols], wstg[:, 0:cols],
                                                   -1.0, wstg[:, 0:cols],
                                                   ALU.mult, ALU.max),
                  ("wstg",), ("wstg2",))
            t_dve(lambda e: e.tensor_reduce(cwork[:, 24:25], wstg2[:, 0:cols], AX, ALU.add),
                  ("wstg2",), ("cw_rs",))
            t_mm(scr[0:1, 256 + sccol:257 + sccol], onescF[:], cwork[:, 24:25],
                 reads=("cw_rs", "onescF"), writes=("scr",),
                 start=(ci == 0), stop=(ci == nch - 1))
        t_dve(lambda e: e.tensor_scalar_mul(cwork[0:1, 25:26],
                                            scr[0:1, 256 + sccol:257 + sccol],
                                            1.0 / (rowsn * cols)),
              ("scr",), ("cw_mean",))
        t_dve(lambda e: e.tensor_scalar_max(cwork[0:1, 25:26], cwork[0:1, 25:26], 1e-5),
              ("cw_mean",), ("cw_mean",))
        t_mm(scr[:, 288:289], ones_row[:], cwork[0:1, 25:26],
             reads=("cw_mean", "ones_row"), writes=("scr",), start=True, stop=True)
        t_dve(lambda e, s=sccol: e.tensor_copy(out=swsc[:, s + 1:s + 2], in_=scr[:, 288:289]),
              ("scr",), (f"{tag}meanb",))
        t_dve(lambda e, s=sccol: e.reciprocal(swsc[:, s:s + 1], swsc[:, s + 1:s + 2]),
              (f"{tag}meanb",), (f"{tag}swrec",))
        for ci in range(nch):
            t_dma("gpsimd", wstg[:, 0:cols], wT_[128 * ci:128 * (ci + 1), :], "dmg",
                  ("wstg",), ("wstg",))
            t_dve(lambda e, s=sccol: e.tensor_scalar_mul(wstg[:, 0:cols], wstg[:, 0:cols],
                                                         swsc[:, s:s + 1]),
                  ("wstg", f"{tag}swrec"), ("wstg",))
            t_dve(lambda e: e.tensor_scalar_add(wstg[:, 0:cols], wstg[:, 0:cols], RND_C),
                  ("wstg",), ("wstg",))
            t_dve(lambda e: e.tensor_scalar(wstg[:, 0:cols], wstg[:, 0:cols], -RND_C, None,
                                            ALU.add),
                  ("wstg",), ("wstg",))
            t_dve(lambda e: e.tensor_scalar(wstg2[:, 0:cols], wstg[:, 0:cols], 1.0, -1.0,
                                            ALU.min, ALU.max),
                  ("wstg",), ("wstg2",))
            t_dma("gpsimd", wqd[128 * ci:128 * (ci + 1), :], wstg2[:, 0:cols], "dmg",
                  ("wstg2",), (f"{tag}qd:{ci}", "wstg2"))

    emit_tern(w1T, DM, FFH, w1qd, "w1", 0)
    emit_tern(w2T, FFH, DM, w2qd, "w2", 2)
    tape.barrier([f"x32:{i}" for i in range(8)] + ["q2sb", "w2r", "qk_ln", "qk_rf"],
                 [f"osb:{s}" for s in range(32)])
    tape.barrier(["ropeCs", "ropeSs"] + [f"qkstg:{i}" for i in range(2)]
                 + [f"qksh:{i}" for i in range(2)] + [f"qkt2:{i}" for i in range(2)],
                 [f"ksup:{i}" for i in range(2)] + [f"vsup:{i}" for i in range(2)])
    tape.barrier([f"ha:{i}" for i in range(8)],
                 [f"pt:{i}" for i in range(2)])

    def load_super(s):
        par = s % 2
        semk = "dmk" if par == 0 else "dmk1"
        r, sub = s // 4, s % 4
        for h in range(NH):
            t_dma("sync",
                  ksup[par][(h % 2) * 64:(h % 2) * 64 + 64,
                            (h // 2) * SUP:(h // 2 + 1) * SUP],
                  kag[1024 * r + 64 * h:1024 * r + 64 * (h + 1),
                      SUP * sub:SUP * (sub + 1)],
                  semk, ("d_kag",), (f"ksup:{par}",) if h == 0 else ())
        tape.op("sync", lambda e, p=par, s_=s: e.dma_start(
                    out=vsup[p][:, 0:CPS * DM].rearrange("p (c d) -> p c d", c=CPS),
                    in_=vag[SUP * s_:SUP * (s_ + 1), :].rearrange(
                        "(c p) d -> p c d", p=128)),
                reads=("d_vag",), writes=(f"vsup:{par}",), sem=semk, inc=16)
        i = len(tape.ops) - 1
        tape.last_write[f"ksup:{par}"] = i
        tape.readers[f"ksup:{par}"] = []

    t_dve(lambda e: e.memset(ptb[0][:, 0:512], 0.0), (), ("pt:0",))
    t_mm(dbank[:], zer128[:, 0:64], ptb[0][:, 0:512], reads=("zer128", "pt:0"),
         writes=("dbank",), start=True, stop=False)

    load_super(0)
    load_super(1)

    def pv_emit(qb, p, head_par, owb, owtag, sup_par, conly=None):
        h = 2 * p + head_par
        row = 32 * head_par + 8 * qb + p
        pbuf = ptb[p % 2]
        for c in ([conly] if conly is not None else range(CPS)):
            u = 2 * c + head_par
            pslot = pbuf[:, 512 * u:512 * (u + 1)]
            vsl = vsup[sup_par][:, DM * c + 64 * h:DM * c + 64 * (h + 1)]
            t_mm(owb[64 * head_par:64 * head_par + 64, :], vsl, pslot,
                 reads=(f"pt:{p % 2}:{u}", f"vsup:{sup_par}"), writes=(owtag,),
                 start=(c == 0), stop=(c == CPS - 1), tp=(0, 64 * head_par))
            t_mm(dbank[32 * head_par:32 * head_par + 32, :],
                 onesel[:, 32 - (row % 32):64 - (row % 32)], pslot,
                 reads=(f"pt:{p % 2}:{u}", "onesel"), writes=("dbank",),
                 start=False, stop=False, tp=(0, 32 * head_par))

    def osb_evac(sup, p, qb):
        slot = p * 4 + qb
        if sup == 0:
            t_dve(lambda e, s=slot, o=ow[p % 2]: e.tensor_copy(
                    out=osb[:, 512 * s:512 * (s + 1)], in_=o[:]),
                  (f"ow:{p % 2}",), (f"osb:{slot}",))
        else:
            t_dve(lambda e, s=slot, o=ow[p % 2]: e.tensor_add(
                    osb[:, 512 * s:512 * (s + 1)],
                    osb[:, 512 * s:512 * (s + 1)], o[:]),
                  (f"ow:{p % 2}", f"osb:{slot}"), (f"osb:{slot}",))

    for sup in range(NSUP):
        par = sup % 2
        for qb in range(NQB):
            qsl = slice(QB * qb, QB * (qb + 1))
            for p in range(8):
                for c in range(CPS):
                    cs = c % 2
                    bA = stb[:, QB * (2 * cs):QB * (2 * cs) + QB]
                    bB = stb[:, QB * (2 * cs + 1):QB * (2 * cs + 1) + QB]
                    kA = ksup[par][0:64, p * SUP + 128 * c:p * SUP + 128 * (c + 1)]
                    kB = ksup[par][64:128, p * SUP + 128 * c:p * SUP + 128 * (c + 1)]
                    t_mm(bA, kA, qsb[p][0:64, qsl],
                         reads=(f"ksup:{par}", f"qsb:{p}"), writes=(f"stb:{2 * cs}",),
                         start=True, stop=True, tp=(0, 0))
                    t_mm(bB, kB, qsb[p][64:128, qsl],
                         reads=(f"ksup:{par}", f"qsb:{p}"), writes=(f"stb:{2 * cs + 1}",),
                         start=True, stop=True, tp=(64, 0))
                    t_act(ptb[p % 2][:, 1024 * c:1024 * (c + 1)],
                          stb[:, QB * 2 * cs:QB * (2 * cs + 2)], AF.Exp,
                          (f"stb:{2 * cs}", f"stb:{2 * cs + 1}", "mneg"),
                          (f"pt:{p % 2}:{2 * c}", f"pt:{p % 2}:{2 * c + 1}"),
                          bias=mneg[:, p:p + 1], scale=1.0)
                    if p > 0:
                        pv_emit(qb, p - 1, 1, ow[(p - 1) % 2], f"ow:{(p - 1) % 2}",
                                par, conly=c)
                    if c > 0:
                        pv_emit(qb, p, 0, ow[p % 2], f"ow:{p % 2}", par, conly=c - 1)
                if p > 0:
                    osb_evac(sup, p - 1, qb)
                pv_emit(qb, p, 0, ow[p % 2], f"ow:{p % 2}", par, conly=CPS - 1)
            pv_emit(qb, 7, 1, ow[1], "ow:1", par)
            osb_evac(sup, 7, qb)
        if sup + 2 < NSUP:
            load_super(sup + 2)

    t_mm(dbank[:], zer128[:, 0:64], ptb[0][:, 0:512], reads=("zer128", "pt:0"),
         writes=("dbank",), start=False, stop=True)

    t_dve(lambda e: e.reciprocal(drec[:], dbank[:]), ("dbank",), ("drec",))
    tap("t_drec", drec[:], ("drec",))
    tap("t_osb", osb[:], tuple(f"osb:{s_}" for s_ in range(32)))
    tape.barrier([f"ksup:{i}" for i in range(2)] + [f"vsup:{i}" for i in range(2)],
                 [f"woTs:{i}" for i in range(8)] + ["drec", "dbcs", "xstg:0", "xstg:1"])
    tape.barrier([f"pt:{i}" for i in range(2)], [f"onrm:{s}" for s in range(32)])
    for ci in range(8):
        t_dma("sync", wq32[:], woT[128 * ci:128 * (ci + 1), :], "dml",
              ("wq32_r",), ("wq32",))
        t_dve(lambda e, c=ci: e.tensor_copy(out=woTs[c][:], in_=wq32[:]),
              ("wq32",), (f"woTs:{ci}", "wq32_r"))
    for p in range(8):
        for qb in range(NQB):
            slot = p * 4 + qb
            rA = 8 * qb + p
            rB = 32 + 8 * qb + p
            t_dve(lambda e, r=rA: e.tensor_scalar(selD[:, 0:64], iotas[:, 0:64],
                                                  float(r), None, ALU.is_ge),
                  ("iotas", "selD"), ("selD",))
            t_dve(lambda e, r=rA: e.tensor_scalar(selDb[:, 0:64], iotas[:, 0:64],
                                                  float(r + 1), None, ALU.is_ge),
                  ("iotas", "selDb"), ("selDb",))
            t_dve(lambda e: e.tensor_sub(selD[:, 0:64], selD[:, 0:64], selDb[:, 0:64]),
                  ("selD", "selDb"), ("selD",))
            t_dve(lambda e, r=rB: e.tensor_scalar(selD[:, 64:128], iotas[:, 64:128],
                                                  float(r), None, ALU.is_ge),
                  ("iotas", "selD"), ("selD",))
            t_dve(lambda e, r=rB: e.tensor_scalar(selDb[:, 64:128], iotas[:, 64:128],
                                                  float(r + 1), None, ALU.is_ge),
                  ("iotas", "selDb"), ("selDb",))
            t_dve(lambda e: e.tensor_sub(selD[:, 64:128], selD[:, 64:128],
                                         selDb[:, 64:128]),
                  ("selD", "selDb"), ("selD",))
            t_mm(scr[:, 0:QB], selD[:], drec[:],
                 reads=("drec", "selD"), writes=("scr",), start=True, stop=True)
            t_dve(lambda e: e.tensor_copy(out=dbcs[:], in_=scr[:, 0:QB]), ("scr",), ("dbcs",))
            t_dve(lambda e, s=slot: e.tensor_mul(onrm[:, 512 * s:512 * (s + 1)],
                                                 osb[:, 512 * s:512 * (s + 1)], dbcs[:]),
                  (f"osb:{slot}", "dbcs"), (f"onrm:{slot}",))
    tape.barrier([f"osb:{s}" for s in range(32)], [f"x1:{i}" for i in range(8)])
    for oc in range(8):
        for qb in range(NQB):
            bank = stb[:, QB * qb:QB * qb + QB]
            for p in range(8):
                slot = p * 4 + qb
                t_mm(bank, woTs[p][:, 128 * oc:128 * (oc + 1)],
                     onrm[:, 512 * slot:512 * (slot + 1)],
                     reads=(f"woTs:{p}", f"onrm:{slot}"), writes=(f"stb:{qb}",),
                     start=(p == 0), stop=(p == 7))
            xs = xstg[qb % 2]
            t_dma("sync", xs[:], xT[128 * oc:128 * (oc + 1), QB * qb:QB * (qb + 1)],
                  "dmx" if qb % 2 == 0 else "dmx1",
                  (f"xstg:{qb % 2}",), (f"xstg:{qb % 2}",))
            t_dve(lambda e, b=bank, x=xs, o=oc, q=qb: e.scalar_tensor_tensor(
                    x1sb[o][:, QB * q:QB * (q + 1)], b, modg[:, o:o + 1], x[:],
                    ALU.mult, ALU.add),
                  (f"stb:{qb}", f"xstg:{qb % 2}", "modg"),
                  (f"x1:{oc}", f"xstg:{qb % 2}"))

    tape.barrier(["wq32", "wstg", "wstg2"] + [f"wbf:{i}" for i in range(8)]
                 + [f"kst:{i}" for i in range(2)] + [f"vst:{i}" for i in range(2)],
                 [f"hb:{i}" for i in range(8)])
    tape.last_write["svt_ln2"] = max(tape.last_write.get("modca", 0),
                                     tape.last_write.get("mod_sb", 0))
    emit_adaln_ln("x1", x1sb, "hb", hb, lambda c: modca[:, 8 + c:9 + c],
                  lambda c: mod_sb[:, 24 + c:25 + c], "svt_ln2", "ln2")

    tape.barrier([f"qsb:{i}" for i in range(8)], [f"h2:{i}" for i in (4, 5, 6, 7)])
    tape.barrier([f"woTs:{i}" for i in range(8)] + ["drec", "dbcs", "xstg:0", "xstg:1"],
                 [f"h2:{i}" for i in (0, 1, 2, 3)])
    h2 = h2lo + h2hi
    tape.last_write["svt_fln"] = max(tape.last_write.get("gammacs", 0),
                                     tape.last_write.get("betacs", 0))
    emit_adaln_ln("hb", hb, "h2", h2, lambda c: gammacs[:, c:c + 1],
                  lambda c: betacs[:, c:c + 1], "svt_fln", "fln", src_bf16=True)

    for i in range(8):
        t_dma("sync", x1d[128 * i:128 * (i + 1), :], x1sb[i][:], "dms",
              (f"x1:{i}",), (f"x1d:{i}",))
    tape.batch_end([f"x1d:{i}" for i in range(8)])
    tape.dma_drain("sync", "dms", SEMS)
    tap("t_x1", x1d, tuple(f"x1d:{i}" for i in range(8)))
    tap("t_h20", h2[0][:], ("h2:0",))
    tape.barrier([f"onrm:{s}" for s in range(32)], ["q1mx", "q1sx", "q1rec"])
    tape.barrier([f"hb:{i}" for i in range(2)] + ["x1:0"],
                 [f"xq1w:{i}" for i in range(2)] + ["tq2p", "q1v"])

    def emit_quant(srcs, src_tags, dsts, dst_tags, tag, mrow_ap):
        t_dve(lambda e: e.scalar_tensor_tensor(mxf[:], srcs[0], -1.0, srcs[0],
                                               ALU.mult, ALU.max),
              reads=(src_tags[0], "mxf"), writes=(f"{tag}mx", "mxf"))
        for ci in range(1, len(srcs)):
            t_dve(lambda e, c=ci: e.scalar_tensor_tensor(mxf[:], srcs[c], 1.0, mxf[:],
                                                         ALU.mult, ALU.max),
                  reads=(src_tags[ci], f"{tag}mx"), writes=(f"{tag}mx", "mxf"))
            t_dve(lambda e, c=ci: e.scalar_tensor_tensor(mxf[:], srcs[c], -1.0, mxf[:],
                                                         ALU.mult, ALU.max),
                  reads=(src_tags[ci], f"{tag}mx"), writes=(f"{tag}mx", "mxf"))
        for b in range(FQB // 128):
            tape.op("tensor", lambda e, b_=b: e.transpose(scr[:, 0:128],
                                                          mxf[:, 128 * b_:128 * (b_ + 1)],
                                                          idn[:]),
                    reads=(f"{tag}mx", "idn", "scr"), writes=("scr",))
            t_dve(lambda e, b_=b: e.tensor_reduce(mxt[:, b_:b_ + 1], scr[:, 0:128],
                                                  AX, ALU.max),
                  ("scr", "mxt"), ("mxt",))
        t_dve(lambda e: e.tensor_scalar_max(mxt[:, 0:FQB // 128], mxt[:, 0:FQB // 128],
                                            1e-5), ("mxt",), ("mxt",))
        t_dve(lambda e: e.reciprocal(mxr[:, 0:FQB // 128], mxt[:, 0:FQB // 128]),
              ("mxt",), ("mxr",))
        t_dve(lambda e: e.tensor_scalar_mul(mxr[:, 0:FQB // 128], mxr[:, 0:FQB // 128],
                                            127.0), ("mxr",), ("mxr",))
        for b in range(FQB // 128):
            tape.op("tensor", lambda e, b_=b: e.transpose(scr[0:1, 128:256],
                                                          mxt[:, b_:b_ + 1], idn[:]),
                    reads=("mxt", "idn", "scr"), writes=("scr",))
            t_dve(lambda e, b_=b: e.tensor_copy(out=mrow_ap[:, 128 * b_:128 * (b_ + 1)],
                                                in_=scr[0:1, 128:256]),
                  ("scr", "rows0"), (f"{tag}mrow", "rows0"))
            tape.op("tensor", lambda e, b_=b: e.transpose(scr[0:1, 128:256],
                                                          mxr[:, b_:b_ + 1], idn[:]),
                    reads=("mxr", "idn", "scr"), writes=("scr",))
            t_dve(lambda e, b_=b: e.tensor_copy(out=tq[0:1, 128 * b_:128 * (b_ + 1)],
                                                in_=scr[0:1, 128:256]),
                  ("scr", "tq"), (f"{tag}sxrow", "tq"))
        t_mm(ow[0][:, 0:FQB], ones_row[:], tq[0:1, 0:FQB],
             reads=(f"{tag}sxrow", "ones_row"), writes=("ow0",), start=True, stop=True)
        for ci in range(len(srcs)):
            t_dve(lambda e, c=ci: e.tensor_mul(tf[:], srcs[c], ow[0][:, 0:FQB]),
                  reads=(src_tags[ci], "ow0", "tf"), writes=(f"{tag}v", "tf"))
            t_dve(lambda e: e.tensor_scalar_add(tf[:], tf[:], RND_C),
                  reads=(f"{tag}v",), writes=(f"{tag}v", "tf"))
            t_dve(lambda e, c=ci: e.tensor_scalar(dsts[c], tf[:], -RND_C, None, ALU.add),
                  reads=(f"{tag}v",), writes=(dst_tags[ci],))


    t_dve(lambda e: e.scalar_tensor_tensor(mxrow[:], h2[0][:], -1.0, h2[0][:],
                                           ALU.mult, ALU.max),
          reads=("h2:0",), writes=("q1mx",))
    for ci in range(1, 8):
        t_dve(lambda e, c=ci: e.scalar_tensor_tensor(mxrow[:], h2[c][:], 1.0, mxrow[:],
                                                     ALU.mult, ALU.max),
              reads=(f"h2:{ci}", "q1mx"), writes=("q1mx",))
        t_dve(lambda e, c=ci: e.scalar_tensor_tensor(mxrow[:], h2[c][:], -1.0, mxrow[:],
                                                     ALU.mult, ALU.max),
              reads=(f"h2:{ci}", "q1mx"), writes=("q1mx",))
    for b in range(16):
        tape.op("tensor", lambda e, b_=b: e.transpose(scr[:, 0:128],
                                                      mxrow[:, 128 * b_:128 * (b_ + 1)],
                                                      idn[:]),
                reads=("q1mx", "idn", "scr"), writes=("scr",))
        t_dve(lambda e, b_=b: e.tensor_reduce(mxt[:, b_:b_ + 1], scr[:, 0:128],
                                              AX, ALU.max),
              ("scr", "mxt"), ("mxt",))
    t_dve(lambda e: e.tensor_scalar_max(mxt[:], mxt[:], 1e-5), ("mxt",), ("mxt",))
    t_dve(lambda e: e.reciprocal(mxr[:], mxt[:]), ("mxt",), ("mxr",))
    t_dve(lambda e: e.tensor_scalar_mul(mxr[:], mxr[:], 127.0), ("mxr",), ("mxr",))
    for b in range(16):
        tape.op("tensor", lambda e, b_=b: e.transpose(scr[0:1, 128:256],
                                                      mxt[:, b_:b_ + 1], idn[:]),
                reads=("mxt", "idn", "scr"), writes=("scr",))
        t_dve(lambda e, b_=b: e.tensor_copy(out=rows[0:1, 128 * b_:128 * (b_ + 1)],
                                            in_=scr[0:1, 128:256]),
              ("scr", "rows0"), ("q1m", "rows0"))
        tape.op("tensor", lambda e, b_=b: e.transpose(scr[0:1, 128:256],
                                                      mxr[:, b_:b_ + 1], idn[:]),
                reads=("mxr", "idn", "scr"), writes=("scr",))
        t_dve(lambda e, b_=b: e.tensor_copy(out=mxrow[0:1, 128 * b_:128 * (b_ + 1)],
                                            in_=scr[0:1, 128:256]),
              ("scr", "q1mx"), ("q1sxrow", "q1mx"))
    for ci in range(8):
        w = xq1w[ci % 2]
        for qb in range(NQB):
            sl = slice(QB * qb, QB * (qb + 1))
            t_mm(ow[0][:, 0:QB], ones_row[:], mxrow[0:1, sl],
                 reads=("q1sxrow", "ones_row"), writes=("ow0",), start=True, stop=True)
            t_dve(lambda e, c=ci, s=sl: e.tensor_mul(tq2[:], h2[c][:, s], ow[0][:, 0:QB]),
                  (f"h2:{ci}", "ow0", "tq2p"), ("q1v", "tq2p"))
            t_dve(lambda e: e.tensor_scalar_add(tq2[:], tq2[:], RND_C),
                  ("q1v",), ("q1v", "tq2p"))
            t_dve(lambda e, w_=w, s=sl: e.tensor_scalar(w_[:, s], tq2[:], -RND_C, None,
                                                        ALU.add),
                  ("q1v",), (f"xq1w:{ci % 2}",))
        t_dma("sync", xq1d[128 * ci:128 * (ci + 1), :], w[:],
              "dms" if ci % 2 == 0 else "dms1",
              (f"xq1w:{ci % 2}",), (f"xq1d:{ci}", f"xq1w:{ci % 2}"))
    tape.barrier([f"xq1d:{i}" for i in range(8)], [])
    tape.batch_end([f"xq1d:{i}" for i in range(8)])
    tape.dma_drain("sync", "dms", SEMS)
    tape.dma_drain("sync", "dms1", SEMS)



    tape.barrier([f"x1d:{i}" for i in range(8)] + ["tq2p", "q1v"],
                 [f"w1q:{i}" for i in range(8)])
    tape.barrier([f"h2:{i}" for i in range(8)] + ["q1mx", "q1sxrow", "q1m",
                  "wstg", "wstg2"],
                 [f"w2q:{i}" for i in range(32)] + [f"hm:{h}" for h in range(32)]
                 + [f"xq2:{c}" for c in range(32)] + [f"xq1s:{i}" for i in range(8)]
                 + ["xstg2:0", "xstg2:1", "pscs", "psc2s", "tf", "tf2", "sxf", "mxf"])
    for ci in range(8):
        t_dma("sync", w1q[ci][:], w1qd[128 * ci:128 * (ci + 1), :], "dml",
              (f"w1qd:{ci}", f"w1q:{ci}"), (f"w1q:{ci}",))
    tape.batch_end([f"w1q:{i}" for i in range(8)])
    tape.dma_drain("sync", "dml", SEMS)
    w2q = w2ql + w2qh
    for ci in range(32):
        t_dma("sync", w2q[ci][:], w2qd[128 * ci:128 * (ci + 1), :], "dml",
              (f"w2qd:{ci}", f"w2q:{ci}"), (f"w2q:{ci}",))
    tape.batch_end([f"w2q:{i}" for i in range(32)])
    tape.dma_drain("sync", "dml", SEMS)

    tap("t_xq1", xq1d, tuple(f"xq1d:{i}" for i in range(8)))
    tap("t_w1q", w1qd, tuple(f"w1qd:{i}" for i in range(8)))
    tap("t_swsc", swsc[:], ("w1swrec", "w2swrec", "w1meanb", "w2meanb"))
    t_dve(lambda e: e.tensor_scalar_mul(rows[0:1, :], rows[0:1, :], 1.0 / 127.0),
          ("q1m",), ("q1m", "rows0"))
    t_dve(lambda e: e.tensor_scalar_mul(rows[0:1, :], rows[0:1, :], swsc[0:1, 1:2]),
          ("q1m", "w1meanb"), ("q1scale", "rows0"))

    for qf in range(NFQB):
        fsl = slice(FQB * qf, FQB * (qf + 1))
        for ic in range(8):
            t_dma("sync", xq1s[ic][:], xq1d[128 * ic:128 * (ic + 1), fsl], "dml",
                  (f"xq1d:{ic}", f"xq1s:{ic}"), (f"xq1s:{ic}",))
        tape.batch_end([f"xq1s:{i}" for i in range(8)])
        t_mm(scr[:, 0:FQB], ones_row[:], rows[0:1, fsl], reads=("q1scale", "ones_row"),
             writes=("scr",), start=True, stop=True)
        t_dve(lambda e: e.tensor_copy(out=pscs[:], in_=scr[:, 0:FQB]), ("scr",), ("pscs",))
        for hc in range(32):
            bank = stb[:, QB * (hc % 2):QB * (hc % 2) + FQB]
            for ic in range(8):
                t_mm(bank, w1q[ic][:, 128 * hc:128 * (hc + 1)], xq1s[ic][:],
                     reads=(f"w1q:{ic}", f"xq1s:{ic}"), writes=(f"stb:{hc % 2}",),
                     start=(ic == 0), stop=(ic == 7))
            t_dve(lambda e, b=bank: e.tensor_mul(tf[:], b, pscs[:]),
                  (f"stb:{hc % 2}", "pscs"), ("tf",))
            t_act(hmid[hc][:], tf[:], AF.Gelu, ("tf", "b1cs", "hmid_all"),
                  ("hmid_all", f"hm:{hc}"), bias=b1cs[:, hc:hc + 1], scale=1.0)
        if qf == 0:
            tap("t_hm0", hmid[0][:], ("hm:0",))
        emit_quant([hmid[c][:] for c in range(32)],
                   [f"hm:{c}" for c in range(32)],
                   [xq2[c][:] for c in range(32)],
                   [f"xq2:{c}" for c in range(32)],
                   f"q2_{qf}", rows[0:1, fsl])
        t_dve(lambda e, s=fsl: e.tensor_scalar_mul(rows[0:1, s], rows[0:1, s], 1.0 / 127.0),
              (f"q2_{qf}mrow",), (f"q2_{qf}mrow", "rows0"))
        t_dve(lambda e, s=fsl: e.tensor_scalar_mul(rows[0:1, s], rows[0:1, s],
                                                   swsc[0:1, 3:4]),
              (f"q2_{qf}mrow", "w2meanb"), (f"q2_{qf}scale", "rows0"))
        t_mm(scr[:, 0:FQB], ones_row[:], rows[0:1, fsl],
             reads=(f"q2_{qf}scale", "ones_row"), writes=("scr",), start=True, stop=True)
        t_dve(lambda e: e.tensor_copy(out=psc2s[:], in_=scr[:, 0:FQB]), ("scr",), ("psc2s",))
        if qf == 0:
            tap("t_xq2a", xq2[0][:], ("xq2:0",))
            tap("t_xq2b", xq2[31][:], ("xq2:31",))
            tap("t_psc2", psc2s[:], ("psc2s",))
            if debug:
                for c_ in range(32):
                    tap_ap = dbgout["t_xq2f"][128 * c_:128 * (c_ + 1), :]
                    tape.op("gpsimd", lambda e, a=tap_ap, c2=c_: e.dma_start(
                                out=a, in_=xq2[c2][:]),
                            reads=(f"xq2:{c_}",), writes=(), sem="dmg", inc=16)
                tap("t_w2q", w2qd, tuple(f"w2qd:{i}" for i in range(32)))
        for oc in range(8):
            bank = stb[:, 2 * QB + QB * (oc % 2):2 * QB + QB * (oc % 2) + FQB]
            for hc in range(32):
                t_mm(bank, w2q[hc][:, 128 * oc:128 * (oc + 1)], xq2[hc][:],
                     reads=(f"w2q:{hc}", f"xq2:{hc}", "xq2_all"),
                     writes=(f"stb2:{oc % 2}",),
                     start=(hc == 0), stop=(hc == 31))
            t_dve(lambda e, b=bank: e.tensor_mul(tf[:], b, psc2s[:]),
                  (f"stb2:{oc % 2}", "psc2s"), ("tf",))
            if qf == 0 and oc == 6:
                tap("t_y26", tf[:], ("tf",))
            if qf == 0 and debug:
                tap_ap2 = dbgout["t_y2all"][128 * oc:128 * (oc + 1), :]
                tape.op("gpsimd", lambda e, a=tap_ap2: e.dma_start(out=a, in_=tf[:]),
                        reads=("tf",), writes=(), sem="dmg", inc=16)
            xs = xstg2[oc % 2]
            t_dma("sync", xs[:, 0:FQB], x1d[128 * oc:128 * (oc + 1), fsl],
                  "dmx" if oc % 2 == 0 else "dmx1",
                  (f"x1d:{oc}", f"xstg2:{oc % 2}"), (f"xstg2:{oc % 2}",))
            t_dve(lambda e, x=xs, o=oc: e.scalar_tensor_tensor(
                    tf2[:], tf[:], modg[:, 8 + o:9 + o], x[:, 0:FQB], ALU.mult, ALU.add),
                  ("tf", f"xstg2:{oc % 2}", "modg"), ("tf2", f"xstg2:{oc % 2}"))
            t_dve(lambda e, o=oc: e.tensor_scalar_add(tf[:], tf2[:], gbb2[:, o:o + 1]),
                  ("tf2", "gbb2"), ("tf", f"x2t:{oc}:{qf}"))
            t_dma("gpsimd", outT[128 * oc:128 * (oc + 1), fsl], tf[:], "dmg",
                  (f"x2t:{oc}:{qf}", "tf"), (f"out:{oc}:{qf}",))

    tape.op("gpsimd", lambda e: e.engine_nop(),
            reads=tuple(f"out:{o}:{q}" for o in range(8) for q in range(NFQB)),
            writes=("done",))

    from contextlib import ExitStack
    es = ExitStack()
    sem_keys = ["tensor", "vector", "scalar", "gpsimd", "dml", "dmx", "dmx1",
                "dmk", "dmk1", "dms", "dms1", "dmg", "cc"]
    sems = {k: es.enter_context(nc.semaphore(f"s_{k}")) for k in sem_keys}
    SEMS.update(sems)
    block = es.enter_context(nc.Block())
    for name in ("tensor", "vector", "scalar", "gpsimd", "sync"):
        def f(eng, n=name):
            tape.emit_engine(n, eng, sems)
        getattr(block, name)(f)
    es.close()
    return nc



def _stage(inputs):
    x = inputs["L0_attn_x"]
    cos, sin = np.asarray(inputs["L0_attn_rope_cos"]), np.asarray(inputs["L0_attn_rope_sin"])
    qn, kn = np.asarray(inputs["L0_attn_q_norm_w"]), np.asarray(inputs["L0_attn_k_norm_w"])
    x = np.asarray(x)
    wqT = np.ascontiguousarray(np.asarray(inputs["L0_attn_wq"]).T)
    wkT = np.ascontiguousarray(np.asarray(inputs["L0_attn_wk"]).T)
    wvT = np.ascontiguousarray(np.asarray(inputs["L0_attn_wv"]).T)
    woT = np.ascontiguousarray(np.asarray(inputs["L0_attn_wo"]).T)
    w1T = np.ascontiguousarray(np.asarray(inputs["L0_ffn_w1"]).T)
    w2T = np.ascontiguousarray(np.asarray(inputs["L0_ffn_w2"]).T)
    wmodT = np.ascontiguousarray(np.asarray(inputs["L0_wmod"]))
    b1cv = np.ascontiguousarray(np.asarray(inputs["L0_ffn_b1"]).reshape(32, 128).T)
    b2cv = np.ascontiguousarray(np.asarray(inputs["L0_ffn_b2"]).reshape(8, 128).T)
    gammarv = np.ascontiguousarray(np.asarray(inputs["L0_ffn_gamma"]).reshape(8, 128).T)
    betacv = np.ascontiguousarray(np.asarray(inputs["L0_ffn_beta"]).reshape(8, 128).T)
    iden = np.eye(128, dtype=np.float32)
    iotaMv = np.tile(np.arange(64, dtype=np.float32)[:, None], (1, 128))
    altv = np.zeros((128, 8), np.float32)
    altv[0::2, 0] = -1.0
    altv[1::2, 0] = 1.0
    wqn2v = np.zeros((2, 128), np.float32)
    wqn2v[0, 0:64] = qn
    wqn2v[1, 64:128] = qn
    wkn2v = np.zeros((2, 128), np.float32)
    wkn2v[0, 0:64] = kn
    wkn2v[1, 64:128] = kn
    maps = []
    for core in range(8):
        b, s = core // 4, core % 4
        xTv = np.ascontiguousarray(x[b, 2048 * s:2048 * (s + 1), :].T)
        c_rowv = np.ascontiguousarray(np.asarray(inputs["L0_c"])[b].reshape(8, 128).T)
        t0 = 2048 * s
        C = np.ones((128, T), np.float32)
        S = np.zeros((128, T), np.float32)
        lo = max(0, (SEQ - PATCH) - t0)
        if lo < T:
            pos0 = t0 + lo - (SEQ - PATCH)
            n = T - lo
            crep = np.repeat(cos[pos0:pos0 + n, :].T, 2, axis=0)
            srep = np.repeat(sin[pos0:pos0 + n, :].T, 2, axis=0)
            C[:, lo:] = np.tile(crep, (2, 1))
            S[:, lo:] = np.tile(srep, (2, 1))
        maps.append(dict(
            xT=xTv, wqT=wqT, wkT=wkT, wvT=wvT, woT=woT, w1T=w1T, w2T=w2T,
            wmodT=wmodT, c_row=c_rowv, ropeC=np.ascontiguousarray(C),
            ropeS=np.ascontiguousarray(S), alt=altv, wqn2=wqn2v, wkn2=wkn2v,
            b1c=b1cv, b2c=b2cv, gammar=gammarv, betac=betacv, iden=iden,
            iotaM=iotaMv,
        ))
    return maps


def kernel(**inputs):
    maps = _stage(inputs)
    nc = build()
    res = run_bass_kernel_spmd(nc, maps, core_ids=list(range(8)))
    out = np.empty((2, 8192, 1024), np.float32)
    for core in range(8):
        b, s = core // 4, core % 4
        out[b, 2048 * s:2048 * (s + 1), :] = res.results[core]["outT"].T
    return out


# version 10
# speedup vs baseline: 1.0108x; 1.0108x over previous
import math
import numpy as np
import concourse.bass as bass
import concourse.mybir as mybir
from concourse.bass_utils import run_bass_kernel_spmd
from concourse import library_config
from concourse.bass_isa import ReduceOp

F32 = mybir.dt.float32
BF16 = mybir.dt.bfloat16
AF = mybir.ActivationFunctionType
ALU = mybir.AluOpType
AX = mybir.AxisListType.X

T = 2048
DM = 1024
NH = 16
HD = 64
SEQ = 8192
PATCH = 4064
FFH = 4096
QB = 512
NQB = T // QB
SUP = 512
NSUP = SEQ // SUP
CPS = SUP // 128
FQB = 256
NFQB = T // FQB
RND_C = float(3 << 22)
PAIRSWAP = [x for i in range(16) for x in (2 * i + 1, 2 * i)]

ENG = ("tensor", "vector", "scalar", "gpsimd", "sync")


class Tape:
    def __init__(self):
        self.ops = []
        self.last_write = {}
        self.readers = {}
        self.counts = {}
        self.op_sem = []
        self.drains = {}
        self._W = None

    def _build_W(self):
        import bisect
        W = {}
        for (_, _, waits, _) in self.ops:
            for k, v in waits.items():
                W.setdefault(k, set()).add(v)
        for (k, v) in self.drains.values():
            if v > 0:
                W.setdefault(k, set()).add(v)
        self._W = {k: sorted(s) for k, s in W.items()}
        self._Wset = {k: set(s) for k, s in W.items()}

    def remap(self, k, v, unit):
        import bisect
        if str(k).startswith("dm") or str(k) == "cc":
            return v
        lst = self._W.get(k, [])
        return unit * bisect.bisect_right(lst, v)

    def op(self, engine, emit, reads=(), writes=(), sem=None, inc=1, extra_deps=()):
        idx = len(self.ops)
        deps = set(extra_deps)
        for r in reads:
            w = self.last_write.get(r)
            if w is not None:
                deps.add(w)
            self.readers.setdefault(r, []).append(idx)
        for r in writes:
            w = self.last_write.get(r)
            if w is not None:
                deps.add(w)
            for rd in self.readers.get(r, ()):
                deps.add(rd)
            self.last_write[r] = idx
            self.readers[r] = []
        key = sem if sem is not None else engine
        cnt = self.counts.get(key, 0) + inc
        self.counts[key] = cnt
        self.op_sem.append((key, cnt))
        waits = {}
        for d in deps:
            if d == idx:
                continue
            k, v = self.op_sem[d]
            if waits.get(k, -1) < v:
                waits[k] = v
        self.ops.append((engine, emit, waits, (key, inc)))
        return idx

    def dma_drain(self, queue, semk, sems_ref):
        v = self.counts.get(semk, 0)
        idx = self.op(queue,
                      lambda e, v_=v, k_=semk: (
                          e.wait_ge(sems_ref[k_], self.remap(k_, v_, 16)), None)[1],
                      sem=semk, inc=0)
        self.drains[idx] = (semk, v)

    def batch_end(self, resources):
        i = len(self.ops) - 1
        for r in resources:
            if r in self.last_write:
                self.last_write[r] = i

    def barrier(self, writes, seed):
        idx = self.op("vector", lambda e: e.engine_nop(), writes=tuple(writes))
        for s in seed:
            self.last_write[s] = idx
            self.readers[s] = []
        return idx

    def emit_engine(self, engine, eng, sems):
        if self._W is None:
            self._build_W()
        satisfied = {}
        for i, (e, emit, waits, (key, inc)) in enumerate(self.ops):
            if e != engine:
                continue
            for k in sorted(waits, key=str):
                unit = 16 if str(k).startswith(("dm", "cc_x")) else 1
                v = self.remap(k, waits[k], unit)
                if satisfied.get(k, -1) >= v:
                    continue
                eng.wait_ge(sems[k], v)
                satisfied[k] = v
            ins = emit(eng)
            if ins is not None and inc > 0:
                cnt = self.op_sem[i][1]
                if (str(key).startswith("dm") or str(key) == "cc"
                        or cnt in self._Wset.get(key, ())):
                    ins.then_inc(sems[key], inc)


def build(debug=False):
    nc = bass.Bass("TRN2", target_bir_lowering=False)
    ein = lambda n, s, dt=F32: nc.dram_tensor(n, s, dt, kind="ExternalInput").ap()
    eout = lambda n, s, dt=F32: nc.dram_tensor(n, s, dt, kind="ExternalOutput").ap()
    idram = lambda n, s, dt=F32: nc.dram_tensor(n, s, dt).ap()

    xT = ein("xT", [DM, T])
    wqT = ein("wqT", [DM, DM])
    wkT = ein("wkT", [DM, DM])
    wvT = ein("wvT", [DM, DM])
    woT = ein("woT", [DM, DM])
    w1T = ein("w1T", [DM, FFH])
    w2T = ein("w2T", [FFH, DM])
    wmodT = ein("wmodT", [DM, 6 * DM])
    c_row = ein("c_row", [128, 8])
    ropeC = ein("ropeC", [128, T])
    ropeS = ein("ropeS", [128, T])
    alt = ein("alt", [128, 8])
    wqn2 = ein("wqn2", [2, 128])
    wkn2 = ein("wkn2", [2, 128])
    b1c = ein("b1c", [128, 32])
    b2c = ein("b2c", [128, 8])
    gammar = ein("gammar", [128, 8])
    betac = ein("betac", [128, 8])
    iden = ein("iden", [128, 128])
    iotaM = ein("iotaM", [64, 128])
    outT = eout("outT", [DM, T])

    kbounce = idram("kbounce", [DM, T], BF16)
    kag = idram("kag", [4 * DM, T], BF16)
    vbounce = idram("vbounce", [T, DM], BF16)
    vag = idram("vag", [SEQ, DM], BF16)
    knb = idram("knb", [2, 8], F32)
    kna = idram("kna", [2, 8], F32)
    x1d = idram("x1d", [DM, T], F32)
    xq1d = idram("xq1d", [DM, T], BF16)
    w1qd = idram("w1qd", [DM, FFH], BF16)
    w2qd = idram("w2qd", [FFH, DM], BF16)
    groups = [[0, 1, 2, 3], [4, 5, 6, 7]]
    dbgout = {}
    if debug:
        for nm, shape, dt in [
            ("t_xq2a", [128, FQB], BF16), ("t_xq2b", [128, FQB], BF16),
            ("t_xq2f", [FFH, FQB], BF16), ("t_y2all", [DM, FQB], F32),
            ("t_w2q", [FFH, DM], BF16),
            ("t_psc2", [128, FQB], F32), ("t_y26", [128, FQB], F32),
            ("t_mod", [128, 48], F32), ("t_ha0", [128, T], BF16),
            ("t_q0", [128, T], BF16), ("t_kb", [DM, T], BF16),
            ("t_vb", [T, DM], BF16), ("t_mneg", [128, 8], F32),
            ("t_drec", [64, QB], F32), ("t_osb", [128, 32 * QB], F32),
            ("t_x1", [DM, T], F32), ("t_h20", [128, T], BF16),
            ("t_xq1", [DM, T], BF16), ("t_w1q", [DM, FFH], BF16),
            ("t_hm0", [128, FQB], BF16), ("t_swsc", [128, 8], F32),
            ("t_kag", [4 * DM, T], BF16), ("t_vag", [SEQ, DM], BF16),
        ]:
            dbgout[nm] = eout(nm, shape, dt)

    BASE = 16512
    R_A, R_B, R_C, R_D, R_E = (BASE, BASE + 65536, BASE + 98304, BASE + 131072,
                               BASE + 163840)
    R_S = BASE + 196608
    SM = BASE + 204800

    def at(name, shape, dt, off):
        return nc.alloc_sbuf_tensor_at(name, shape, dt, offset=off)

    x32 = [at(f"x32_{i}", [128, T], F32, R_A + i * 8192) for i in range(8)]
    osb = at("osb", [128, 32 * QB], F32, R_A)
    x1sb = [at(f"x1_{i}", [128, T], F32, R_A + i * 8192) for i in range(8)]
    w1q = [at(f"w1q_{i}", [128, FFH], BF16, R_A + i * 8192) for i in range(8)]
    ha = [at(f"ha_{i}", [128, T], BF16, R_B + i * 4096) for i in range(8)]
    ptb = [at(f"ptb_{i}", [128, 4 * SUP * 2], BF16, R_B + i * 8192) for i in range(2)]
    onrm = at("onrm", [128, 32 * QB], BF16, R_B)
    ropeCs = at("ropeCs", [128, T], F32, R_C)
    ropeSs = at("ropeSs", [128, T], F32, R_C + 8192)
    qstg = [at(f"qstg_{i}", [128, QB], F32, R_C + 16384 + i * 2048) for i in range(4)]
    ksup = [at(f"ksup_{i}", [128, CPS * 128 * 8], BF16, R_C + i * 8192) for i in range(2)]
    vsup = [at(f"vsup_{i}", [128, CPS * DM], BF16, R_C + 16384 + i * 8192) for i in range(2)]
    woTs = [at(f"woTs_{i}", [128, DM], BF16, R_C + i * 2048) for i in range(8)]
    h2lo = [at(f"h2lo_{i}", [128, T], BF16, R_C + 16384 + i * 4096) for i in range(4)]
    w2qh = [at(f"w2qh_{i}", [128, DM], BF16, R_C + i * 2048) for i in range(16)]
    wstg = at("wstg", [128, FFH], F32, R_D)
    wstg2 = at("wstg2", [128, FFH], BF16, R_D + 16384)
    wbf = [at(f"wbf_{i}", [128, DM], BF16, R_D + i * 2048) for i in range(8)]
    wq32 = at("wq32", [128, DM], F32, R_D + 16384)
    kstg = [at(f"kstg_{i}", [128, T], BF16, R_D + 20480 + i * 4096) for i in range(2)]
    vstg = [at(f"vstg_{i}", [128, DM], BF16, R_D + 28672 + i * 2048) for i in range(2)]
    hb = [at(f"hb_{i}", [128, T], BF16, R_D + i * 4096) for i in range(8)]
    hmid = [at(f"hmid_{i}", [128, FQB], BF16, R_D + i * 512) for i in range(32)]
    xq2 = [at(f"xq2_{i}", [128, FQB], BF16, R_D + 16384 + i * 512) for i in range(32)]
    xq1s = [at(f"xq1s_{i}", [128, FQB], BF16, R_B + 10240 + i * 512) for i in range(8)]
    xq1w = [at(f"xq1w_{i}", [128, T], BF16, R_D + i * 4096) for i in range(2)]
    qsb = [at(f"qsb_{i}", [128, T], BF16, R_E + i * 4096) for i in range(8)]
    h2hi = [at(f"h2hi_{i}", [128, T], BF16, R_E + i * 4096) for i in range(4)]
    w2ql = [at(f"w2ql_{i}", [128, DM], BF16, R_E + i * 2048) for i in range(16)]
    rows = at("rows", [6, T], F32, R_S)
    tts = at("tts", [128, T], F32, R_B + 24576)
    q2sb = at("q2sb", [128, QB], F32, R_A)
    w2r = at("w2r", [128, QB], F32, R_A + 2048)
    tq2 = at("tq2", [128, QB], F32, R_A + 4096)
    q2sbB = at("q2sbB", [128, QB], F32, R_A + 6144)
    w2rB = at("w2rB", [128, QB], F32, R_A + 8192)
    tq2B = at("tq2B", [128, QB], F32, R_A + 10240)
    tqB = at("tqB", [128, QB], F32, R_A + 12288)
    drec = at("drec", [64, QB], F32, R_C + 16384)
    dbcs = at("dbcs", [128, QB], F32, R_C + 18432)
    xstg = [at(f"xstg_{i}", [128, QB], F32, R_C + 20480 + i * 2048)
            for i in range(2)]
    xstg2 = [at(f"xstg2_{i}", [128, QB], F32, R_B + i * 2048) for i in range(2)]
    pscs = at("pscs", [128, FQB], F32, R_B + 4096)
    psc2s = at("psc2s", [128, FQB], F32, R_B + 5120)
    tf = at("tf", [128, FQB], F32, R_B + 6144)
    tf2 = at("tf2", [128, FQB], F32, R_B + 7168)
    sxf = at("sxf", [128, FQB], F32, R_B + 8192)
    mxf = at("mxf", [128, FQB], F32, R_B + 9216)
    off = [SM]

    def sm(name, shape, dt):
        nbytes = int(np.prod(shape[1:])) * mybir.dt.size(dt)
        t = at(name, shape, dt, off[0])
        off[0] += (nbytes + 31) // 32 * 32
        return t

    tq = sm("tq", [128, QB], F32)
    CONSTS = [0.0, 1.0, -1.0, -0.5, 0.5, 1e-5, 1e-6, 1.0 / HD, math.log(0.125)]
    cpool = sm("cpool", [128, len(CONSTS)], F32)
    mod_sb = sm("mod_sb", [128, 48], F32)
    modg = sm("modg", [128, 16], F32)
    modca = sm("modca", [128, 16], F32)
    gbb2 = sm("gbb2", [128, 8], F32)
    csrow = sm("csrow", [128, 8], F32)
    cwork = sm("cwork", [128, 32], F32)
    onesel = sm("onesel", [128, 64], BF16)
    zer128 = sm("zer128", [128, 128], BF16)
    idn = sm("idn", [128, 128], F32)
    alts = sm("alts", [128, 8], F32)
    wqn2s = sm("wqn2s", [2, 128], F32)
    wkn2s = sm("wkn2s", [2, 128], F32)
    b1cs = sm("b1cs", [128, 32], F32)
    b2cs = sm("b2cs", [128, 8], F32)
    gammacs = sm("gammacs", [128, 8], F32)
    betacs = sm("betacs", [128, 8], F32)
    mneg = sm("mneg", [128, 8], F32)
    qn2m = sm("qn2m", [2, 16], F32)
    kn2m = sm("kn2m", [2, 8], F32)
    kn2g = sm("kn2g", [2, 8], F32)
    mm2s = sm("mm2s", [2, 8], F32)
    selD = sm("selD", [64, 128], F32)
    selDb = sm("selDb", [64, 128], F32)
    mxt = sm("mxt", [128, 16], F32)
    mxr = sm("mxr", [128, 16], F32)
    iotas = sm("iotas", [64, 128], F32)
    hselF = sm("hselF", [128, 2], F32)
    ones_row = sm("ones_row", [1, 128], F32)
    onescF = sm("onescF", [128, 1], F32)
    csrowb = sm("csrowb", [128, 8], BF16)
    swsc = sm("swsc", [128, 8], F32)
    assert off[0] <= 229344, f"smalls overflow {off[0]}"

    sxr = at("sxr", [128, T], F32, R_B)
    mxrow = at("mxrow", [128, T], F32, R_B + 8192)
    tts2 = at("tts2", [128, T], F32, R_B + 16384)

    stb = nc.alloc_psum_tensor("stb", [128, 4 * QB], F32)
    ow = [nc.alloc_psum_tensor(f"ow_{i}", [128, QB], F32) for i in range(2)]
    dbank = nc.alloc_psum_tensor("dbank", [64, QB], F32)
    scr = nc.alloc_psum_tensor("scr", [128, 512], F32)

    tape = Tape()
    SEMS = {}

    def t_mm(out_ap, lhs_ap, rhs_ap, reads, writes, start=True, stop=True, tp=None):
        return tape.op("tensor",
                       lambda e: e.matmul(out_ap, lhs_ap, rhs_ap, start=start, stop=stop,
                                          tile_position=tp),
                       reads=reads, writes=writes)

    def t_act(out_ap, in_ap, func, reads, writes, bias=0.0, scale=1.0):
        return tape.op("scalar",
                       lambda e: e.activation(out_ap, in_ap, func, bias=bias, scale=scale),
                       reads=reads, writes=writes)

    def t_dve(fn, reads, writes):
        return tape.op("vector", fn, reads=reads, writes=writes)

    def tap(nm, in_ap, reads):
        if debug:
            tape.op("gpsimd", lambda e: e.dma_start(out=dbgout[nm], in_=in_ap),
                    reads=reads, writes=(), sem="dmg", inc=16)

    def t_dma(queue, out_ap, in_ap, semk, reads, writes):
        return tape.op(queue, lambda e: e.dma_start(out=out_ap, in_=in_ap),
                       reads=reads, writes=writes, sem=semk, inc=16)

    t_dma("sync", cwork[:, 0:8], c_row, "dml", (), ("c_in",))
    t_dma("sync", ropeCs[:], ropeC, "dml", (), ("ropeCs",))
    t_dma("sync", ropeSs[:], ropeS, "dml", (), ("ropeSs",))
    t_dma("sync", alts[:], alt, "dml", (), ("alts",))
    t_dma("sync", wqn2s[:], wqn2, "dml", (), ("wqn2s",))
    t_dma("sync", wkn2s[:], wkn2, "dml", (), ("wkn2s",))
    t_dma("sync", b1cs[:], b1c, "dml", (), ("b1cs",))
    t_dma("sync", b2cs[:], b2c, "dml", (), ("b2cs",))
    t_dma("sync", gammacs[:], gammar, "dml", (), ("gammacs",))
    t_dma("sync", betacs[:], betac, "dml", (), ("betacs",))
    t_dma("sync", idn[:], iden, "dml", (), ("idn",))
    t_dma("sync", iotas[:], iotaM, "dml", (), ("iotas",))
    tape.batch_end(["c_in", "ropeCs", "ropeSs", "alts", "wqn2s", "wkn2s", "b1cs",
                    "b2cs", "gammacs", "betacs", "idn", "iotas"])
    tape.dma_drain("sync", "dml", SEMS)
    for i in range(8):
        t_dma("sync", x32[i][:], xT[128 * i:128 * (i + 1), :], "dmx", (), (f"x32:{i}",))
    tape.batch_end([f"x32:{i}" for i in range(8)])
    tape.dma_drain("sync", "dmx", SEMS)

    for i_, v_ in enumerate(CONSTS):
        t_dve(lambda e, i=i_, v=v_: e.memset(cpool[:, i:i + 1], v), ("cpool",), ("cpool",))
        nc.const_aps.register(F32, v_, cpool[:, i_:i_ + 1])
    t_dve(lambda e: e.memset(onesel[:, 0:32], 0.0), (), ("onesel",))
    t_dve(lambda e: e.memset(onesel[:, 32:33], 1.0), ("onesel",), ("onesel",))
    t_dve(lambda e: e.memset(onesel[:, 33:64], 0.0), ("onesel",), ("onesel",))
    t_dve(lambda e: e.memset(zer128[:], 0.0), (), ("zer128",))
    t_dve(lambda e: e.memset(hselF[0:64, 0:1], 1.0), (), ("hselF",))
    t_dve(lambda e: e.memset(hselF[64:128, 0:1], 0.0), ("hselF",), ("hselF",))
    t_dve(lambda e: e.memset(hselF[0:64, 1:2], 0.0), ("hselF",), ("hselF",))
    t_dve(lambda e: e.memset(hselF[64:128, 1:2], 1.0), ("hselF",), ("hselF",))
    t_dve(lambda e: e.memset(ones_row[:], 1.0), (), ("ones_row",))
    t_dve(lambda e: e.memset(onescF[:], 1.0), (), ("onescF",))

    t_act(cwork[:, 8:16], cwork[:, 0:8], AF.Exp, ("c_in", "cpool"), ("c_exp",), scale=-1.0)
    t_dve(lambda e: e.tensor_scalar_add(cwork[:, 8:16], cwork[:, 8:16], 1.0),
          ("c_exp",), ("c_exp",))
    t_dve(lambda e: e.reciprocal(cwork[:, 16:24], cwork[:, 8:16]), ("c_exp",), ("c_rec",))
    t_dve(lambda e: e.tensor_mul(csrow[:], cwork[:, 0:8], cwork[:, 16:24]),
          ("c_in", "c_rec"), ("csrow",))
    t_dve(lambda e: e.tensor_copy(out=csrowb[:], in_=csrow[:]), ("csrow",), ("csrowb",))

    first = [True]
    for ic in range(8):
        for half in range(2):
            sl = slice(3072 * half, 3072 * (half + 1))
            t_dma("sync", wstg[:, 0:3072], wmodT[128 * ic:128 * (ic + 1), sl],
                  "dml", ("wstg_r",), ("wstg",))
            t_dve(lambda e: e.tensor_copy(out=wstg2[:, 0:3072], in_=wstg[:, 0:3072]),
                  ("wstg",), ("wstg2", "wstg_r"))
            for j in range(24):
                oc = 24 * half + j
                t_mm(scr[:, oc:oc + 1], wstg2[:, 128 * j:128 * (j + 1)],
                     csrowb[:, ic:ic + 1],
                     reads=("wstg2", "csrowb"), writes=("scr",),
                     start=first[0], stop=(ic == 7 and half == 1 and j == 23))
                first[0] = False
    t_dve(lambda e: e.tensor_copy(out=mod_sb[:], in_=scr[:, 0:48]), ("scr",), ("mod_sb",))
    tape.barrier(["wstg", "wstg2"], ["wq32"] + [f"kst:{i}" for i in range(2)])
    tap("t_mod", mod_sb[:], ("mod_sb",))

    def g_clamp(src_lo, dst_lo):
        t_dve(lambda e: e.tensor_scalar(cwork[:, 0:8], mod_sb[:, src_lo:src_lo + 8],
                                        0.0, -1.0, ALU.is_ge, ALU.bypass),
              ("mod_sb",), ("cw_ge",))
        t_dve(lambda e: e.tensor_scalar(cwork[:, 8:16], cwork[:, 0:8], 2.0, -1.0,
                                        ALU.mult, ALU.add), ("cw_ge",), ("cw_sg",))
        t_dve(lambda e: e.scalar_tensor_tensor(cwork[:, 16:24],
                                               mod_sb[:, src_lo:src_lo + 8], -1.0,
                                               mod_sb[:, src_lo:src_lo + 8],
                                               ALU.mult, ALU.max),
              ("mod_sb",), ("cw_ab",))
        t_dve(lambda e: e.tensor_scalar_max(cwork[:, 16:24], cwork[:, 16:24], 0.25),
              ("cw_ab",), ("cw_ab",))
        t_dve(lambda e, d=dst_lo: e.tensor_mul(modg[:, d:d + 8], cwork[:, 8:16],
                                               cwork[:, 16:24]),
              ("cw_sg", "cw_ab"), ("modg",))
    g_clamp(16, 0)
    g_clamp(40, 8)

    t_dve(lambda e: e.tensor_scalar_add(modca[:, 0:8], mod_sb[:, 8:16], 1.0),
          ("mod_sb",), ("modca",))
    t_dve(lambda e: e.tensor_scalar_add(modca[:, 8:16], mod_sb[:, 32:40], 1.0),
          ("mod_sb", "modca"), ("modca",))
    t_dve(lambda e: e.tensor_mul(gbb2[:], modg[:, 8:16], b2cs[:]), ("modg", "b2cs"), ("gbb2",))

    def emit_adaln_ln(src_tag, src, dst_tag, dst, scale_vec, shift_vec, svtag, tag,
                      src_bf16=False):
        ones_col = onesel[:, 32:33] if src_bf16 else onescF[:]
        ones_tag = "onesel" if src_bf16 else "onescF"
        for qb in range(NQB):
            sl = slice(QB * qb, QB * (qb + 1))
            for ci in range(8):
                t_mm(scr[0:1, 0:QB], ones_col, src[ci][:, sl],
                     reads=(f"{src_tag}:{ci}", ones_tag), writes=("scr",),
                     start=(ci == 0), stop=(ci == 7))
            for ci in range(8):
                t_dve(lambda e, c=ci, s=sl: e.tensor_mul(tq[:], src[c][:, s], src[c][:, s]),
                      reads=(f"{src_tag}:{ci}", "tq"), writes=("tq",))
                t_mm(ow[0][0:1, 0:QB], onescF[:], tq[:],
                     reads=("tq", "onescF"), writes=("ow0",),
                     start=(ci == 0), stop=(ci == 7))
            t_dve(lambda e: e.tensor_scalar_mul(rows[0:1, 1024:1536], scr[0:1, 0:QB],
                                                1.0 / DM),
                  reads=("scr", "rows0"), writes=(f"{tag}mean", "rows0"))
            t_dve(lambda e: e.tensor_scalar_mul(rows[0:1, 1536:2048], ow[0][0:1, 0:QB],
                                                1.0 / DM),
                  reads=("ow0", "rows0"), writes=(f"{tag}e2", "rows0"))
            t_dve(lambda e: e.tensor_mul(scr[0:1, 0:QB], rows[0:1, 1024:1536],
                                         rows[0:1, 1024:1536]),
                  reads=(f"{tag}mean", "scr"), writes=("scr",))
            t_dve(lambda e: e.tensor_sub(rows[0:1, 1536:2048], rows[0:1, 1536:2048],
                                         scr[0:1, 0:QB]),
                  reads=(f"{tag}e2", "scr"), writes=(f"{tag}var", "rows0"))
            t_act(scr[0:1, 0:QB], rows[0:1, 1536:2048], AF.Ln,
                  (f"{tag}var",), ("scr",), bias=1e-5, scale=1.0)
            t_act(rows[0:1, 0:QB], scr[0:1, 0:QB], AF.Exp,
                  ("scr", "rows0"), (f"{tag}rs", "rows0"), bias=0.0, scale=-0.5)
            t_dve(lambda e: e.tensor_mul(rows[0:1, 512:1024], rows[0:1, 0:QB],
                                         rows[0:1, 1024:1536]),
                  reads=(f"{tag}mean", f"{tag}rs", "rows0"), writes=(f"{tag}mrs", "rows0"))
            t_mm(scr[:, 0:QB], ones_row[:], rows[0:1, 0:QB],
                 reads=(f"{tag}rs", "ones_row"), writes=("scr",), start=True, stop=True)
            t_mm(ow[0][:, 0:QB], ones_row[:], rows[0:1, 512:1024],
                 reads=(f"{tag}mrs", "ones_row"), writes=("ow0",), start=True, stop=True)
            for ci in range(8):
                t_dve(lambda e, c=ci, s=sl: e.tensor_mul(tq[:], src[c][:, s], scr[:, 0:QB]),
                      reads=(f"{src_tag}:{ci}", "scr", "tq"), writes=("tq",))
                t_dve(lambda e: e.tensor_sub(tq[:], tq[:], ow[0][:, 0:QB]),
                      reads=("tq", "ow0"), writes=("tq",))
                t_dve(lambda e, c=ci, s=sl: e.tensor_scalar(
                        dst[c][:, s], tq[:], scale_vec(c), shift_vec(c),
                        ALU.mult, ALU.add),
                      reads=("tq", svtag), writes=(f"{dst_tag}:{ci}",))

    tape.last_write["svt_ln1"] = max(tape.last_write.get("modca", 0),
                                     tape.last_write.get("mod_sb", 0))
    emit_adaln_ln("x32", x32, "ha", ha, lambda c: modca[:, c:c + 1],
                  lambda c: mod_sb[:, 0 + c:1 + c], "svt_ln1", "ln1")

    tape.barrier([f"x32:{i}" for i in range(8)],
                 ["q2sb", "w2r", "qk_ln:0", "qk_ln:1", "qk_rf:0", "qk_rf:1", "tq2p",
                  "q2sbB", "w2rB", "tq2B", "tqB"])
    tap("t_ha0", ha[0][:], ("ha:0",))

    def load_w_bf16(wT_, dtag):
        for ci in range(8):
            t_dma("sync", wq32[:], wT_[128 * ci:128 * (ci + 1), :], "dml",
                  ("wq32_r",), ("wq32",))
            t_dve(lambda e, c=ci: e.tensor_copy(out=wbf[c][:], in_=wq32[:]),
                  ("wq32",), (f"wbf:{ci}", "wq32_r"))

    def emit_qk(wT_, wn2s_ap, wn2tag, out_write, lnbias, kn_tag=None):
        for oc in range(8):
            for qb in range(NQB):
                par2 = qb % 2
                sl = slice(QB * qb, QB * (qb + 1))
                bank = stb[:, QB * par2:QB * par2 + QB]
                sbank = scr[:, 0:QB] if par2 == 0 else ow[1][:, 0:QB]
                sbank2 = scr[0:2, 0:QB] if par2 == 0 else ow[1][0:2, 0:QB]
                sbtag = "scr" if par2 == 0 else "ow1b"
                q2 = q2sb if par2 == 0 else q2sbB
                q2t = "q2sb" if par2 == 0 else "q2sbB"
                wr = w2r if par2 == 0 else w2rB
                wrt = "w2r" if par2 == 0 else "w2rB"
                t2 = tq2 if par2 == 0 else tq2B
                t2t = "tq2p" if par2 == 0 else "tq2B"
                t1 = tq if par2 == 0 else tqB
                t1t = "tq" if par2 == 0 else "tqB"
                for ic in range(8):
                    t_mm(bank, wbf[ic][:, 128 * oc:128 * (oc + 1)], ha[ic][:, sl],
                         reads=(f"wbf:{ic}", f"ha:{ic}"), writes=(f"stb:{par2}",),
                         start=(ic == 0), stop=(ic == 7))
                t_act(q2[:], bank, AF.Square, (f"stb:{par2}",), (q2t,))
                t_mm(sbank2, hselF[:], q2[:],
                     reads=(q2t, "hselF"), writes=(sbtag,), start=True, stop=True)
                t_act(t1[0:2, 0:QB], sbank2, AF.Ln, (sbtag,), (f"qk_ln:{par2}", t1t),
                      bias=1e-6, scale=1.0 / HD)
                t_act(t2[0:2, 0:QB], t1[0:2, 0:QB], AF.Exp, (f"qk_ln:{par2}",),
                      (f"qk_rf:{par2}", t2t), bias=lnbias, scale=-0.5)
                t_mm(sbank, wn2s_ap, t2[0:2, 0:QB],
                     reads=(f"qk_rf:{par2}", wn2tag), writes=(sbtag,),
                     start=True, stop=True)
                t_dve(lambda e, w_=wr, sb=sbank: e.tensor_copy(out=w_[:], in_=sb),
                      (sbtag,), (wrt,))
                stg = qstg[par2]
                stg2 = qstg[2 + par2]
                t_dve(lambda e, b=bank, s=stg, w_=wr: e.tensor_mul(s[:], b, w_[:]),
                      (f"stb:{par2}", wrt), (f"qkstg:{par2}",))
                t_dve(lambda e, s=stg, d=stg2: e.stream_shuffle(d[:], s[:], PAIRSWAP),
                      (f"qkstg:{par2}",), (f"qksh:{par2}",))
                t_dve(lambda e, s=stg, sli=sl, t_=t1: e.tensor_mul(t_[:], s[:],
                                                                  ropeCs[:, sli]),
                      (f"qkstg:{par2}", "ropeCs"), (t1t,))
                t_dve(lambda e, d=stg2, sli=sl: e.scalar_tensor_tensor(
                        d[:], d[:], alts[:, 0:1], ropeSs[:, sli], ALU.mult, ALU.mult),
                      (f"qksh:{par2}", "alts", "ropeSs"), (f"qkt2:{par2}",))
                out_write(oc, qb, stg2, t1, t1t)
            if kn_tag is not None:
                kn_tag(oc)

    load_w_bf16(wqT, "wq")

    def q_write(oc, qb, stg2, t1, t1t):
        sl = slice(QB * qb, QB * (qb + 1))
        t_dve(lambda e, d=stg2, o=qsb[oc], s=sl, t_=t1: e.tensor_add(o[:, s], t_[:], d[:]),
              (t1t, f"qkt2:{qb % 2}"), (f"qsb:{oc}",))

    emit_qk(wqT, wqn2s[:], "wqn2s", q_write, math.log(0.125))

    load_w_bf16(wkT, "wk")

    def k_write(oc, qb, stg2, t1, t1t):
        sl = slice(QB * qb, QB * (qb + 1))
        kst = kstg[oc % 2]
        t_dve(lambda e, d=stg2, o=kst, s=sl, t_=t1: e.tensor_add(o[:, s], t_[:], d[:]),
              (t1t, f"qkt2:{qb % 2}"), (f"kst:{oc % 2}",))
        stg = qstg[qb % 2]
        t_dve(lambda e, s=stg, t_=t1: e.tensor_mul(t_[:], s[:], s[:]),
              (f"qkstg:{qb % 2}",), (t1t,))
        t_mm(scr[0:2, 0:QB], hselF[:], t1[:], reads=(t1t, "hselF"), writes=("scr",),
             start=True, stop=True)
        t_dve(lambda e, q=qb: e.tensor_reduce(qn2m[0:2, 8 + q:9 + q], scr[0:2, 0:QB],
                                              AX, ALU.max),
              ("scr",), ("qn2m_s",))

    def k_fin(oc):
        t_dve(lambda e, o=oc: e.tensor_reduce(kn2m[0:2, o:o + 1], qn2m[0:2, 8:12],
                                              AX, ALU.max),
              ("qn2m_s",), ("kn2m",))
        t_dma("sync", kbounce[128 * oc:128 * (oc + 1), :], kstg[oc % 2][:],
              "dms" if oc % 2 == 0 else "dms1",
              (f"kst:{oc % 2}",), (f"kb:{oc}", f"kst_r:{oc % 2}"))

    emit_qk(wkT, wkn2s[:], "wkn2s", k_write, 0.0, kn_tag=k_fin)

    load_w_bf16(wvT, "wv")
    for tc in range(16):
        vst = vstg[tc % 2]
        for half in range(2):
            bank = stb[:, QB * half:QB * half + QB]
            for ic in range(8):
                t_mm(bank, ha[ic][:, 128 * tc:128 * (tc + 1)],
                     wbf[ic][:, 512 * half:512 * (half + 1)],
                     reads=(f"wbf:{ic}", f"ha:{ic}"), writes=(f"stb:{half}",),
                     start=(ic == 0), stop=(ic == 7))
            t_dve(lambda e, h=half, v=vst, b=bank: e.tensor_copy(
                    out=v[:, 512 * h:512 * (h + 1)], in_=b),
                  (f"stb:{half}",), (f"vst:{tc % 2}",))
        t_dma("sync", vbounce[128 * tc:128 * (tc + 1), :], vst[:],
              "dms" if tc % 2 == 0 else "dms1",
              (f"vst:{tc % 2}",), (f"vb:{tc}",))

    for oc in range(8):
        for qb in range(NQB):
            sl = slice(QB * qb, QB * (qb + 1))
            t_dve(lambda e, o=oc, s=sl: e.tensor_mul(tq[:], qsb[o][:, s], qsb[o][:, s]),
                  (f"qsb:{oc}",), ("tq",))
            t_mm(scr[0:2, 0:QB], hselF[:], tq[:], reads=("tq", "hselF"), writes=("scr",),
                 start=True, stop=True)
            t_dve(lambda e, q=qb: e.tensor_reduce(qn2m[0:2, 8 + q:9 + q], scr[0:2, 0:QB],
                                                  AX, ALU.max),
                  ("scr",), ("qn2m_s",))
        t_dve(lambda e, o=oc: e.tensor_reduce(qn2m[0:2, o:o + 1], qn2m[0:2, 8:12],
                                              AX, ALU.max),
              ("qn2m_s",), ("qn2m",))

    t_dma("gpsimd", knb, kn2m[:], "dmg", ("kn2m",), ())
    kn_dma = len(tape.ops) - 1
    tape.op("gpsimd",
            lambda e: e.collective_compute("AllReduce", ALU.max, replica_groups=groups,
                                           ins=[knb], outs=[kna]),
            reads=(), writes=("d_kna",), sem="cc", extra_deps=(kn_dma,))
    t_dma("gpsimd", kn2g[:], kna, "dmg", ("d_kna",), ("kn2g",))
    tape.op("gpsimd",
            lambda e: e.collective_compute("AllGather", ALU.bypass, replica_groups=groups,
                                           ins=[kbounce], outs=[kag]),
            reads=tuple(f"kb:{i}" for i in range(8)), writes=("d_kag",), sem="cc")
    tape.op("gpsimd",
            lambda e: e.collective_compute("AllGather", ALU.bypass, replica_groups=groups,
                                           ins=[vbounce], outs=[vag]),
            reads=tuple(f"vb:{i}" for i in range(16)), writes=("d_vag",), sem="cc")

    t_dve(lambda e: e.tensor_mul(mm2s[:], qn2m[0:2, 0:8], kn2g[:]),
          ("qn2m", "kn2g"), ("mm2s",))
    t_act(mm2s[:], mm2s[:], AF.Ln, ("mm2s",), ("mm2s",))
    t_act(mm2s[:], mm2s[:], AF.Exp, ("mm2s",), ("mm2s",), scale=0.5)
    tape.op("tensor", lambda e: e.transpose(scr[0:8, 32:34], mm2s[:], idn[0:2, 0:2]),
            reads=("mm2s", "idn"), writes=("scr",))
    t_dve(lambda e: e.tensor_reduce(cwork[0:8, 0:1], scr[0:8, 32:34], AX, ALU.max),
          ("scr",), ("cw_m",))
    tape.op("tensor", lambda e: e.transpose(scr[0:1, 64:72], cwork[0:8, 0:1], idn[0:8, 0:8]),
            reads=("cw_m", "idn"), writes=("scr",))
    t_dve(lambda e: e.tensor_copy(out=cwork[0:1, 8:16], in_=scr[0:1, 64:72]),
          ("scr",), ("cw_m2",))
    t_mm(scr[:, 0:8], ones_row[:], cwork[0:1, 8:16], reads=("cw_m2", "ones_row"),
         writes=("scr",), start=True, stop=True)
    t_dve(lambda e: e.tensor_scalar_mul(mneg[:], scr[:, 0:8], -1.0), ("scr",), ("mneg",))
    tap("t_q0", qsb[0][:], ("qsb:0",))
    tap("t_kb", kbounce, tuple(f"kb:{i}" for i in range(8)))
    tap("t_vb", vbounce, tuple(f"vb:{i}" for i in range(16)))
    tap("t_kag", kag, ("d_kag",))
    tap("t_vag", vag, ("d_vag",))
    tap("t_mneg", mneg[:], ("mneg",))



    def emit_tern(wT_, rowsn, cols, wqd, tag, sccol):
        nch = rowsn // 128
        for ci in range(nch):
            t_dma("gpsimd", wstg[:, 0:cols], wT_[128 * ci:128 * (ci + 1), :], "dmg",
                  ("wstg",), ("wstg",))
            t_dve(lambda e: e.scalar_tensor_tensor(wstg2[:, 0:cols], wstg[:, 0:cols],
                                                   -1.0, wstg[:, 0:cols],
                                                   ALU.mult, ALU.max),
                  ("wstg",), ("wstg2",))
            t_dve(lambda e: e.tensor_reduce(cwork[:, 24:25], wstg2[:, 0:cols], AX, ALU.add),
                  ("wstg2",), ("cw_rs",))
            t_mm(scr[0:1, 256 + sccol:257 + sccol], onescF[:], cwork[:, 24:25],
                 reads=("cw_rs", "onescF"), writes=("scr",),
                 start=(ci == 0), stop=(ci == nch - 1))
        t_dve(lambda e: e.tensor_scalar_mul(cwork[0:1, 25:26],
                                            scr[0:1, 256 + sccol:257 + sccol],
                                            1.0 / (rowsn * cols)),
              ("scr",), ("cw_mean",))
        t_dve(lambda e: e.tensor_scalar_max(cwork[0:1, 25:26], cwork[0:1, 25:26], 1e-5),
              ("cw_mean",), ("cw_mean",))
        t_mm(scr[:, 288:289], ones_row[:], cwork[0:1, 25:26],
             reads=("cw_mean", "ones_row"), writes=("scr",), start=True, stop=True)
        t_dve(lambda e, s=sccol: e.tensor_copy(out=swsc[:, s + 1:s + 2], in_=scr[:, 288:289]),
              ("scr",), (f"{tag}meanb",))
        t_dve(lambda e, s=sccol: e.reciprocal(swsc[:, s:s + 1], swsc[:, s + 1:s + 2]),
              (f"{tag}meanb",), (f"{tag}swrec",))
        for ci in range(nch):
            t_dma("gpsimd", wstg[:, 0:cols], wT_[128 * ci:128 * (ci + 1), :], "dmg",
                  ("wstg",), ("wstg",))
            t_dve(lambda e, s=sccol: e.tensor_scalar_mul(wstg[:, 0:cols], wstg[:, 0:cols],
                                                         swsc[:, s:s + 1]),
                  ("wstg", f"{tag}swrec"), ("wstg",))
            t_dve(lambda e: e.tensor_scalar_add(wstg[:, 0:cols], wstg[:, 0:cols], RND_C),
                  ("wstg",), ("wstg",))
            t_dve(lambda e: e.tensor_scalar(wstg[:, 0:cols], wstg[:, 0:cols], -RND_C, None,
                                            ALU.add),
                  ("wstg",), ("wstg",))
            t_dve(lambda e: e.tensor_scalar(wstg2[:, 0:cols], wstg[:, 0:cols], 1.0, -1.0,
                                            ALU.min, ALU.max),
                  ("wstg",), ("wstg2",))
            t_dma("gpsimd", wqd[128 * ci:128 * (ci + 1), :], wstg2[:, 0:cols], "dmg",
                  ("wstg2",), (f"{tag}qd:{ci}", "wstg2"))

    emit_tern(w1T, DM, FFH, w1qd, "w1", 0)
    emit_tern(w2T, FFH, DM, w2qd, "w2", 2)
    tape.barrier([f"x32:{i}" for i in range(8)] + ["q2sb", "w2r", "q2sbB", "w2rB",
                  "tq2B", "tqB", "qk_ln:0", "qk_ln:1", "qk_rf:0", "qk_rf:1"],
                 [f"osb:{s}" for s in range(32)])
    tape.barrier(["ropeCs", "ropeSs"] + [f"qkstg:{i}" for i in range(2)]
                 + [f"qksh:{i}" for i in range(2)] + [f"qkt2:{i}" for i in range(2)],
                 [f"ksup:{i}" for i in range(2)] + [f"vsup:{i}" for i in range(2)])
    tape.barrier([f"ha:{i}" for i in range(8)],
                 [f"pt:{i}" for i in range(2)])

    def load_super(s):
        par = s % 2
        semk = "dmk" if par == 0 else "dmk1"
        r, sub = s // 4, s % 4
        for h in range(NH):
            t_dma("sync",
                  ksup[par][(h % 2) * 64:(h % 2) * 64 + 64,
                            (h // 2) * SUP:(h // 2 + 1) * SUP],
                  kag[1024 * r + 64 * h:1024 * r + 64 * (h + 1),
                      SUP * sub:SUP * (sub + 1)],
                  semk, ("d_kag",), (f"ksup:{par}",) if h == 0 else ())
        tape.op("sync", lambda e, p=par, s_=s: e.dma_start(
                    out=vsup[p][:, 0:CPS * DM].rearrange("p (c d) -> p c d", c=CPS),
                    in_=vag[SUP * s_:SUP * (s_ + 1), :].rearrange(
                        "(c p) d -> p c d", p=128)),
                reads=("d_vag",), writes=(f"vsup:{par}",), sem=semk, inc=16)
        i = len(tape.ops) - 1
        tape.last_write[f"ksup:{par}"] = i
        tape.readers[f"ksup:{par}"] = []

    t_dve(lambda e: e.memset(ptb[0][:, 0:512], 0.0), (), ("pt:0",))
    t_mm(dbank[:], zer128[:, 0:64], ptb[0][:, 0:512], reads=("zer128", "pt:0"),
         writes=("dbank",), start=True, stop=False)

    load_super(0)
    load_super(1)

    def pv_emit(qb, p, head_par, owb, owtag, sup_par, conly=None):
        h = 2 * p + head_par
        row = 32 * head_par + 8 * qb + p
        pbuf = ptb[p % 2]
        for c in ([conly] if conly is not None else range(CPS)):
            u = 2 * c + head_par
            pslot = pbuf[:, 512 * u:512 * (u + 1)]
            vsl = vsup[sup_par][:, DM * c + 64 * h:DM * c + 64 * (h + 1)]
            t_mm(owb[64 * head_par:64 * head_par + 64, :], vsl, pslot,
                 reads=(f"pt:{p % 2}:{u}", f"vsup:{sup_par}"), writes=(owtag,),
                 start=(c == 0), stop=(c == CPS - 1), tp=(0, 64 * head_par))
            t_mm(dbank[32 * head_par:32 * head_par + 32, :],
                 onesel[:, 32 - (row % 32):64 - (row % 32)], pslot,
                 reads=(f"pt:{p % 2}:{u}", "onesel"), writes=("dbank",),
                 start=False, stop=False, tp=(0, 32 * head_par))

    def osb_evac(sup, p, qb):
        slot = p * 4 + qb
        if sup == 0:
            t_dve(lambda e, s=slot, o=ow[p % 2]: e.tensor_copy(
                    out=osb[:, 512 * s:512 * (s + 1)], in_=o[:]),
                  (f"ow:{p % 2}",), (f"osb:{slot}",))
        else:
            t_dve(lambda e, s=slot, o=ow[p % 2]: e.tensor_add(
                    osb[:, 512 * s:512 * (s + 1)],
                    osb[:, 512 * s:512 * (s + 1)], o[:]),
                  (f"ow:{p % 2}", f"osb:{slot}"), (f"osb:{slot}",))

    for sup in range(NSUP):
        par = sup % 2
        for qb in range(NQB):
            qsl = slice(QB * qb, QB * (qb + 1))
            for p in range(8):
                for c in range(CPS):
                    cs = c % 2
                    bA = stb[:, QB * (2 * cs):QB * (2 * cs) + QB]
                    bB = stb[:, QB * (2 * cs + 1):QB * (2 * cs + 1) + QB]
                    kA = ksup[par][0:64, p * SUP + 128 * c:p * SUP + 128 * (c + 1)]
                    kB = ksup[par][64:128, p * SUP + 128 * c:p * SUP + 128 * (c + 1)]
                    t_mm(bA, kA, qsb[p][0:64, qsl],
                         reads=(f"ksup:{par}", f"qsb:{p}"), writes=(f"stb:{2 * cs}",),
                         start=True, stop=True, tp=(0, 0))
                    t_mm(bB, kB, qsb[p][64:128, qsl],
                         reads=(f"ksup:{par}", f"qsb:{p}"), writes=(f"stb:{2 * cs + 1}",),
                         start=True, stop=True, tp=(64, 0))
                    t_act(ptb[p % 2][:, 1024 * c:1024 * (c + 1)],
                          stb[:, QB * 2 * cs:QB * (2 * cs + 2)], AF.Exp,
                          (f"stb:{2 * cs}", f"stb:{2 * cs + 1}", "mneg"),
                          (f"pt:{p % 2}:{2 * c}", f"pt:{p % 2}:{2 * c + 1}"),
                          bias=mneg[:, p:p + 1], scale=1.0)
                    if p > 0:
                        pv_emit(qb, p - 1, 1, ow[(p - 1) % 2], f"ow:{(p - 1) % 2}",
                                par, conly=c)
                    if c > 0:
                        pv_emit(qb, p, 0, ow[p % 2], f"ow:{p % 2}", par, conly=c - 1)
                if p > 0:
                    osb_evac(sup, p - 1, qb)
                pv_emit(qb, p, 0, ow[p % 2], f"ow:{p % 2}", par, conly=CPS - 1)
            pv_emit(qb, 7, 1, ow[1], "ow:1", par)
            osb_evac(sup, 7, qb)
        if sup + 2 < NSUP:
            load_super(sup + 2)

    t_mm(dbank[:], zer128[:, 0:64], ptb[0][:, 0:512], reads=("zer128", "pt:0"),
         writes=("dbank",), start=False, stop=True)

    t_dve(lambda e: e.reciprocal(drec[:], dbank[:]), ("dbank",), ("drec",))
    tap("t_drec", drec[:], ("drec",))
    tap("t_osb", osb[:], tuple(f"osb:{s_}" for s_ in range(32)))
    tape.barrier([f"ksup:{i}" for i in range(2)] + [f"vsup:{i}" for i in range(2)],
                 [f"woTs:{i}" for i in range(8)] + ["drec", "dbcs", "xstg:0", "xstg:1"])
    tape.barrier([f"pt:{i}" for i in range(2)], [f"onrm:{s}" for s in range(32)])
    for ci in range(8):
        t_dma("sync", wq32[:], woT[128 * ci:128 * (ci + 1), :], "dml",
              ("wq32_r",), ("wq32",))
        t_dve(lambda e, c=ci: e.tensor_copy(out=woTs[c][:], in_=wq32[:]),
              ("wq32",), (f"woTs:{ci}", "wq32_r"))
    for p in range(8):
        for qb in range(NQB):
            slot = p * 4 + qb
            rA = 8 * qb + p
            rB = 32 + 8 * qb + p
            t_dve(lambda e, r=rA: e.tensor_scalar(selD[:, 0:64], iotas[:, 0:64],
                                                  float(r), None, ALU.is_ge),
                  ("iotas", "selD"), ("selD",))
            t_dve(lambda e, r=rA: e.tensor_scalar(selDb[:, 0:64], iotas[:, 0:64],
                                                  float(r + 1), None, ALU.is_ge),
                  ("iotas", "selDb"), ("selDb",))
            t_dve(lambda e: e.tensor_sub(selD[:, 0:64], selD[:, 0:64], selDb[:, 0:64]),
                  ("selD", "selDb"), ("selD",))
            t_dve(lambda e, r=rB: e.tensor_scalar(selD[:, 64:128], iotas[:, 64:128],
                                                  float(r), None, ALU.is_ge),
                  ("iotas", "selD"), ("selD",))
            t_dve(lambda e, r=rB: e.tensor_scalar(selDb[:, 64:128], iotas[:, 64:128],
                                                  float(r + 1), None, ALU.is_ge),
                  ("iotas", "selDb"), ("selDb",))
            t_dve(lambda e: e.tensor_sub(selD[:, 64:128], selD[:, 64:128],
                                         selDb[:, 64:128]),
                  ("selD", "selDb"), ("selD",))
            t_mm(scr[:, 0:QB], selD[:], drec[:],
                 reads=("drec", "selD"), writes=("scr",), start=True, stop=True)
            t_dve(lambda e: e.tensor_copy(out=dbcs[:], in_=scr[:, 0:QB]), ("scr",), ("dbcs",))
            t_dve(lambda e, s=slot: e.tensor_mul(onrm[:, 512 * s:512 * (s + 1)],
                                                 osb[:, 512 * s:512 * (s + 1)], dbcs[:]),
                  (f"osb:{slot}", "dbcs"), (f"onrm:{slot}",))
    tape.barrier([f"osb:{s}" for s in range(32)], [f"x1:{i}" for i in range(8)])
    for oc in range(8):
        for qb in range(NQB):
            bank = stb[:, QB * qb:QB * qb + QB]
            for p in range(8):
                slot = p * 4 + qb
                t_mm(bank, woTs[p][:, 128 * oc:128 * (oc + 1)],
                     onrm[:, 512 * slot:512 * (slot + 1)],
                     reads=(f"woTs:{p}", f"onrm:{slot}"), writes=(f"stb:{qb}",),
                     start=(p == 0), stop=(p == 7))
            xs = xstg[qb % 2]
            t_dma("sync", xs[:], xT[128 * oc:128 * (oc + 1), QB * qb:QB * (qb + 1)],
                  "dmx" if qb % 2 == 0 else "dmx1",
                  (f"xstg:{qb % 2}",), (f"xstg:{qb % 2}",))
            t_dve(lambda e, b=bank, x=xs, o=oc, q=qb: e.scalar_tensor_tensor(
                    x1sb[o][:, QB * q:QB * (q + 1)], b, modg[:, o:o + 1], x[:],
                    ALU.mult, ALU.add),
                  (f"stb:{qb}", f"xstg:{qb % 2}", "modg"),
                  (f"x1:{oc}", f"xstg:{qb % 2}"))

    tape.barrier(["wq32", "wstg", "wstg2"] + [f"wbf:{i}" for i in range(8)]
                 + [f"kst:{i}" for i in range(2)] + [f"vst:{i}" for i in range(2)],
                 [f"hb:{i}" for i in range(8)])
    tape.last_write["svt_ln2"] = max(tape.last_write.get("modca", 0),
                                     tape.last_write.get("mod_sb", 0))
    emit_adaln_ln("x1", x1sb, "hb", hb, lambda c: modca[:, 8 + c:9 + c],
                  lambda c: mod_sb[:, 24 + c:25 + c], "svt_ln2", "ln2")

    tape.barrier([f"qsb:{i}" for i in range(8)], [f"h2:{i}" for i in (4, 5, 6, 7)])
    tape.barrier([f"woTs:{i}" for i in range(8)] + ["drec", "dbcs", "xstg:0", "xstg:1"],
                 [f"h2:{i}" for i in (0, 1, 2, 3)])
    h2 = h2lo + h2hi
    tape.last_write["svt_fln"] = max(tape.last_write.get("gammacs", 0),
                                     tape.last_write.get("betacs", 0))
    emit_adaln_ln("hb", hb, "h2", h2, lambda c: gammacs[:, c:c + 1],
                  lambda c: betacs[:, c:c + 1], "svt_fln", "fln", src_bf16=True)

    for i in range(8):
        t_dma("sync", x1d[128 * i:128 * (i + 1), :], x1sb[i][:], "dms",
              (f"x1:{i}",), (f"x1d:{i}",))
    tape.batch_end([f"x1d:{i}" for i in range(8)])
    tape.dma_drain("sync", "dms", SEMS)
    tap("t_x1", x1d, tuple(f"x1d:{i}" for i in range(8)))
    tap("t_h20", h2[0][:], ("h2:0",))
    tape.barrier([f"onrm:{s}" for s in range(32)], ["q1mx", "q1sx", "q1rec"])
    tape.barrier([f"hb:{i}" for i in range(2)] + ["x1:0"],
                 [f"xq1w:{i}" for i in range(2)] + ["tq2p", "q1v"])

    def emit_quant(srcs, src_tags, dsts, dst_tags, tag, mrow_ap):
        t_dve(lambda e: e.scalar_tensor_tensor(mxf[:], srcs[0], -1.0, srcs[0],
                                               ALU.mult, ALU.max),
              reads=(src_tags[0], "mxf"), writes=(f"{tag}mx", "mxf"))
        for ci in range(1, len(srcs)):
            t_dve(lambda e, c=ci: e.scalar_tensor_tensor(mxf[:], srcs[c], 1.0, mxf[:],
                                                         ALU.mult, ALU.max),
                  reads=(src_tags[ci], f"{tag}mx"), writes=(f"{tag}mx", "mxf"))
            t_dve(lambda e, c=ci: e.scalar_tensor_tensor(mxf[:], srcs[c], -1.0, mxf[:],
                                                         ALU.mult, ALU.max),
                  reads=(src_tags[ci], f"{tag}mx"), writes=(f"{tag}mx", "mxf"))
        for b in range(FQB // 128):
            tape.op("tensor", lambda e, b_=b: e.transpose(scr[:, 0:128],
                                                          mxf[:, 128 * b_:128 * (b_ + 1)],
                                                          idn[:]),
                    reads=(f"{tag}mx", "idn", "scr"), writes=("scr",))
            t_dve(lambda e, b_=b: e.tensor_reduce(mxt[:, b_:b_ + 1], scr[:, 0:128],
                                                  AX, ALU.max),
                  ("scr", "mxt"), ("mxt",))
        t_dve(lambda e: e.tensor_scalar_max(mxt[:, 0:FQB // 128], mxt[:, 0:FQB // 128],
                                            1e-5), ("mxt",), ("mxt",))
        t_dve(lambda e: e.reciprocal(mxr[:, 0:FQB // 128], mxt[:, 0:FQB // 128]),
              ("mxt",), ("mxr",))
        t_dve(lambda e: e.tensor_scalar_mul(mxr[:, 0:FQB // 128], mxr[:, 0:FQB // 128],
                                            127.0), ("mxr",), ("mxr",))
        for b in range(FQB // 128):
            tape.op("tensor", lambda e, b_=b: e.transpose(scr[0:1, 128:256],
                                                          mxt[:, b_:b_ + 1], idn[:]),
                    reads=("mxt", "idn", "scr"), writes=("scr",))
            t_dve(lambda e, b_=b: e.tensor_copy(out=mrow_ap[:, 128 * b_:128 * (b_ + 1)],
                                                in_=scr[0:1, 128:256]),
                  ("scr", "rows0"), (f"{tag}mrow", "rows0"))
            tape.op("tensor", lambda e, b_=b: e.transpose(scr[0:1, 128:256],
                                                          mxr[:, b_:b_ + 1], idn[:]),
                    reads=("mxr", "idn", "scr"), writes=("scr",))
            t_dve(lambda e, b_=b: e.tensor_copy(out=tq[0:1, 128 * b_:128 * (b_ + 1)],
                                                in_=scr[0:1, 128:256]),
                  ("scr", "tq"), (f"{tag}sxrow", "tq"))
        t_mm(ow[0][:, 0:FQB], ones_row[:], tq[0:1, 0:FQB],
             reads=(f"{tag}sxrow", "ones_row"), writes=("ow0",), start=True, stop=True)
        for ci in range(len(srcs)):
            t_dve(lambda e, c=ci: e.tensor_mul(tf[:], srcs[c], ow[0][:, 0:FQB]),
                  reads=(src_tags[ci], "ow0", "tf"), writes=(f"{tag}v", "tf"))
            t_dve(lambda e: e.tensor_scalar_add(tf[:], tf[:], RND_C),
                  reads=(f"{tag}v",), writes=(f"{tag}v", "tf"))
            t_dve(lambda e, c=ci: e.tensor_scalar(dsts[c], tf[:], -RND_C, None, ALU.add),
                  reads=(f"{tag}v",), writes=(dst_tags[ci],))


    t_dve(lambda e: e.scalar_tensor_tensor(mxrow[:], h2[0][:], -1.0, h2[0][:],
                                           ALU.mult, ALU.max),
          reads=("h2:0",), writes=("q1mx",))
    for ci in range(1, 8):
        t_dve(lambda e, c=ci: e.scalar_tensor_tensor(mxrow[:], h2[c][:], 1.0, mxrow[:],
                                                     ALU.mult, ALU.max),
              reads=(f"h2:{ci}", "q1mx"), writes=("q1mx",))
        t_dve(lambda e, c=ci: e.scalar_tensor_tensor(mxrow[:], h2[c][:], -1.0, mxrow[:],
                                                     ALU.mult, ALU.max),
              reads=(f"h2:{ci}", "q1mx"), writes=("q1mx",))
    for b in range(16):
        tape.op("tensor", lambda e, b_=b: e.transpose(scr[:, 0:128],
                                                      mxrow[:, 128 * b_:128 * (b_ + 1)],
                                                      idn[:]),
                reads=("q1mx", "idn", "scr"), writes=("scr",))
        t_dve(lambda e, b_=b: e.tensor_reduce(mxt[:, b_:b_ + 1], scr[:, 0:128],
                                              AX, ALU.max),
              ("scr", "mxt"), ("mxt",))
    t_dve(lambda e: e.tensor_scalar_max(mxt[:], mxt[:], 1e-5), ("mxt",), ("mxt",))
    t_dve(lambda e: e.reciprocal(mxr[:], mxt[:]), ("mxt",), ("mxr",))
    t_dve(lambda e: e.tensor_scalar_mul(mxr[:], mxr[:], 127.0), ("mxr",), ("mxr",))
    for b in range(16):
        tape.op("tensor", lambda e, b_=b: e.transpose(scr[0:1, 128:256],
                                                      mxt[:, b_:b_ + 1], idn[:]),
                reads=("mxt", "idn", "scr"), writes=("scr",))
        t_dve(lambda e, b_=b: e.tensor_copy(out=rows[0:1, 128 * b_:128 * (b_ + 1)],
                                            in_=scr[0:1, 128:256]),
              ("scr", "rows0"), ("q1m", "rows0"))
        tape.op("tensor", lambda e, b_=b: e.transpose(scr[0:1, 128:256],
                                                      mxr[:, b_:b_ + 1], idn[:]),
                reads=("mxr", "idn", "scr"), writes=("scr",))
        t_dve(lambda e, b_=b: e.tensor_copy(out=mxrow[0:1, 128 * b_:128 * (b_ + 1)],
                                            in_=scr[0:1, 128:256]),
              ("scr", "q1mx"), ("q1sxrow", "q1mx"))
    for ci in range(8):
        w = xq1w[ci % 2]
        for qb in range(NQB):
            sl = slice(QB * qb, QB * (qb + 1))
            t_mm(ow[0][:, 0:QB], ones_row[:], mxrow[0:1, sl],
                 reads=("q1sxrow", "ones_row"), writes=("ow0",), start=True, stop=True)
            t_dve(lambda e, c=ci, s=sl: e.tensor_mul(tq2[:], h2[c][:, s], ow[0][:, 0:QB]),
                  (f"h2:{ci}", "ow0", "tq2p"), ("q1v", "tq2p"))
            t_dve(lambda e: e.tensor_scalar_add(tq2[:], tq2[:], RND_C),
                  ("q1v",), ("q1v", "tq2p"))
            t_dve(lambda e, w_=w, s=sl: e.tensor_scalar(w_[:, s], tq2[:], -RND_C, None,
                                                        ALU.add),
                  ("q1v",), (f"xq1w:{ci % 2}",))
        t_dma("sync", xq1d[128 * ci:128 * (ci + 1), :], w[:],
              "dms" if ci % 2 == 0 else "dms1",
              (f"xq1w:{ci % 2}",), (f"xq1d:{ci}", f"xq1w:{ci % 2}"))
    tape.barrier([f"xq1d:{i}" for i in range(8)], [])
    tape.batch_end([f"xq1d:{i}" for i in range(8)])
    tape.dma_drain("sync", "dms", SEMS)
    tape.dma_drain("sync", "dms1", SEMS)



    tape.barrier([f"x1d:{i}" for i in range(8)] + ["tq2p", "q1v"],
                 [f"w1q:{i}" for i in range(8)])
    tape.barrier([f"h2:{i}" for i in range(8)] + ["q1mx", "q1sxrow", "q1m",
                  "wstg", "wstg2"],
                 [f"w2q:{i}" for i in range(32)] + [f"hm:{h}" for h in range(32)]
                 + [f"xq2:{c}" for c in range(32)] + [f"xq1s:{i}" for i in range(8)]
                 + ["xstg2:0", "xstg2:1", "pscs", "psc2s", "tf", "tf2", "sxf", "mxf"])
    for ci in range(8):
        t_dma("sync", w1q[ci][:], w1qd[128 * ci:128 * (ci + 1), :], "dml",
              (f"w1qd:{ci}", f"w1q:{ci}"), (f"w1q:{ci}",))
    tape.batch_end([f"w1q:{i}" for i in range(8)])
    tape.dma_drain("sync", "dml", SEMS)
    w2q = w2ql + w2qh
    for ci in range(32):
        t_dma("sync", w2q[ci][:], w2qd[128 * ci:128 * (ci + 1), :], "dml",
              (f"w2qd:{ci}", f"w2q:{ci}"), (f"w2q:{ci}",))
    tape.batch_end([f"w2q:{i}" for i in range(32)])
    tape.dma_drain("sync", "dml", SEMS)

    tap("t_xq1", xq1d, tuple(f"xq1d:{i}" for i in range(8)))
    tap("t_w1q", w1qd, tuple(f"w1qd:{i}" for i in range(8)))
    tap("t_swsc", swsc[:], ("w1swrec", "w2swrec", "w1meanb", "w2meanb"))
    t_dve(lambda e: e.tensor_scalar_mul(rows[0:1, :], rows[0:1, :], 1.0 / 127.0),
          ("q1m",), ("q1m", "rows0"))
    t_dve(lambda e: e.tensor_scalar_mul(rows[0:1, :], rows[0:1, :], swsc[0:1, 1:2]),
          ("q1m", "w1meanb"), ("q1scale", "rows0"))

    for qf in range(NFQB):
        fsl = slice(FQB * qf, FQB * (qf + 1))
        for ic in range(8):
            t_dma("sync", xq1s[ic][:], xq1d[128 * ic:128 * (ic + 1), fsl], "dml",
                  (f"xq1d:{ic}", f"xq1s:{ic}"), (f"xq1s:{ic}",))
        tape.batch_end([f"xq1s:{i}" for i in range(8)])
        t_mm(scr[:, 0:FQB], ones_row[:], rows[0:1, fsl], reads=("q1scale", "ones_row"),
             writes=("scr",), start=True, stop=True)
        t_dve(lambda e: e.tensor_copy(out=pscs[:], in_=scr[:, 0:FQB]), ("scr",), ("pscs",))
        for hc in range(32):
            bank = stb[:, QB * (hc % 2):QB * (hc % 2) + FQB]
            for ic in range(8):
                t_mm(bank, w1q[ic][:, 128 * hc:128 * (hc + 1)], xq1s[ic][:],
                     reads=(f"w1q:{ic}", f"xq1s:{ic}"), writes=(f"stb:{hc % 2}",),
                     start=(ic == 0), stop=(ic == 7))
            t_dve(lambda e, b=bank: e.tensor_mul(tf[:], b, pscs[:]),
                  (f"stb:{hc % 2}", "pscs"), ("tf",))
            t_act(hmid[hc][:], tf[:], AF.Gelu, ("tf", "b1cs", "hmid_all"),
                  ("hmid_all", f"hm:{hc}"), bias=b1cs[:, hc:hc + 1], scale=1.0)
        if qf == 0:
            tap("t_hm0", hmid[0][:], ("hm:0",))
        emit_quant([hmid[c][:] for c in range(32)],
                   [f"hm:{c}" for c in range(32)],
                   [xq2[c][:] for c in range(32)],
                   [f"xq2:{c}" for c in range(32)],
                   f"q2_{qf}", rows[0:1, fsl])
        t_dve(lambda e, s=fsl: e.tensor_scalar_mul(rows[0:1, s], rows[0:1, s], 1.0 / 127.0),
              (f"q2_{qf}mrow",), (f"q2_{qf}mrow", "rows0"))
        t_dve(lambda e, s=fsl: e.tensor_scalar_mul(rows[0:1, s], rows[0:1, s],
                                                   swsc[0:1, 3:4]),
              (f"q2_{qf}mrow", "w2meanb"), (f"q2_{qf}scale", "rows0"))
        t_mm(scr[:, 0:FQB], ones_row[:], rows[0:1, fsl],
             reads=(f"q2_{qf}scale", "ones_row"), writes=("scr",), start=True, stop=True)
        t_dve(lambda e: e.tensor_copy(out=psc2s[:], in_=scr[:, 0:FQB]), ("scr",), ("psc2s",))
        if qf == 0:
            tap("t_xq2a", xq2[0][:], ("xq2:0",))
            tap("t_xq2b", xq2[31][:], ("xq2:31",))
            tap("t_psc2", psc2s[:], ("psc2s",))
            if debug:
                for c_ in range(32):
                    tap_ap = dbgout["t_xq2f"][128 * c_:128 * (c_ + 1), :]
                    tape.op("gpsimd", lambda e, a=tap_ap, c2=c_: e.dma_start(
                                out=a, in_=xq2[c2][:]),
                            reads=(f"xq2:{c_}",), writes=(), sem="dmg", inc=16)
                tap("t_w2q", w2qd, tuple(f"w2qd:{i}" for i in range(32)))
        for oc in range(8):
            bank = stb[:, 2 * QB + QB * (oc % 2):2 * QB + QB * (oc % 2) + FQB]
            for hc in range(32):
                t_mm(bank, w2q[hc][:, 128 * oc:128 * (oc + 1)], xq2[hc][:],
                     reads=(f"w2q:{hc}", f"xq2:{hc}", "xq2_all"),
                     writes=(f"stb2:{oc % 2}",),
                     start=(hc == 0), stop=(hc == 31))
            t_dve(lambda e, b=bank: e.tensor_mul(tf[:], b, psc2s[:]),
                  (f"stb2:{oc % 2}", "psc2s"), ("tf",))
            if qf == 0 and oc == 6:
                tap("t_y26", tf[:], ("tf",))
            if qf == 0 and debug:
                tap_ap2 = dbgout["t_y2all"][128 * oc:128 * (oc + 1), :]
                tape.op("gpsimd", lambda e, a=tap_ap2: e.dma_start(out=a, in_=tf[:]),
                        reads=("tf",), writes=(), sem="dmg", inc=16)
            xs = xstg2[oc % 2]
            t_dma("sync", xs[:, 0:FQB], x1d[128 * oc:128 * (oc + 1), fsl],
                  "dmx" if oc % 2 == 0 else "dmx1",
                  (f"x1d:{oc}", f"xstg2:{oc % 2}"), (f"xstg2:{oc % 2}",))
            t_dve(lambda e, x=xs, o=oc: e.scalar_tensor_tensor(
                    tf2[:], tf[:], modg[:, 8 + o:9 + o], x[:, 0:FQB], ALU.mult, ALU.add),
                  ("tf", f"xstg2:{oc % 2}", "modg"), ("tf2", f"xstg2:{oc % 2}"))
            t_dve(lambda e, o=oc: e.tensor_scalar_add(tf[:], tf2[:], gbb2[:, o:o + 1]),
                  ("tf2", "gbb2"), ("tf", f"x2t:{oc}:{qf}"))
            t_dma("gpsimd", outT[128 * oc:128 * (oc + 1), fsl], tf[:], "dmg",
                  (f"x2t:{oc}:{qf}", "tf"), (f"out:{oc}:{qf}",))

    tape.op("gpsimd", lambda e: e.engine_nop(),
            reads=tuple(f"out:{o}:{q}" for o in range(8) for q in range(NFQB)),
            writes=("done",))

    from contextlib import ExitStack
    es = ExitStack()
    sem_keys = ["tensor", "vector", "scalar", "gpsimd", "dml", "dmx", "dmx1",
                "dmk", "dmk1", "dms", "dms1", "dmg", "cc"]
    sems = {k: es.enter_context(nc.semaphore(f"s_{k}")) for k in sem_keys}
    SEMS.update(sems)
    block = es.enter_context(nc.Block())
    for name in ("tensor", "vector", "scalar", "gpsimd", "sync"):
        def f(eng, n=name):
            tape.emit_engine(n, eng, sems)
        getattr(block, name)(f)
    es.close()
    return nc



def _stage(inputs):
    x = inputs["L0_attn_x"]
    cos, sin = np.asarray(inputs["L0_attn_rope_cos"]), np.asarray(inputs["L0_attn_rope_sin"])
    qn, kn = np.asarray(inputs["L0_attn_q_norm_w"]), np.asarray(inputs["L0_attn_k_norm_w"])
    x = np.asarray(x)
    wqT = np.ascontiguousarray(np.asarray(inputs["L0_attn_wq"]).T)
    wkT = np.ascontiguousarray(np.asarray(inputs["L0_attn_wk"]).T)
    wvT = np.ascontiguousarray(np.asarray(inputs["L0_attn_wv"]).T)
    woT = np.ascontiguousarray(np.asarray(inputs["L0_attn_wo"]).T)
    w1T = np.ascontiguousarray(np.asarray(inputs["L0_ffn_w1"]).T)
    w2T = np.ascontiguousarray(np.asarray(inputs["L0_ffn_w2"]).T)
    wmodT = np.ascontiguousarray(np.asarray(inputs["L0_wmod"]))
    b1cv = np.ascontiguousarray(np.asarray(inputs["L0_ffn_b1"]).reshape(32, 128).T)
    b2cv = np.ascontiguousarray(np.asarray(inputs["L0_ffn_b2"]).reshape(8, 128).T)
    gammarv = np.ascontiguousarray(np.asarray(inputs["L0_ffn_gamma"]).reshape(8, 128).T)
    betacv = np.ascontiguousarray(np.asarray(inputs["L0_ffn_beta"]).reshape(8, 128).T)
    iden = np.eye(128, dtype=np.float32)
    iotaMv = np.tile(np.arange(64, dtype=np.float32)[:, None], (1, 128))
    altv = np.zeros((128, 8), np.float32)
    altv[0::2, 0] = -1.0
    altv[1::2, 0] = 1.0
    wqn2v = np.zeros((2, 128), np.float32)
    wqn2v[0, 0:64] = qn
    wqn2v[1, 64:128] = qn
    wkn2v = np.zeros((2, 128), np.float32)
    wkn2v[0, 0:64] = kn
    wkn2v[1, 64:128] = kn
    maps = []
    for core in range(8):
        b, s = core // 4, core % 4
        xTv = np.ascontiguousarray(x[b, 2048 * s:2048 * (s + 1), :].T)
        c_rowv = np.ascontiguousarray(np.asarray(inputs["L0_c"])[b].reshape(8, 128).T)
        t0 = 2048 * s
        C = np.ones((128, T), np.float32)
        S = np.zeros((128, T), np.float32)
        lo = max(0, (SEQ - PATCH) - t0)
        if lo < T:
            pos0 = t0 + lo - (SEQ - PATCH)
            n = T - lo
            crep = np.repeat(cos[pos0:pos0 + n, :].T, 2, axis=0)
            srep = np.repeat(sin[pos0:pos0 + n, :].T, 2, axis=0)
            C[:, lo:] = np.tile(crep, (2, 1))
            S[:, lo:] = np.tile(srep, (2, 1))
        maps.append(dict(
            xT=xTv, wqT=wqT, wkT=wkT, wvT=wvT, woT=woT, w1T=w1T, w2T=w2T,
            wmodT=wmodT, c_row=c_rowv, ropeC=np.ascontiguousarray(C),
            ropeS=np.ascontiguousarray(S), alt=altv, wqn2=wqn2v, wkn2=wkn2v,
            b1c=b1cv, b2c=b2cv, gammar=gammarv, betac=betacv, iden=iden,
            iotaM=iotaMv,
        ))
    return maps


def kernel(**inputs):
    maps = _stage(inputs)
    nc = build()
    res = run_bass_kernel_spmd(nc, maps, core_ids=list(range(8)))
    out = np.empty((2, 8192, 1024), np.float32)
    for core in range(8):
        b, s = core // 4, core % 4
        out[b, 2048 * s:2048 * (s + 1), :] = res.results[core]["outT"].T
    return out


# version 11 (final)
# speedup vs baseline: 1.0134x; 1.0026x over previous
import math
import numpy as np
import concourse.bass as bass
import concourse.mybir as mybir
from concourse.bass_utils import run_bass_kernel_spmd
from concourse import library_config
from concourse.bass_isa import ReduceOp

F32 = mybir.dt.float32
BF16 = mybir.dt.bfloat16
AF = mybir.ActivationFunctionType
ALU = mybir.AluOpType
AX = mybir.AxisListType.X

T = 2048
DM = 1024
NH = 16
HD = 64
SEQ = 8192
PATCH = 4064
FFH = 4096
QB = 512
NQB = T // QB
SUP = 512
NSUP = SEQ // SUP
CPS = SUP // 128
FQB = 256
NFQB = T // FQB
RND_C = float(3 << 22)
PAIRSWAP = [x for i in range(16) for x in (2 * i + 1, 2 * i)]

ENG = ("tensor", "vector", "scalar", "gpsimd", "sync")


class Tape:
    def __init__(self):
        self.ops = []
        self.last_write = {}
        self.readers = {}
        self.counts = {}
        self.op_sem = []
        self.drains = {}
        self._W = None

    def _build_W(self):
        import bisect
        W = {}
        for (_, _, waits, _) in self.ops:
            for k, v in waits.items():
                W.setdefault(k, set()).add(v)
        for (k, v) in self.drains.values():
            if v > 0:
                W.setdefault(k, set()).add(v)
        self._W = {k: sorted(s) for k, s in W.items()}
        self._Wset = {k: set(s) for k, s in W.items()}

    def remap(self, k, v, unit):
        import bisect
        if str(k).startswith("dm") or str(k) == "cc":
            return v
        lst = self._W.get(k, [])
        return unit * bisect.bisect_right(lst, v)

    def op(self, engine, emit, reads=(), writes=(), sem=None, inc=1, extra_deps=()):
        idx = len(self.ops)
        deps = set(extra_deps)
        for r in reads:
            w = self.last_write.get(r)
            if w is not None:
                deps.add(w)
            self.readers.setdefault(r, []).append(idx)
        for r in writes:
            w = self.last_write.get(r)
            if w is not None:
                deps.add(w)
            for rd in self.readers.get(r, ()):
                deps.add(rd)
            self.last_write[r] = idx
            self.readers[r] = []
        key = sem if sem is not None else engine
        cnt = self.counts.get(key, 0) + inc
        self.counts[key] = cnt
        self.op_sem.append((key, cnt))
        waits = {}
        for d in deps:
            if d == idx:
                continue
            k, v = self.op_sem[d]
            if waits.get(k, -1) < v:
                waits[k] = v
        self.ops.append((engine, emit, waits, (key, inc)))
        return idx

    def dma_drain(self, queue, semk, sems_ref):
        v = self.counts.get(semk, 0)
        idx = self.op(queue,
                      lambda e, v_=v, k_=semk: (
                          e.wait_ge(sems_ref[k_], self.remap(k_, v_, 16)), None)[1],
                      sem=semk, inc=0)
        self.drains[idx] = (semk, v)

    def batch_end(self, resources):
        i = len(self.ops) - 1
        for r in resources:
            if r in self.last_write:
                self.last_write[r] = i

    def barrier(self, writes, seed):
        idx = self.op("vector", lambda e: e.engine_nop(), writes=tuple(writes))
        for s in seed:
            self.last_write[s] = idx
            self.readers[s] = []
        return idx

    def emit_engine(self, engine, eng, sems):
        if self._W is None:
            self._build_W()
        satisfied = {}
        for i, (e, emit, waits, (key, inc)) in enumerate(self.ops):
            if e != engine:
                continue
            for k in sorted(waits, key=str):
                unit = 16 if str(k).startswith(("dm", "cc_x")) else 1
                v = self.remap(k, waits[k], unit)
                if satisfied.get(k, -1) >= v:
                    continue
                eng.wait_ge(sems[k], v)
                satisfied[k] = v
            ins = emit(eng)
            if ins is not None and inc > 0:
                cnt = self.op_sem[i][1]
                if (str(key).startswith("dm") or str(key) == "cc"
                        or cnt in self._Wset.get(key, ())):
                    ins.then_inc(sems[key], inc)


def build(debug=False):
    nc = bass.Bass("TRN2", target_bir_lowering=False)
    ein = lambda n, s, dt=F32: nc.dram_tensor(n, s, dt, kind="ExternalInput").ap()
    eout = lambda n, s, dt=F32: nc.dram_tensor(n, s, dt, kind="ExternalOutput").ap()
    idram = lambda n, s, dt=F32: nc.dram_tensor(n, s, dt).ap()

    xT = ein("xT", [DM, T])
    wqT = ein("wqT", [DM, DM])
    wkT = ein("wkT", [DM, DM])
    wvT = ein("wvT", [DM, DM])
    woT = ein("woT", [DM, DM])
    w1T = ein("w1T", [DM, FFH])
    w2T = ein("w2T", [FFH, DM])
    wmodT = ein("wmodT", [DM, 6 * DM])
    c_row = ein("c_row", [128, 8])
    ropeC = ein("ropeC", [128, T])
    ropeS = ein("ropeS", [128, T])
    alt = ein("alt", [128, 8])
    wqn2 = ein("wqn2", [2, 128])
    wkn2 = ein("wkn2", [2, 128])
    b1c = ein("b1c", [128, 32])
    b2c = ein("b2c", [128, 8])
    gammar = ein("gammar", [128, 8])
    betac = ein("betac", [128, 8])
    iden = ein("iden", [128, 128])
    iotaM = ein("iotaM", [64, 128])
    outT = eout("outT", [DM, T])

    kbounce = idram("kbounce", [DM, T], BF16)
    kag = idram("kag", [4 * DM, T], BF16)
    vbounce = idram("vbounce", [T, DM], BF16)
    vag = idram("vag", [SEQ, DM], BF16)
    knb = idram("knb", [2, 8], F32)
    kna = idram("kna", [2, 8], F32)
    x1d = idram("x1d", [DM, T], F32)
    xq1d = idram("xq1d", [DM, T], BF16)
    w1qd = idram("w1qd", [DM, FFH], BF16)
    w2qd = idram("w2qd", [FFH, DM], BF16)
    groups = [[0, 1, 2, 3], [4, 5, 6, 7]]
    dbgout = {}
    if debug:
        for nm, shape, dt in [
            ("t_xq2a", [128, FQB], BF16), ("t_xq2b", [128, FQB], BF16),
            ("t_xq2f", [FFH, FQB], BF16), ("t_y2all", [DM, FQB], F32),
            ("t_w2q", [FFH, DM], BF16),
            ("t_psc2", [128, FQB], F32), ("t_y26", [128, FQB], F32),
            ("t_mod", [128, 48], F32), ("t_ha0", [128, T], BF16),
            ("t_q0", [128, T], BF16), ("t_kb", [DM, T], BF16),
            ("t_vb", [T, DM], BF16), ("t_mneg", [128, 8], F32),
            ("t_drec", [64, QB], F32), ("t_osb", [128, 32 * QB], F32),
            ("t_x1", [DM, T], F32), ("t_h20", [128, T], BF16),
            ("t_xq1", [DM, T], BF16), ("t_w1q", [DM, FFH], BF16),
            ("t_hm0", [128, FQB], BF16), ("t_swsc", [128, 8], F32),
            ("t_kag", [4 * DM, T], BF16), ("t_vag", [SEQ, DM], BF16),
        ]:
            dbgout[nm] = eout(nm, shape, dt)

    BASE = 16512
    R_A, R_B, R_C, R_D, R_E = (BASE, BASE + 65536, BASE + 98304, BASE + 131072,
                               BASE + 163840)
    R_S = BASE + 196608
    SM = BASE + 204800

    def at(name, shape, dt, off):
        return nc.alloc_sbuf_tensor_at(name, shape, dt, offset=off)

    x32 = [at(f"x32_{i}", [128, T], F32, R_A + i * 8192) for i in range(8)]
    osb = at("osb", [128, 32 * QB], F32, R_A)
    x1sb = [at(f"x1_{i}", [128, T], F32, R_A + i * 8192) for i in range(8)]
    w1q = [at(f"w1q_{i}", [128, FFH], BF16, R_A + i * 8192) for i in range(8)]
    ha = [at(f"ha_{i}", [128, T], BF16, R_B + i * 4096) for i in range(8)]
    ptb = [at(f"ptb_{i}", [128, 4 * SUP * 2], BF16, R_B + i * 8192) for i in range(2)]
    onrm = at("onrm", [128, 32 * QB], BF16, R_B)
    ropeCs = at("ropeCs", [128, T], F32, R_C)
    ropeSs = at("ropeSs", [128, T], F32, R_C + 8192)
    qstg = [at(f"qstg_{i}", [128, QB], F32, R_C + 16384 + i * 2048) for i in range(4)]
    ksup = [at(f"ksup_{i}", [128, CPS * 128 * 8], BF16, R_C + i * 8192) for i in range(2)]
    vsup = [at(f"vsup_{i}", [128, CPS * DM], BF16, R_C + 16384 + i * 8192) for i in range(2)]
    woTs = [at(f"woTs_{i}", [128, DM], BF16, R_C + i * 2048) for i in range(8)]
    h2lo = [at(f"h2lo_{i}", [128, T], BF16, R_C + 16384 + i * 4096) for i in range(4)]
    w2qh = [at(f"w2qh_{i}", [128, DM], BF16, R_C + i * 2048) for i in range(16)]
    wstg = at("wstg", [128, FFH], F32, R_D)
    wstg2 = at("wstg2", [128, FFH], BF16, R_D + 16384)
    wbf = [at(f"wbf_{i}", [128, DM], BF16, R_D + i * 2048) for i in range(8)]
    wq32 = at("wq32", [128, DM], F32, R_D + 16384)
    kstg = [at(f"kstg_{i}", [128, T], BF16, R_D + 20480 + i * 4096) for i in range(2)]
    vstg = [at(f"vstg_{i}", [128, DM], BF16, R_D + 28672 + i * 2048) for i in range(2)]
    hb = [at(f"hb_{i}", [128, T], BF16, R_D + i * 4096) for i in range(8)]
    hmid = [at(f"hmid_{i}", [128, FQB], BF16, R_D + i * 512) for i in range(32)]
    xq2 = [at(f"xq2_{i}", [128, FQB], BF16, R_D + 16384 + i * 512) for i in range(32)]
    xq1s = [at(f"xq1s_{i}", [128, FQB], BF16, R_B + 10240 + i * 512) for i in range(8)]
    xq1w = [at(f"xq1w_{i}", [128, T], BF16, R_D + i * 4096) for i in range(2)]
    qsb = [at(f"qsb_{i}", [128, T], BF16, R_E + i * 4096) for i in range(8)]
    h2hi = [at(f"h2hi_{i}", [128, T], BF16, R_E + i * 4096) for i in range(4)]
    w2ql = [at(f"w2ql_{i}", [128, DM], BF16, R_E + i * 2048) for i in range(16)]
    rows = at("rows", [6, T], F32, R_S)
    tts = at("tts", [128, T], F32, R_B + 24576)
    q2sb = at("q2sb", [128, QB], F32, R_A)
    w2r = at("w2r", [128, QB], F32, R_A + 2048)
    tq2 = at("tq2", [128, QB], F32, R_A + 4096)
    q2sbB = at("q2sbB", [128, QB], F32, R_A + 6144)
    w2rB = at("w2rB", [128, QB], F32, R_A + 8192)
    tq2B = at("tq2B", [128, QB], F32, R_A + 10240)
    tqB = at("tqB", [128, QB], F32, R_A + 12288)
    drec = at("drec", [64, QB], F32, R_C + 16384)
    dbcs = at("dbcs", [128, QB], F32, R_C + 18432)
    xstg = [at(f"xstg_{i}", [128, QB], F32, R_C + 20480 + i * 2048)
            for i in range(2)]
    xstg2 = [at(f"xstg2_{i}", [128, QB], F32, R_B + i * 2048) for i in range(2)]
    pscs = at("pscs", [128, FQB], F32, R_B + 4096)
    psc2s = at("psc2s", [128, FQB], F32, R_B + 5120)
    tf = at("tf", [128, FQB], F32, R_B + 6144)
    tf2 = at("tf2", [128, FQB], F32, R_B + 7168)
    sxf = at("sxf", [128, FQB], F32, R_B + 8192)
    mxf = at("mxf", [128, FQB], F32, R_B + 9216)
    off = [SM]

    def sm(name, shape, dt):
        nbytes = int(np.prod(shape[1:])) * mybir.dt.size(dt)
        t = at(name, shape, dt, off[0])
        off[0] += (nbytes + 31) // 32 * 32
        return t

    tq = sm("tq", [128, QB], F32)
    CONSTS = [0.0, 1.0, -1.0, -0.5, 0.5, 1e-5, 1e-6, 1.0 / HD, math.log(0.125)]
    cpool = sm("cpool", [128, len(CONSTS)], F32)
    mod_sb = sm("mod_sb", [128, 48], F32)
    modg = sm("modg", [128, 16], F32)
    modca = sm("modca", [128, 16], F32)
    gbb2 = sm("gbb2", [128, 8], F32)
    csrow = sm("csrow", [128, 8], F32)
    cwork = sm("cwork", [128, 32], F32)
    onesel = sm("onesel", [128, 64], BF16)
    zer128 = sm("zer128", [128, 128], BF16)
    idn = sm("idn", [128, 128], F32)
    alts = sm("alts", [128, 8], F32)
    wqn2s = sm("wqn2s", [2, 128], F32)
    wkn2s = sm("wkn2s", [2, 128], F32)
    b1cs = sm("b1cs", [128, 32], F32)
    b2cs = sm("b2cs", [128, 8], F32)
    gammacs = sm("gammacs", [128, 8], F32)
    betacs = sm("betacs", [128, 8], F32)
    mneg = sm("mneg", [128, 8], F32)
    qn2m = sm("qn2m", [2, 16], F32)
    kn2m = sm("kn2m", [2, 8], F32)
    kn2g = sm("kn2g", [2, 8], F32)
    mm2s = sm("mm2s", [2, 8], F32)
    selD = sm("selD", [64, 128], F32)
    selDb = sm("selDb", [64, 128], F32)
    mxt = sm("mxt", [128, 16], F32)
    mxr = sm("mxr", [128, 16], F32)
    iotas = sm("iotas", [64, 128], F32)
    hselF = sm("hselF", [128, 2], F32)
    ones_row = sm("ones_row", [1, 128], F32)
    onescF = sm("onescF", [128, 1], F32)
    csrowb = sm("csrowb", [128, 8], BF16)
    swsc = sm("swsc", [128, 8], F32)
    assert off[0] <= 229344, f"smalls overflow {off[0]}"

    sxr = at("sxr", [128, T], F32, R_B)
    mxrow = at("mxrow", [128, T], F32, R_B + 8192)
    tts2 = at("tts2", [128, T], F32, R_B + 16384)

    stb = nc.alloc_psum_tensor("stb", [128, 4 * QB], F32)
    ow = [nc.alloc_psum_tensor(f"ow_{i}", [128, QB], F32) for i in range(2)]
    dbank = nc.alloc_psum_tensor("dbank", [64, QB], F32)
    scr = nc.alloc_psum_tensor("scr", [128, 512], F32)

    tape = Tape()
    SEMS = {}

    def t_mm(out_ap, lhs_ap, rhs_ap, reads, writes, start=True, stop=True, tp=None):
        return tape.op("tensor",
                       lambda e: e.matmul(out_ap, lhs_ap, rhs_ap, start=start, stop=stop,
                                          tile_position=tp),
                       reads=reads, writes=writes)

    def t_act(out_ap, in_ap, func, reads, writes, bias=0.0, scale=1.0):
        return tape.op("scalar",
                       lambda e: e.activation(out_ap, in_ap, func, bias=bias, scale=scale),
                       reads=reads, writes=writes)

    def t_dve(fn, reads, writes):
        return tape.op("vector", fn, reads=reads, writes=writes)

    def tap(nm, in_ap, reads):
        if debug:
            tape.op("gpsimd", lambda e: e.dma_start(out=dbgout[nm], in_=in_ap),
                    reads=reads, writes=(), sem="dmg", inc=16)

    def t_dma(queue, out_ap, in_ap, semk, reads, writes):
        return tape.op(queue, lambda e: e.dma_start(out=out_ap, in_=in_ap),
                       reads=reads, writes=writes, sem=semk, inc=16)

    t_dma("sync", cwork[:, 0:8], c_row, "dml", (), ("c_in",))
    t_dma("sync", ropeCs[:], ropeC, "dml", (), ("ropeCs",))
    t_dma("sync", ropeSs[:], ropeS, "dml", (), ("ropeSs",))
    t_dma("sync", alts[:], alt, "dml", (), ("alts",))
    t_dma("sync", wqn2s[:], wqn2, "dml", (), ("wqn2s",))
    t_dma("sync", wkn2s[:], wkn2, "dml", (), ("wkn2s",))
    t_dma("sync", b1cs[:], b1c, "dml", (), ("b1cs",))
    t_dma("sync", b2cs[:], b2c, "dml", (), ("b2cs",))
    t_dma("sync", gammacs[:], gammar, "dml", (), ("gammacs",))
    t_dma("sync", betacs[:], betac, "dml", (), ("betacs",))
    t_dma("sync", idn[:], iden, "dml", (), ("idn",))
    t_dma("sync", iotas[:], iotaM, "dml", (), ("iotas",))
    tape.batch_end(["c_in", "ropeCs", "ropeSs", "alts", "wqn2s", "wkn2s", "b1cs",
                    "b2cs", "gammacs", "betacs", "idn", "iotas"])
    tape.dma_drain("sync", "dml", SEMS)
    for i in range(8):
        t_dma("sync", x32[i][:], xT[128 * i:128 * (i + 1), :], "dmx", (), (f"x32:{i}",))
    tape.batch_end([f"x32:{i}" for i in range(8)])
    tape.dma_drain("sync", "dmx", SEMS)

    for i_, v_ in enumerate(CONSTS):
        t_dve(lambda e, i=i_, v=v_: e.memset(cpool[:, i:i + 1], v), ("cpool",), ("cpool",))
        nc.const_aps.register(F32, v_, cpool[:, i_:i_ + 1])
    t_dve(lambda e: e.memset(onesel[:, 0:32], 0.0), (), ("onesel",))
    t_dve(lambda e: e.memset(onesel[:, 32:33], 1.0), ("onesel",), ("onesel",))
    t_dve(lambda e: e.memset(onesel[:, 33:64], 0.0), ("onesel",), ("onesel",))
    t_dve(lambda e: e.memset(zer128[:], 0.0), (), ("zer128",))
    t_dve(lambda e: e.memset(hselF[0:64, 0:1], 1.0), (), ("hselF",))
    t_dve(lambda e: e.memset(hselF[64:128, 0:1], 0.0), ("hselF",), ("hselF",))
    t_dve(lambda e: e.memset(hselF[0:64, 1:2], 0.0), ("hselF",), ("hselF",))
    t_dve(lambda e: e.memset(hselF[64:128, 1:2], 1.0), ("hselF",), ("hselF",))
    t_dve(lambda e: e.memset(ones_row[:], 1.0), (), ("ones_row",))
    t_dve(lambda e: e.memset(onescF[:], 1.0), (), ("onescF",))

    t_act(cwork[:, 8:16], cwork[:, 0:8], AF.Exp, ("c_in", "cpool"), ("c_exp",), scale=-1.0)
    t_dve(lambda e: e.tensor_scalar_add(cwork[:, 8:16], cwork[:, 8:16], 1.0),
          ("c_exp",), ("c_exp",))
    t_dve(lambda e: e.reciprocal(cwork[:, 16:24], cwork[:, 8:16]), ("c_exp",), ("c_rec",))
    t_dve(lambda e: e.tensor_mul(csrow[:], cwork[:, 0:8], cwork[:, 16:24]),
          ("c_in", "c_rec"), ("csrow",))
    t_dve(lambda e: e.tensor_copy(out=csrowb[:], in_=csrow[:]), ("csrow",), ("csrowb",))

    first = [True]
    for ic in range(8):
        for half in range(2):
            sl = slice(3072 * half, 3072 * (half + 1))
            t_dma("sync", wstg[:, 0:3072], wmodT[128 * ic:128 * (ic + 1), sl],
                  "dml", ("wstg_r",), ("wstg",))
            t_dve(lambda e: e.tensor_copy(out=wstg2[:, 0:3072], in_=wstg[:, 0:3072]),
                  ("wstg",), ("wstg2", "wstg_r"))
            for j in range(24):
                oc = 24 * half + j
                t_mm(scr[:, oc:oc + 1], wstg2[:, 128 * j:128 * (j + 1)],
                     csrowb[:, ic:ic + 1],
                     reads=("wstg2", "csrowb"), writes=("scr",),
                     start=first[0], stop=(ic == 7 and half == 1 and j == 23))
                first[0] = False
    t_dve(lambda e: e.tensor_copy(out=mod_sb[:], in_=scr[:, 0:48]), ("scr",), ("mod_sb",))
    tape.barrier(["wstg", "wstg2"], ["wq32"] + [f"kst:{i}" for i in range(2)])
    tap("t_mod", mod_sb[:], ("mod_sb",))

    def g_clamp(src_lo, dst_lo):
        t_dve(lambda e: e.tensor_scalar(cwork[:, 0:8], mod_sb[:, src_lo:src_lo + 8],
                                        0.0, -1.0, ALU.is_ge, ALU.bypass),
              ("mod_sb",), ("cw_ge",))
        t_dve(lambda e: e.tensor_scalar(cwork[:, 8:16], cwork[:, 0:8], 2.0, -1.0,
                                        ALU.mult, ALU.add), ("cw_ge",), ("cw_sg",))
        t_dve(lambda e: e.scalar_tensor_tensor(cwork[:, 16:24],
                                               mod_sb[:, src_lo:src_lo + 8], -1.0,
                                               mod_sb[:, src_lo:src_lo + 8],
                                               ALU.mult, ALU.max),
              ("mod_sb",), ("cw_ab",))
        t_dve(lambda e: e.tensor_scalar_max(cwork[:, 16:24], cwork[:, 16:24], 0.25),
              ("cw_ab",), ("cw_ab",))
        t_dve(lambda e, d=dst_lo: e.tensor_mul(modg[:, d:d + 8], cwork[:, 8:16],
                                               cwork[:, 16:24]),
              ("cw_sg", "cw_ab"), ("modg",))
    g_clamp(16, 0)
    g_clamp(40, 8)

    t_dve(lambda e: e.tensor_scalar_add(modca[:, 0:8], mod_sb[:, 8:16], 1.0),
          ("mod_sb",), ("modca",))
    t_dve(lambda e: e.tensor_scalar_add(modca[:, 8:16], mod_sb[:, 32:40], 1.0),
          ("mod_sb", "modca"), ("modca",))
    t_dve(lambda e: e.tensor_mul(gbb2[:], modg[:, 8:16], b2cs[:]), ("modg", "b2cs"), ("gbb2",))

    def emit_adaln_ln(src_tag, src, dst_tag, dst, scale_vec, shift_vec, svtag, tag,
                      src_bf16=False):
        ones_col = onesel[:, 32:33] if src_bf16 else onescF[:]
        ones_tag = "onesel" if src_bf16 else "onescF"
        for qb in range(NQB):
            sl = slice(QB * qb, QB * (qb + 1))
            for ci in range(8):
                t_mm(scr[0:1, 0:QB], ones_col, src[ci][:, sl],
                     reads=(f"{src_tag}:{ci}", ones_tag), writes=("scr",),
                     start=(ci == 0), stop=(ci == 7))
            for ci in range(8):
                t_dve(lambda e, c=ci, s=sl: e.tensor_mul(tq[:], src[c][:, s], src[c][:, s]),
                      reads=(f"{src_tag}:{ci}", "tq"), writes=("tq",))
                t_mm(ow[0][0:1, 0:QB], onescF[:], tq[:],
                     reads=("tq", "onescF"), writes=("ow0",),
                     start=(ci == 0), stop=(ci == 7))
            t_dve(lambda e: e.tensor_scalar_mul(rows[0:1, 1024:1536], scr[0:1, 0:QB],
                                                1.0 / DM),
                  reads=("scr", "rows0"), writes=(f"{tag}mean", "rows0"))
            t_dve(lambda e: e.tensor_scalar_mul(rows[0:1, 1536:2048], ow[0][0:1, 0:QB],
                                                1.0 / DM),
                  reads=("ow0", "rows0"), writes=(f"{tag}e2", "rows0"))
            t_dve(lambda e: e.tensor_mul(scr[0:1, 0:QB], rows[0:1, 1024:1536],
                                         rows[0:1, 1024:1536]),
                  reads=(f"{tag}mean", "scr"), writes=("scr",))
            t_dve(lambda e: e.tensor_sub(rows[0:1, 1536:2048], rows[0:1, 1536:2048],
                                         scr[0:1, 0:QB]),
                  reads=(f"{tag}e2", "scr"), writes=(f"{tag}var", "rows0"))
            t_act(scr[0:1, 0:QB], rows[0:1, 1536:2048], AF.Ln,
                  (f"{tag}var",), ("scr",), bias=1e-5, scale=1.0)
            t_act(rows[0:1, 0:QB], scr[0:1, 0:QB], AF.Exp,
                  ("scr", "rows0"), (f"{tag}rs", "rows0"), bias=0.0, scale=-0.5)
            t_dve(lambda e: e.tensor_mul(rows[0:1, 512:1024], rows[0:1, 0:QB],
                                         rows[0:1, 1024:1536]),
                  reads=(f"{tag}mean", f"{tag}rs", "rows0"), writes=(f"{tag}mrs", "rows0"))
            t_mm(scr[:, 0:QB], ones_row[:], rows[0:1, 0:QB],
                 reads=(f"{tag}rs", "ones_row"), writes=("scr",), start=True, stop=True)
            t_mm(ow[0][:, 0:QB], ones_row[:], rows[0:1, 512:1024],
                 reads=(f"{tag}mrs", "ones_row"), writes=("ow0",), start=True, stop=True)
            for ci in range(8):
                t_dve(lambda e, c=ci, s=sl: e.tensor_mul(tq[:], src[c][:, s], scr[:, 0:QB]),
                      reads=(f"{src_tag}:{ci}", "scr", "tq"), writes=("tq",))
                t_dve(lambda e: e.tensor_sub(tq[:], tq[:], ow[0][:, 0:QB]),
                      reads=("tq", "ow0"), writes=("tq",))
                t_dve(lambda e, c=ci, s=sl: e.tensor_scalar(
                        dst[c][:, s], tq[:], scale_vec(c), shift_vec(c),
                        ALU.mult, ALU.add),
                      reads=("tq", svtag), writes=(f"{dst_tag}:{ci}",))

    tape.last_write["svt_ln1"] = max(tape.last_write.get("modca", 0),
                                     tape.last_write.get("mod_sb", 0))
    emit_adaln_ln("x32", x32, "ha", ha, lambda c: modca[:, c:c + 1],
                  lambda c: mod_sb[:, 0 + c:1 + c], "svt_ln1", "ln1")

    tape.barrier([f"x32:{i}" for i in range(8)],
                 ["q2sb", "w2r", "qk_ln:0", "qk_ln:1", "qk_rf:0", "qk_rf:1", "tq2p",
                  "q2sbB", "w2rB", "tq2B", "tqB"])
    tap("t_ha0", ha[0][:], ("ha:0",))

    def load_w_bf16(wT_, dtag):
        for ci in range(8):
            t_dma("sync", wq32[:], wT_[128 * ci:128 * (ci + 1), :], "dml",
                  ("wq32_r",), ("wq32",))
            t_dve(lambda e, c=ci: e.tensor_copy(out=wbf[c][:], in_=wq32[:]),
                  ("wq32",), (f"wbf:{ci}", "wq32_r"))

    def emit_qk(wT_, wn2s_ap, wn2tag, out_write, lnbias, kn_tag=None):
        for oc in range(8):
            for qb in range(NQB):
                par2 = qb % 2
                sl = slice(QB * qb, QB * (qb + 1))
                bank = stb[:, QB * par2:QB * par2 + QB]
                sbank = scr[:, 0:QB] if par2 == 0 else ow[1][:, 0:QB]
                sbank2 = scr[0:2, 0:QB] if par2 == 0 else ow[1][0:2, 0:QB]
                sbtag = "scr" if par2 == 0 else "ow1b"
                q2 = q2sb if par2 == 0 else q2sbB
                q2t = "q2sb" if par2 == 0 else "q2sbB"
                wr = w2r if par2 == 0 else w2rB
                wrt = "w2r" if par2 == 0 else "w2rB"
                t2 = tq2 if par2 == 0 else tq2B
                t2t = "tq2p" if par2 == 0 else "tq2B"
                t1 = tq if par2 == 0 else tqB
                t1t = "tq" if par2 == 0 else "tqB"
                for ic in range(8):
                    t_mm(bank, wbf[ic][:, 128 * oc:128 * (oc + 1)], ha[ic][:, sl],
                         reads=(f"wbf:{ic}", f"ha:{ic}"), writes=(f"stb:{par2}",),
                         start=(ic == 0), stop=(ic == 7))
                t_act(q2[:], bank, AF.Square, (f"stb:{par2}",), (q2t,))
                t_mm(sbank2, hselF[:], q2[:],
                     reads=(q2t, "hselF"), writes=(sbtag,), start=True, stop=True)
                t_act(t1[0:2, 0:QB], sbank2, AF.Ln, (sbtag,), (f"qk_ln:{par2}", t1t),
                      bias=1e-6, scale=1.0 / HD)
                t_act(t2[0:2, 0:QB], t1[0:2, 0:QB], AF.Exp, (f"qk_ln:{par2}",),
                      (f"qk_rf:{par2}", t2t), bias=lnbias, scale=-0.5)
                t_mm(sbank, wn2s_ap, t2[0:2, 0:QB],
                     reads=(f"qk_rf:{par2}", wn2tag), writes=(sbtag,),
                     start=True, stop=True)
                t_dve(lambda e, w_=wr, sb=sbank: e.tensor_copy(out=w_[:], in_=sb),
                      (sbtag,), (wrt,))
                stg = qstg[par2]
                stg2 = qstg[2 + par2]
                t_dve(lambda e, b=bank, s=stg, w_=wr: e.tensor_mul(s[:], b, w_[:]),
                      (f"stb:{par2}", wrt), (f"qkstg:{par2}",))
                t_dve(lambda e, s=stg, d=stg2: e.stream_shuffle(d[:], s[:], PAIRSWAP),
                      (f"qkstg:{par2}",), (f"qksh:{par2}",))
                t_dve(lambda e, s=stg, sli=sl, t_=t1: e.tensor_mul(t_[:], s[:],
                                                                  ropeCs[:, sli]),
                      (f"qkstg:{par2}", "ropeCs"), (t1t,))
                t_dve(lambda e, d=stg2, sli=sl: e.scalar_tensor_tensor(
                        d[:], d[:], alts[:, 0:1], ropeSs[:, sli], ALU.mult, ALU.mult),
                      (f"qksh:{par2}", "alts", "ropeSs"), (f"qkt2:{par2}",))
                out_write(oc, qb, stg2, t1, t1t)
            if kn_tag is not None:
                kn_tag(oc)

    load_w_bf16(wqT, "wq")

    def q_write(oc, qb, stg2, t1, t1t):
        sl = slice(QB * qb, QB * (qb + 1))
        t_dve(lambda e, d=stg2, o=qsb[oc], s=sl, t_=t1: e.tensor_add(o[:, s], t_[:], d[:]),
              (t1t, f"qkt2:{qb % 2}"), (f"qsb:{oc}",))

    emit_qk(wqT, wqn2s[:], "wqn2s", q_write, math.log(0.125))

    load_w_bf16(wkT, "wk")

    def k_write(oc, qb, stg2, t1, t1t):
        sl = slice(QB * qb, QB * (qb + 1))
        kst = kstg[oc % 2]
        t_dve(lambda e, d=stg2, o=kst, s=sl, t_=t1: e.tensor_add(o[:, s], t_[:], d[:]),
              (t1t, f"qkt2:{qb % 2}"), (f"kst:{oc % 2}",))
        stg = qstg[qb % 2]
        t_dve(lambda e, s=stg, t_=t1: e.tensor_mul(t_[:], s[:], s[:]),
              (f"qkstg:{qb % 2}",), (t1t,))
        t_mm(scr[0:2, 0:QB], hselF[:], t1[:], reads=(t1t, "hselF"), writes=("scr",),
             start=True, stop=True)
        t_dve(lambda e, q=qb: e.tensor_reduce(qn2m[0:2, 8 + q:9 + q], scr[0:2, 0:QB],
                                              AX, ALU.max),
              ("scr",), ("qn2m_s",))

    def k_fin(oc):
        t_dve(lambda e, o=oc: e.tensor_reduce(kn2m[0:2, o:o + 1], qn2m[0:2, 8:12],
                                              AX, ALU.max),
              ("qn2m_s",), ("kn2m",))
        t_dma("sync", kbounce[128 * oc:128 * (oc + 1), :], kstg[oc % 2][:],
              "dms" if oc % 2 == 0 else "dms1",
              (f"kst:{oc % 2}",), (f"kb:{oc}", f"kst_r:{oc % 2}"))

    emit_qk(wkT, wkn2s[:], "wkn2s", k_write, 0.0, kn_tag=k_fin)

    load_w_bf16(wvT, "wv")
    for tc in range(16):
        vst = vstg[tc % 2]
        for half in range(2):
            bank = stb[:, QB * half:QB * half + QB]
            for ic in range(8):
                t_mm(bank, ha[ic][:, 128 * tc:128 * (tc + 1)],
                     wbf[ic][:, 512 * half:512 * (half + 1)],
                     reads=(f"wbf:{ic}", f"ha:{ic}"), writes=(f"stb:{half}",),
                     start=(ic == 0), stop=(ic == 7))
            t_dve(lambda e, h=half, v=vst, b=bank: e.tensor_copy(
                    out=v[:, 512 * h:512 * (h + 1)], in_=b),
                  (f"stb:{half}",), (f"vst:{tc % 2}",))
        t_dma("sync", vbounce[128 * tc:128 * (tc + 1), :], vst[:],
              "dms" if tc % 2 == 0 else "dms1",
              (f"vst:{tc % 2}",), (f"vb:{tc}",))

    for oc in range(8):
        for qb in range(NQB):
            sl = slice(QB * qb, QB * (qb + 1))
            t_dve(lambda e, o=oc, s=sl: e.tensor_mul(tq[:], qsb[o][:, s], qsb[o][:, s]),
                  (f"qsb:{oc}",), ("tq",))
            t_mm(scr[0:2, 0:QB], hselF[:], tq[:], reads=("tq", "hselF"), writes=("scr",),
                 start=True, stop=True)
            t_dve(lambda e, q=qb: e.tensor_reduce(qn2m[0:2, 8 + q:9 + q], scr[0:2, 0:QB],
                                                  AX, ALU.max),
                  ("scr",), ("qn2m_s",))
        t_dve(lambda e, o=oc: e.tensor_reduce(qn2m[0:2, o:o + 1], qn2m[0:2, 8:12],
                                              AX, ALU.max),
              ("qn2m_s",), ("qn2m",))

    t_dma("gpsimd", knb, kn2m[:], "dmg", ("kn2m",), ())
    kn_dma = len(tape.ops) - 1
    tape.op("gpsimd",
            lambda e: e.collective_compute("AllReduce", ALU.max, replica_groups=groups,
                                           ins=[knb], outs=[kna]),
            reads=(), writes=("d_kna",), sem="cc", extra_deps=(kn_dma,))
    t_dma("gpsimd", kn2g[:], kna, "dmg", ("d_kna",), ("kn2g",))
    tape.op("gpsimd",
            lambda e: e.collective_compute("AllGather", ALU.bypass, replica_groups=groups,
                                           ins=[kbounce], outs=[kag]),
            reads=tuple(f"kb:{i}" for i in range(8)), writes=("d_kag",), sem="cc")
    tape.op("gpsimd",
            lambda e: e.collective_compute("AllGather", ALU.bypass, replica_groups=groups,
                                           ins=[vbounce], outs=[vag]),
            reads=tuple(f"vb:{i}" for i in range(16)), writes=("d_vag",), sem="cc")

    t_dve(lambda e: e.tensor_mul(mm2s[:], qn2m[0:2, 0:8], kn2g[:]),
          ("qn2m", "kn2g"), ("mm2s",))
    t_act(mm2s[:], mm2s[:], AF.Ln, ("mm2s",), ("mm2s",))
    t_act(mm2s[:], mm2s[:], AF.Exp, ("mm2s",), ("mm2s",), scale=0.5)
    tape.op("tensor", lambda e: e.transpose(scr[0:8, 32:34], mm2s[:], idn[0:2, 0:2]),
            reads=("mm2s", "idn"), writes=("scr",))
    t_dve(lambda e: e.tensor_reduce(cwork[0:8, 0:1], scr[0:8, 32:34], AX, ALU.max),
          ("scr",), ("cw_m",))
    tape.op("tensor", lambda e: e.transpose(scr[0:1, 64:72], cwork[0:8, 0:1], idn[0:8, 0:8]),
            reads=("cw_m", "idn"), writes=("scr",))
    t_dve(lambda e: e.tensor_copy(out=cwork[0:1, 8:16], in_=scr[0:1, 64:72]),
          ("scr",), ("cw_m2",))
    t_mm(scr[:, 0:8], ones_row[:], cwork[0:1, 8:16], reads=("cw_m2", "ones_row"),
         writes=("scr",), start=True, stop=True)
    t_dve(lambda e: e.tensor_scalar_mul(mneg[:], scr[:, 0:8], -1.0), ("scr",), ("mneg",))
    tap("t_q0", qsb[0][:], ("qsb:0",))
    tap("t_kb", kbounce, tuple(f"kb:{i}" for i in range(8)))
    tap("t_vb", vbounce, tuple(f"vb:{i}" for i in range(16)))
    tap("t_kag", kag, ("d_kag",))
    tap("t_vag", vag, ("d_vag",))
    tap("t_mneg", mneg[:], ("mneg",))



    def emit_tern(wT_, rowsn, cols, wqd, tag, sccol):
        nch = rowsn // 128
        for ci in range(nch):
            t_dma("gpsimd", wstg[:, 0:cols], wT_[128 * ci:128 * (ci + 1), :], "dmg",
                  ("wstg",), ("wstg",))
            t_dve(lambda e: e.scalar_tensor_tensor(wstg2[:, 0:cols], wstg[:, 0:cols],
                                                   -1.0, wstg[:, 0:cols],
                                                   ALU.mult, ALU.max),
                  ("wstg",), ("wstg2",))
            t_dve(lambda e: e.tensor_reduce(cwork[:, 24:25], wstg2[:, 0:cols], AX, ALU.add),
                  ("wstg2",), ("cw_rs",))
            t_mm(scr[0:1, 256 + sccol:257 + sccol], onescF[:], cwork[:, 24:25],
                 reads=("cw_rs", "onescF"), writes=("scr",),
                 start=(ci == 0), stop=(ci == nch - 1))
        t_dve(lambda e: e.tensor_scalar_mul(cwork[0:1, 25:26],
                                            scr[0:1, 256 + sccol:257 + sccol],
                                            1.0 / (rowsn * cols)),
              ("scr",), ("cw_mean",))
        t_dve(lambda e: e.tensor_scalar_max(cwork[0:1, 25:26], cwork[0:1, 25:26], 1e-5),
              ("cw_mean",), ("cw_mean",))
        t_mm(scr[:, 288:289], ones_row[:], cwork[0:1, 25:26],
             reads=("cw_mean", "ones_row"), writes=("scr",), start=True, stop=True)
        t_dve(lambda e, s=sccol: e.tensor_copy(out=swsc[:, s + 1:s + 2], in_=scr[:, 288:289]),
              ("scr",), (f"{tag}meanb",))
        t_dve(lambda e, s=sccol: e.reciprocal(swsc[:, s:s + 1], swsc[:, s + 1:s + 2]),
              (f"{tag}meanb",), (f"{tag}swrec",))
        for ci in range(nch):
            t_dma("gpsimd", wstg[:, 0:cols], wT_[128 * ci:128 * (ci + 1), :], "dmg",
                  ("wstg",), ("wstg",))
            t_dve(lambda e, s=sccol: e.tensor_scalar_mul(wstg[:, 0:cols], wstg[:, 0:cols],
                                                         swsc[:, s:s + 1]),
                  ("wstg", f"{tag}swrec"), ("wstg",))
            t_dve(lambda e: e.tensor_scalar_add(wstg[:, 0:cols], wstg[:, 0:cols], RND_C),
                  ("wstg",), ("wstg",))
            t_dve(lambda e: e.tensor_scalar(wstg[:, 0:cols], wstg[:, 0:cols], -RND_C, None,
                                            ALU.add),
                  ("wstg",), ("wstg",))
            t_dve(lambda e: e.tensor_scalar(wstg2[:, 0:cols], wstg[:, 0:cols], 1.0, -1.0,
                                            ALU.min, ALU.max),
                  ("wstg",), ("wstg2",))
            t_dma("gpsimd", wqd[128 * ci:128 * (ci + 1), :], wstg2[:, 0:cols], "dmg",
                  ("wstg2",), (f"{tag}qd:{ci}", "wstg2"))

    emit_tern(w1T, DM, FFH, w1qd, "w1", 0)
    emit_tern(w2T, FFH, DM, w2qd, "w2", 2)
    tape.barrier([f"x32:{i}" for i in range(8)] + ["q2sb", "w2r", "q2sbB", "w2rB",
                  "tq2B", "tqB", "qk_ln:0", "qk_ln:1", "qk_rf:0", "qk_rf:1"],
                 [f"osb:{s}" for s in range(32)])
    tape.barrier(["ropeCs", "ropeSs"] + [f"qkstg:{i}" for i in range(2)]
                 + [f"qksh:{i}" for i in range(2)] + [f"qkt2:{i}" for i in range(2)],
                 [f"ksup:{i}" for i in range(2)] + [f"vsup:{i}" for i in range(2)])
    tape.barrier([f"ha:{i}" for i in range(8)],
                 [f"pt:{i}" for i in range(2)])

    def load_super(s):
        par = s % 2
        semk = "dmk" if par == 0 else "dmk1"
        r, sub = s // 4, s % 4
        for h in range(NH):
            t_dma("sync",
                  ksup[par][(h % 2) * 64:(h % 2) * 64 + 64,
                            (h // 2) * SUP:(h // 2 + 1) * SUP],
                  kag[1024 * r + 64 * h:1024 * r + 64 * (h + 1),
                      SUP * sub:SUP * (sub + 1)],
                  semk, ("d_kag",), (f"ksup:{par}",) if h == 0 else ())
        tape.op("sync", lambda e, p=par, s_=s: e.dma_start(
                    out=vsup[p][:, 0:CPS * DM].rearrange("p (c d) -> p c d", c=CPS),
                    in_=vag[SUP * s_:SUP * (s_ + 1), :].rearrange(
                        "(c p) d -> p c d", p=128)),
                reads=("d_vag",), writes=(f"vsup:{par}",), sem=semk, inc=16)
        i = len(tape.ops) - 1
        tape.last_write[f"ksup:{par}"] = i
        tape.readers[f"ksup:{par}"] = []

    t_dve(lambda e: e.memset(ptb[0][:, 0:512], 0.0), (), ("pt:0",))
    t_mm(dbank[:], zer128[:, 0:64], ptb[0][:, 0:512], reads=("zer128", "pt:0"),
         writes=("dbank",), start=True, stop=False)

    load_super(0)
    load_super(1)

    def pv_emit(qb, p, head_par, owb, owtag, sup_par, conly=None):
        h = 2 * p + head_par
        row = 32 * head_par + 8 * qb + p
        pbuf = ptb[p % 2]
        for c in ([conly] if conly is not None else range(CPS)):
            u = 2 * c + head_par
            pslot = pbuf[:, 512 * u:512 * (u + 1)]
            vsl = vsup[sup_par][:, DM * c + 64 * h:DM * c + 64 * (h + 1)]
            t_mm(owb[64 * head_par:64 * head_par + 64, :], vsl, pslot,
                 reads=(f"pt:{p % 2}:{u}", f"vsup:{sup_par}"), writes=(owtag,),
                 start=(c == 0), stop=(c == CPS - 1), tp=(0, 64 * head_par))
            t_mm(dbank[32 * head_par:32 * head_par + 32, :],
                 onesel[:, 32 - (row % 32):64 - (row % 32)], pslot,
                 reads=(f"pt:{p % 2}:{u}", "onesel"), writes=("dbank",),
                 start=False, stop=False, tp=(0, 32 * head_par))

    OWB = [ow[0], ow[1], scr]
    OWT = ["ow:0", "ow:1", "scr"]

    def osb_evac(sup, p, qb):
        slot = p * 4 + qb
        if sup == 0:
            t_dve(lambda e, s=slot, o=OWB[p % 3]: e.tensor_copy(
                    out=osb[:, 512 * s:512 * (s + 1)], in_=o[:]),
                  (OWT[p % 3],), (f"osb:{slot}",))
        else:
            t_dve(lambda e, s=slot, o=OWB[p % 3]: e.tensor_add(
                    osb[:, 512 * s:512 * (s + 1)],
                    osb[:, 512 * s:512 * (s + 1)], o[:]),
                  (OWT[p % 3], f"osb:{slot}"), (f"osb:{slot}",))

    for sup in range(NSUP):
        par = sup % 2
        for qb in range(NQB):
            qsl = slice(QB * qb, QB * (qb + 1))
            for p in range(8):
                for c in range(CPS):
                    cs = c % 2
                    bA = stb[:, QB * (2 * cs):QB * (2 * cs) + QB]
                    bB = stb[:, QB * (2 * cs + 1):QB * (2 * cs + 1) + QB]
                    kA = ksup[par][0:64, p * SUP + 128 * c:p * SUP + 128 * (c + 1)]
                    kB = ksup[par][64:128, p * SUP + 128 * c:p * SUP + 128 * (c + 1)]
                    t_mm(bA, kA, qsb[p][0:64, qsl],
                         reads=(f"ksup:{par}", f"qsb:{p}"), writes=(f"stb:{2 * cs}",),
                         start=True, stop=True, tp=(0, 0))
                    t_mm(bB, kB, qsb[p][64:128, qsl],
                         reads=(f"ksup:{par}", f"qsb:{p}"), writes=(f"stb:{2 * cs + 1}",),
                         start=True, stop=True, tp=(64, 0))
                    t_act(ptb[p % 2][:, 1024 * c:1024 * (c + 1)],
                          stb[:, QB * 2 * cs:QB * (2 * cs + 2)], AF.Exp,
                          (f"stb:{2 * cs}", f"stb:{2 * cs + 1}", "mneg"),
                          (f"pt:{p % 2}:{2 * c}", f"pt:{p % 2}:{2 * c + 1}"),
                          bias=mneg[:, p:p + 1], scale=1.0)
                    if p > 0:
                        pv_emit(qb, p - 1, 1, OWB[(p - 1) % 3], OWT[(p - 1) % 3],
                                par, conly=c)
                    if c > 0:
                        pv_emit(qb, p, 0, OWB[p % 3], OWT[p % 3], par, conly=c - 1)
                if p > 0:
                    osb_evac(sup, p - 1, qb)
                pv_emit(qb, p, 0, OWB[p % 3], OWT[p % 3], par, conly=CPS - 1)
            pv_emit(qb, 7, 1, OWB[7 % 3], OWT[7 % 3], par)
            osb_evac(sup, 7, qb)
        if sup + 2 < NSUP:
            load_super(sup + 2)

    t_mm(dbank[:], zer128[:, 0:64], ptb[0][:, 0:512], reads=("zer128", "pt:0"),
         writes=("dbank",), start=False, stop=True)

    t_dve(lambda e: e.reciprocal(drec[:], dbank[:]), ("dbank",), ("drec",))
    tap("t_drec", drec[:], ("drec",))
    tap("t_osb", osb[:], tuple(f"osb:{s_}" for s_ in range(32)))
    tape.barrier([f"ksup:{i}" for i in range(2)] + [f"vsup:{i}" for i in range(2)],
                 [f"woTs:{i}" for i in range(8)] + ["drec", "dbcs", "xstg:0", "xstg:1"])
    tape.barrier([f"pt:{i}" for i in range(2)], [f"onrm:{s}" for s in range(32)])
    for ci in range(8):
        t_dma("sync", wq32[:], woT[128 * ci:128 * (ci + 1), :], "dml",
              ("wq32_r",), ("wq32",))
        t_dve(lambda e, c=ci: e.tensor_copy(out=woTs[c][:], in_=wq32[:]),
              ("wq32",), (f"woTs:{ci}", "wq32_r"))
    for p in range(8):
        for qb in range(NQB):
            slot = p * 4 + qb
            rA = 8 * qb + p
            rB = 32 + 8 * qb + p
            t_dve(lambda e, r=rA: e.tensor_scalar(selD[:, 0:64], iotas[:, 0:64],
                                                  float(r), None, ALU.is_ge),
                  ("iotas", "selD"), ("selD",))
            t_dve(lambda e, r=rA: e.tensor_scalar(selDb[:, 0:64], iotas[:, 0:64],
                                                  float(r + 1), None, ALU.is_ge),
                  ("iotas", "selDb"), ("selDb",))
            t_dve(lambda e: e.tensor_sub(selD[:, 0:64], selD[:, 0:64], selDb[:, 0:64]),
                  ("selD", "selDb"), ("selD",))
            t_dve(lambda e, r=rB: e.tensor_scalar(selD[:, 64:128], iotas[:, 64:128],
                                                  float(r), None, ALU.is_ge),
                  ("iotas", "selD"), ("selD",))
            t_dve(lambda e, r=rB: e.tensor_scalar(selDb[:, 64:128], iotas[:, 64:128],
                                                  float(r + 1), None, ALU.is_ge),
                  ("iotas", "selDb"), ("selDb",))
            t_dve(lambda e: e.tensor_sub(selD[:, 64:128], selD[:, 64:128],
                                         selDb[:, 64:128]),
                  ("selD", "selDb"), ("selD",))
            t_mm(scr[:, 0:QB], selD[:], drec[:],
                 reads=("drec", "selD"), writes=("scr",), start=True, stop=True)
            t_dve(lambda e: e.tensor_copy(out=dbcs[:], in_=scr[:, 0:QB]), ("scr",), ("dbcs",))
            t_dve(lambda e, s=slot: e.tensor_mul(onrm[:, 512 * s:512 * (s + 1)],
                                                 osb[:, 512 * s:512 * (s + 1)], dbcs[:]),
                  (f"osb:{slot}", "dbcs"), (f"onrm:{slot}",))
    tape.barrier([f"osb:{s}" for s in range(32)], [f"x1:{i}" for i in range(8)])
    for oc in range(8):
        for qb in range(NQB):
            bank = stb[:, QB * qb:QB * qb + QB]
            for p in range(8):
                slot = p * 4 + qb
                t_mm(bank, woTs[p][:, 128 * oc:128 * (oc + 1)],
                     onrm[:, 512 * slot:512 * (slot + 1)],
                     reads=(f"woTs:{p}", f"onrm:{slot}"), writes=(f"stb:{qb}",),
                     start=(p == 0), stop=(p == 7))
            xs = xstg[qb % 2]
            t_dma("sync", xs[:], xT[128 * oc:128 * (oc + 1), QB * qb:QB * (qb + 1)],
                  "dmx" if qb % 2 == 0 else "dmx1",
                  (f"xstg:{qb % 2}",), (f"xstg:{qb % 2}",))
            t_dve(lambda e, b=bank, x=xs, o=oc, q=qb: e.scalar_tensor_tensor(
                    x1sb[o][:, QB * q:QB * (q + 1)], b, modg[:, o:o + 1], x[:],
                    ALU.mult, ALU.add),
                  (f"stb:{qb}", f"xstg:{qb % 2}", "modg"),
                  (f"x1:{oc}", f"xstg:{qb % 2}"))

    tape.barrier(["wq32", "wstg", "wstg2"] + [f"wbf:{i}" for i in range(8)]
                 + [f"kst:{i}" for i in range(2)] + [f"vst:{i}" for i in range(2)],
                 [f"hb:{i}" for i in range(8)])
    tape.last_write["svt_ln2"] = max(tape.last_write.get("modca", 0),
                                     tape.last_write.get("mod_sb", 0))
    emit_adaln_ln("x1", x1sb, "hb", hb, lambda c: modca[:, 8 + c:9 + c],
                  lambda c: mod_sb[:, 24 + c:25 + c], "svt_ln2", "ln2")

    tape.barrier([f"qsb:{i}" for i in range(8)], [f"h2:{i}" for i in (4, 5, 6, 7)])
    tape.barrier([f"woTs:{i}" for i in range(8)] + ["drec", "dbcs", "xstg:0", "xstg:1"],
                 [f"h2:{i}" for i in (0, 1, 2, 3)])
    h2 = h2lo + h2hi
    tape.last_write["svt_fln"] = max(tape.last_write.get("gammacs", 0),
                                     tape.last_write.get("betacs", 0))
    emit_adaln_ln("hb", hb, "h2", h2, lambda c: gammacs[:, c:c + 1],
                  lambda c: betacs[:, c:c + 1], "svt_fln", "fln", src_bf16=True)

    for i in range(8):
        t_dma("sync", x1d[128 * i:128 * (i + 1), :], x1sb[i][:], "dms",
              (f"x1:{i}",), (f"x1d:{i}",))
    tape.batch_end([f"x1d:{i}" for i in range(8)])
    tape.dma_drain("sync", "dms", SEMS)
    tap("t_x1", x1d, tuple(f"x1d:{i}" for i in range(8)))
    tap("t_h20", h2[0][:], ("h2:0",))
    tape.barrier([f"onrm:{s}" for s in range(32)], ["q1mx", "q1sx", "q1rec"])
    tape.barrier([f"hb:{i}" for i in range(2)] + ["x1:0"],
                 [f"xq1w:{i}" for i in range(2)] + ["tq2p", "q1v"])

    def emit_quant(srcs, src_tags, dsts, dst_tags, tag, mrow_ap):
        t_dve(lambda e: e.scalar_tensor_tensor(mxf[:], srcs[0], -1.0, srcs[0],
                                               ALU.mult, ALU.max),
              reads=(src_tags[0], "mxf"), writes=(f"{tag}mx", "mxf"))
        for ci in range(1, len(srcs)):
            t_dve(lambda e, c=ci: e.scalar_tensor_tensor(mxf[:], srcs[c], 1.0, mxf[:],
                                                         ALU.mult, ALU.max),
                  reads=(src_tags[ci], f"{tag}mx"), writes=(f"{tag}mx", "mxf"))
            t_dve(lambda e, c=ci: e.scalar_tensor_tensor(mxf[:], srcs[c], -1.0, mxf[:],
                                                         ALU.mult, ALU.max),
                  reads=(src_tags[ci], f"{tag}mx"), writes=(f"{tag}mx", "mxf"))
        for b in range(FQB // 128):
            tape.op("tensor", lambda e, b_=b: e.transpose(scr[:, 0:128],
                                                          mxf[:, 128 * b_:128 * (b_ + 1)],
                                                          idn[:]),
                    reads=(f"{tag}mx", "idn", "scr"), writes=("scr",))
            t_dve(lambda e, b_=b: e.tensor_reduce(mxt[:, b_:b_ + 1], scr[:, 0:128],
                                                  AX, ALU.max),
                  ("scr", "mxt"), ("mxt",))
        t_dve(lambda e: e.tensor_scalar_max(mxt[:, 0:FQB // 128], mxt[:, 0:FQB // 128],
                                            1e-5), ("mxt",), ("mxt",))
        t_dve(lambda e: e.reciprocal(mxr[:, 0:FQB // 128], mxt[:, 0:FQB // 128]),
              ("mxt",), ("mxr",))
        t_dve(lambda e: e.tensor_scalar_mul(mxr[:, 0:FQB // 128], mxr[:, 0:FQB // 128],
                                            127.0), ("mxr",), ("mxr",))
        for b in range(FQB // 128):
            tape.op("tensor", lambda e, b_=b: e.transpose(scr[0:1, 128:256],
                                                          mxt[:, b_:b_ + 1], idn[:]),
                    reads=("mxt", "idn", "scr"), writes=("scr",))
            t_dve(lambda e, b_=b: e.tensor_copy(out=mrow_ap[:, 128 * b_:128 * (b_ + 1)],
                                                in_=scr[0:1, 128:256]),
                  ("scr", "rows0"), (f"{tag}mrow", "rows0"))
            tape.op("tensor", lambda e, b_=b: e.transpose(scr[0:1, 128:256],
                                                          mxr[:, b_:b_ + 1], idn[:]),
                    reads=("mxr", "idn", "scr"), writes=("scr",))
            t_dve(lambda e, b_=b: e.tensor_copy(out=tq[0:1, 128 * b_:128 * (b_ + 1)],
                                                in_=scr[0:1, 128:256]),
                  ("scr", "tq"), (f"{tag}sxrow", "tq"))
        t_mm(ow[0][:, 0:FQB], ones_row[:], tq[0:1, 0:FQB],
             reads=(f"{tag}sxrow", "ones_row"), writes=("ow0",), start=True, stop=True)
        for ci in range(len(srcs)):
            t_dve(lambda e, c=ci: e.tensor_mul(tf[:], srcs[c], ow[0][:, 0:FQB]),
                  reads=(src_tags[ci], "ow0", "tf"), writes=(f"{tag}v", "tf"))
            t_dve(lambda e: e.tensor_scalar_add(tf[:], tf[:], RND_C),
                  reads=(f"{tag}v",), writes=(f"{tag}v", "tf"))
            t_dve(lambda e, c=ci: e.tensor_scalar(dsts[c], tf[:], -RND_C, None, ALU.add),
                  reads=(f"{tag}v",), writes=(dst_tags[ci],))


    t_dve(lambda e: e.scalar_tensor_tensor(mxrow[:], h2[0][:], -1.0, h2[0][:],
                                           ALU.mult, ALU.max),
          reads=("h2:0",), writes=("q1mx",))
    for ci in range(1, 8):
        t_dve(lambda e, c=ci: e.scalar_tensor_tensor(mxrow[:], h2[c][:], 1.0, mxrow[:],
                                                     ALU.mult, ALU.max),
              reads=(f"h2:{ci}", "q1mx"), writes=("q1mx",))
        t_dve(lambda e, c=ci: e.scalar_tensor_tensor(mxrow[:], h2[c][:], -1.0, mxrow[:],
                                                     ALU.mult, ALU.max),
              reads=(f"h2:{ci}", "q1mx"), writes=("q1mx",))
    for b in range(16):
        tape.op("tensor", lambda e, b_=b: e.transpose(scr[:, 0:128],
                                                      mxrow[:, 128 * b_:128 * (b_ + 1)],
                                                      idn[:]),
                reads=("q1mx", "idn", "scr"), writes=("scr",))
        t_dve(lambda e, b_=b: e.tensor_reduce(mxt[:, b_:b_ + 1], scr[:, 0:128],
                                              AX, ALU.max),
              ("scr", "mxt"), ("mxt",))
    t_dve(lambda e: e.tensor_scalar_max(mxt[:], mxt[:], 1e-5), ("mxt",), ("mxt",))
    t_dve(lambda e: e.reciprocal(mxr[:], mxt[:]), ("mxt",), ("mxr",))
    t_dve(lambda e: e.tensor_scalar_mul(mxr[:], mxr[:], 127.0), ("mxr",), ("mxr",))
    for b in range(16):
        tape.op("tensor", lambda e, b_=b: e.transpose(scr[0:1, 128:256],
                                                      mxt[:, b_:b_ + 1], idn[:]),
                reads=("mxt", "idn", "scr"), writes=("scr",))
        t_dve(lambda e, b_=b: e.tensor_copy(out=rows[0:1, 128 * b_:128 * (b_ + 1)],
                                            in_=scr[0:1, 128:256]),
              ("scr", "rows0"), ("q1m", "rows0"))
        tape.op("tensor", lambda e, b_=b: e.transpose(scr[0:1, 128:256],
                                                      mxr[:, b_:b_ + 1], idn[:]),
                reads=("mxr", "idn", "scr"), writes=("scr",))
        t_dve(lambda e, b_=b: e.tensor_copy(out=mxrow[0:1, 128 * b_:128 * (b_ + 1)],
                                            in_=scr[0:1, 128:256]),
              ("scr", "q1mx"), ("q1sxrow", "q1mx"))
    for ci in range(8):
        w = xq1w[ci % 2]
        for qb in range(NQB):
            sl = slice(QB * qb, QB * (qb + 1))
            t_mm(ow[0][:, 0:QB], ones_row[:], mxrow[0:1, sl],
                 reads=("q1sxrow", "ones_row"), writes=("ow0",), start=True, stop=True)
            t_dve(lambda e, c=ci, s=sl: e.tensor_mul(tq2[:], h2[c][:, s], ow[0][:, 0:QB]),
                  (f"h2:{ci}", "ow0", "tq2p"), ("q1v", "tq2p"))
            t_dve(lambda e: e.tensor_scalar_add(tq2[:], tq2[:], RND_C),
                  ("q1v",), ("q1v", "tq2p"))
            t_dve(lambda e, w_=w, s=sl: e.tensor_scalar(w_[:, s], tq2[:], -RND_C, None,
                                                        ALU.add),
                  ("q1v",), (f"xq1w:{ci % 2}",))
        t_dma("sync", xq1d[128 * ci:128 * (ci + 1), :], w[:],
              "dms" if ci % 2 == 0 else "dms1",
              (f"xq1w:{ci % 2}",), (f"xq1d:{ci}", f"xq1w:{ci % 2}"))
    tape.barrier([f"xq1d:{i}" for i in range(8)], [])
    tape.batch_end([f"xq1d:{i}" for i in range(8)])
    tape.dma_drain("sync", "dms", SEMS)
    tape.dma_drain("sync", "dms1", SEMS)



    tape.barrier([f"x1d:{i}" for i in range(8)] + ["tq2p", "q1v"],
                 [f"w1q:{i}" for i in range(8)])
    tape.barrier([f"h2:{i}" for i in range(8)] + ["q1mx", "q1sxrow", "q1m",
                  "wstg", "wstg2"],
                 [f"w2q:{i}" for i in range(32)] + [f"hm:{h}" for h in range(32)]
                 + [f"xq2:{c}" for c in range(32)] + [f"xq1s:{i}" for i in range(8)]
                 + ["xstg2:0", "xstg2:1", "pscs", "psc2s", "tf", "tf2", "sxf", "mxf"])
    for ci in range(8):
        t_dma("sync", w1q[ci][:], w1qd[128 * ci:128 * (ci + 1), :], "dml",
              (f"w1qd:{ci}", f"w1q:{ci}"), (f"w1q:{ci}",))
    tape.batch_end([f"w1q:{i}" for i in range(8)])
    tape.dma_drain("sync", "dml", SEMS)
    w2q = w2ql + w2qh
    for ci in range(32):
        t_dma("sync", w2q[ci][:], w2qd[128 * ci:128 * (ci + 1), :], "dml",
              (f"w2qd:{ci}", f"w2q:{ci}"), (f"w2q:{ci}",))
    tape.batch_end([f"w2q:{i}" for i in range(32)])
    tape.dma_drain("sync", "dml", SEMS)

    tap("t_xq1", xq1d, tuple(f"xq1d:{i}" for i in range(8)))
    tap("t_w1q", w1qd, tuple(f"w1qd:{i}" for i in range(8)))
    tap("t_swsc", swsc[:], ("w1swrec", "w2swrec", "w1meanb", "w2meanb"))
    t_dve(lambda e: e.tensor_scalar_mul(rows[0:1, :], rows[0:1, :], 1.0 / 127.0),
          ("q1m",), ("q1m", "rows0"))
    t_dve(lambda e: e.tensor_scalar_mul(rows[0:1, :], rows[0:1, :], swsc[0:1, 1:2]),
          ("q1m", "w1meanb"), ("q1scale", "rows0"))

    for qf in range(NFQB):
        fsl = slice(FQB * qf, FQB * (qf + 1))
        for ic in range(8):
            t_dma("sync", xq1s[ic][:], xq1d[128 * ic:128 * (ic + 1), fsl], "dml",
                  (f"xq1d:{ic}", f"xq1s:{ic}"), (f"xq1s:{ic}",))
        tape.batch_end([f"xq1s:{i}" for i in range(8)])
        t_mm(scr[:, 0:FQB], ones_row[:], rows[0:1, fsl], reads=("q1scale", "ones_row"),
             writes=("scr",), start=True, stop=True)
        t_dve(lambda e: e.tensor_copy(out=pscs[:], in_=scr[:, 0:FQB]), ("scr",), ("pscs",))
        for hc in range(32):
            bank = stb[:, QB * (hc % 2):QB * (hc % 2) + FQB]
            for ic in range(8):
                t_mm(bank, w1q[ic][:, 128 * hc:128 * (hc + 1)], xq1s[ic][:],
                     reads=(f"w1q:{ic}", f"xq1s:{ic}"), writes=(f"stb:{hc % 2}",),
                     start=(ic == 0), stop=(ic == 7))
            t_dve(lambda e, b=bank: e.tensor_mul(tf[:], b, pscs[:]),
                  (f"stb:{hc % 2}", "pscs"), ("tf",))
            t_act(hmid[hc][:], tf[:], AF.Gelu, ("tf", "b1cs", "hmid_all"),
                  ("hmid_all", f"hm:{hc}"), bias=b1cs[:, hc:hc + 1], scale=1.0)
        if qf == 0:
            tap("t_hm0", hmid[0][:], ("hm:0",))
        emit_quant([hmid[c][:] for c in range(32)],
                   [f"hm:{c}" for c in range(32)],
                   [xq2[c][:] for c in range(32)],
                   [f"xq2:{c}" for c in range(32)],
                   f"q2_{qf}", rows[0:1, fsl])
        t_dve(lambda e, s=fsl: e.tensor_scalar_mul(rows[0:1, s], rows[0:1, s], 1.0 / 127.0),
              (f"q2_{qf}mrow",), (f"q2_{qf}mrow", "rows0"))
        t_dve(lambda e, s=fsl: e.tensor_scalar_mul(rows[0:1, s], rows[0:1, s],
                                                   swsc[0:1, 3:4]),
              (f"q2_{qf}mrow", "w2meanb"), (f"q2_{qf}scale", "rows0"))
        t_mm(scr[:, 0:FQB], ones_row[:], rows[0:1, fsl],
             reads=(f"q2_{qf}scale", "ones_row"), writes=("scr",), start=True, stop=True)
        t_dve(lambda e: e.tensor_copy(out=psc2s[:], in_=scr[:, 0:FQB]), ("scr",), ("psc2s",))
        if qf == 0:
            tap("t_xq2a", xq2[0][:], ("xq2:0",))
            tap("t_xq2b", xq2[31][:], ("xq2:31",))
            tap("t_psc2", psc2s[:], ("psc2s",))
            if debug:
                for c_ in range(32):
                    tap_ap = dbgout["t_xq2f"][128 * c_:128 * (c_ + 1), :]
                    tape.op("gpsimd", lambda e, a=tap_ap, c2=c_: e.dma_start(
                                out=a, in_=xq2[c2][:]),
                            reads=(f"xq2:{c_}",), writes=(), sem="dmg", inc=16)
                tap("t_w2q", w2qd, tuple(f"w2qd:{i}" for i in range(32)))
        for oc in range(8):
            bank = stb[:, 2 * QB + QB * (oc % 2):2 * QB + QB * (oc % 2) + FQB]
            for hc in range(32):
                t_mm(bank, w2q[hc][:, 128 * oc:128 * (oc + 1)], xq2[hc][:],
                     reads=(f"w2q:{hc}", f"xq2:{hc}", "xq2_all"),
                     writes=(f"stb2:{oc % 2}",),
                     start=(hc == 0), stop=(hc == 31))
            t_dve(lambda e, b=bank: e.tensor_mul(tf[:], b, psc2s[:]),
                  (f"stb2:{oc % 2}", "psc2s"), ("tf",))
            if qf == 0 and oc == 6:
                tap("t_y26", tf[:], ("tf",))
            if qf == 0 and debug:
                tap_ap2 = dbgout["t_y2all"][128 * oc:128 * (oc + 1), :]
                tape.op("gpsimd", lambda e, a=tap_ap2: e.dma_start(out=a, in_=tf[:]),
                        reads=("tf",), writes=(), sem="dmg", inc=16)
            xs = xstg2[oc % 2]
            t_dma("sync", xs[:, 0:FQB], x1d[128 * oc:128 * (oc + 1), fsl],
                  "dmx" if oc % 2 == 0 else "dmx1",
                  (f"x1d:{oc}", f"xstg2:{oc % 2}"), (f"xstg2:{oc % 2}",))
            t_dve(lambda e, x=xs, o=oc: e.scalar_tensor_tensor(
                    tf2[:], tf[:], modg[:, 8 + o:9 + o], x[:, 0:FQB], ALU.mult, ALU.add),
                  ("tf", f"xstg2:{oc % 2}", "modg"), ("tf2", f"xstg2:{oc % 2}"))
            t_dve(lambda e, o=oc: e.tensor_scalar_add(tf[:], tf2[:], gbb2[:, o:o + 1]),
                  ("tf2", "gbb2"), ("tf", f"x2t:{oc}:{qf}"))
            t_dma("gpsimd", outT[128 * oc:128 * (oc + 1), fsl], tf[:], "dmg",
                  (f"x2t:{oc}:{qf}", "tf"), (f"out:{oc}:{qf}",))

    tape.op("gpsimd", lambda e: e.engine_nop(),
            reads=tuple(f"out:{o}:{q}" for o in range(8) for q in range(NFQB)),
            writes=("done",))

    from contextlib import ExitStack
    es = ExitStack()
    sem_keys = ["tensor", "vector", "scalar", "gpsimd", "dml", "dmx", "dmx1",
                "dmk", "dmk1", "dms", "dms1", "dmg", "cc"]
    sems = {k: es.enter_context(nc.semaphore(f"s_{k}")) for k in sem_keys}
    SEMS.update(sems)
    block = es.enter_context(nc.Block())
    for name in ("tensor", "vector", "scalar", "gpsimd", "sync"):
        def f(eng, n=name):
            tape.emit_engine(n, eng, sems)
        getattr(block, name)(f)
    es.close()
    return nc



def _stage(inputs):
    x = inputs["L0_attn_x"]
    cos, sin = np.asarray(inputs["L0_attn_rope_cos"]), np.asarray(inputs["L0_attn_rope_sin"])
    qn, kn = np.asarray(inputs["L0_attn_q_norm_w"]), np.asarray(inputs["L0_attn_k_norm_w"])
    x = np.asarray(x)
    wqT = np.ascontiguousarray(np.asarray(inputs["L0_attn_wq"]).T)
    wkT = np.ascontiguousarray(np.asarray(inputs["L0_attn_wk"]).T)
    wvT = np.ascontiguousarray(np.asarray(inputs["L0_attn_wv"]).T)
    woT = np.ascontiguousarray(np.asarray(inputs["L0_attn_wo"]).T)
    w1T = np.ascontiguousarray(np.asarray(inputs["L0_ffn_w1"]).T)
    w2T = np.ascontiguousarray(np.asarray(inputs["L0_ffn_w2"]).T)
    wmodT = np.ascontiguousarray(np.asarray(inputs["L0_wmod"]))
    b1cv = np.ascontiguousarray(np.asarray(inputs["L0_ffn_b1"]).reshape(32, 128).T)
    b2cv = np.ascontiguousarray(np.asarray(inputs["L0_ffn_b2"]).reshape(8, 128).T)
    gammarv = np.ascontiguousarray(np.asarray(inputs["L0_ffn_gamma"]).reshape(8, 128).T)
    betacv = np.ascontiguousarray(np.asarray(inputs["L0_ffn_beta"]).reshape(8, 128).T)
    iden = np.eye(128, dtype=np.float32)
    iotaMv = np.tile(np.arange(64, dtype=np.float32)[:, None], (1, 128))
    altv = np.zeros((128, 8), np.float32)
    altv[0::2, 0] = -1.0
    altv[1::2, 0] = 1.0
    wqn2v = np.zeros((2, 128), np.float32)
    wqn2v[0, 0:64] = qn
    wqn2v[1, 64:128] = qn
    wkn2v = np.zeros((2, 128), np.float32)
    wkn2v[0, 0:64] = kn
    wkn2v[1, 64:128] = kn
    maps = []
    for core in range(8):
        b, s = core // 4, core % 4
        xTv = np.ascontiguousarray(x[b, 2048 * s:2048 * (s + 1), :].T)
        c_rowv = np.ascontiguousarray(np.asarray(inputs["L0_c"])[b].reshape(8, 128).T)
        t0 = 2048 * s
        C = np.ones((128, T), np.float32)
        S = np.zeros((128, T), np.float32)
        lo = max(0, (SEQ - PATCH) - t0)
        if lo < T:
            pos0 = t0 + lo - (SEQ - PATCH)
            n = T - lo
            crep = np.repeat(cos[pos0:pos0 + n, :].T, 2, axis=0)
            srep = np.repeat(sin[pos0:pos0 + n, :].T, 2, axis=0)
            C[:, lo:] = np.tile(crep, (2, 1))
            S[:, lo:] = np.tile(srep, (2, 1))
        maps.append(dict(
            xT=xTv, wqT=wqT, wkT=wkT, wvT=wvT, woT=woT, w1T=w1T, w2T=w2T,
            wmodT=wmodT, c_row=c_rowv, ropeC=np.ascontiguousarray(C),
            ropeS=np.ascontiguousarray(S), alt=altv, wqn2=wqn2v, wkn2=wkn2v,
            b1c=b1cv, b2c=b2cv, gammar=gammarv, betac=betacv, iden=iden,
            iotaM=iotaMv,
        ))
    return maps


def kernel(**inputs):
    maps = _stage(inputs)
    nc = build()
    res = run_bass_kernel_spmd(nc, maps, core_ids=list(range(8)))
    out = np.empty((2, 8192, 1024), np.float32)
    for core in range(8):
        b, s = core // 4, core % 4
        out[b, 2048 * s:2048 * (s + 1), :] = res.results[core]["outT"].T
    return out
